# version 02
# baseline (speedup 1.0000x reference)
import os
import numpy as np
from contextlib import ExitStack

import concourse.bass as bass
import concourse.mybir as mybir
from concourse.bass_utils import run_bass_kernel_spmd

F32 = mybir.dt.float32
BF16 = mybir.dt.bfloat16
AX = mybir.AxisListType.X
ALU = mybir.AluOpType
ACT = mybir.ActivationFunctionType

T = 2048
D = 1024
KC = 8
NSPAN = 4
SPAN = 512
NH = 16
DH = 64
NEG = -1.0e30

TRUNC = int(os.environ.get("MK_TRUNC", "99"))
MK_DEBUG = os.environ.get("MK_DEBUG", "")


class B:
    COMPUTE = ("t", "v", "s", "g")
    QUEUES = ("y", "gd")
    POOL = {"y": 20, "gd": 8}

    def __init__(self, nc):
        self.nc = nc
        self.st = ExitStack()
        self.sems = {e: self.st.enter_context(nc.semaphore(f"sem_{e}"))
                     for e in self.COMPUTE}
        self.qsems = {}
        self.qtot = {}
        self.qptr = {}
        for q in self.QUEUES:
            self.qsems[q] = [self.st.enter_context(nc.semaphore(f"qs_{q}{i}"))
                             for i in range(self.POOL[q])]
            self.qtot[q] = [0] * self.POOL[q]
            self.qptr[q] = 0
        self.seq = {e: 0 for e in self.COMPUTE + self.QUEUES}
        self.progs = {e: [] for e in self.COMPUTE + self.QUEUES}
        self.last_write = {}
        self.readers = {}
        self.watermark = {e: {} for e in self.COMPUTE + self.QUEUES}
        self._names = set()
        self._sbufs = {}
        self.pending_free = {}

    def sbuf(self, name, shape, dtype):
        assert name not in self._names, name
        self._names.add(name)
        h = self.st.enter_context(self.nc.sbuf_tensor(name, list(shape), dtype))
        return h

    def psum(self, name, shape, dtype):
        h = self.st.enter_context(self.nc.psum_tensor(name, list(shape), dtype))
        return h

    def scope(self):
        return ExitStack()

    def sbuf_in(self, stack, name, shape, dtype):
        assert name not in self._names, name
        self._names.add(name)
        h = stack.enter_context(self.nc.sbuf_tensor(name, list(shape), dtype))
        pend = getattr(self, "pending_free", None)
        if pend:
            self.readers[name] = [key + (v,) for key, v in pend.items()]
        def _cleanup(nm=name):
            self._names.discard(nm)
            pendd = self.pending_free
            lw = self.last_write.pop(nm, None)
            toks = list(self.readers.pop(nm, ()) or ())
            if lw is not None:
                toks.append(lw)
            for tok in toks:
                key, v = tok[:-1], tok[-1]
                if pendd.get(key, 0) < v:
                    pendd[key] = v
        stack.callback(_cleanup)
        return h

    @staticmethod
    def _key(x):
        t = getattr(x, "tensor", x)
        return t.name

    def _collect(self, reads, writes):
        waits = {}
        def need(tok):
            key = tok[:-1]
            v = tok[-1]
            if key not in waits or v > waits[key]:
                waits[key] = v
        for r in reads:
            lw = self.last_write.get(self._key(r))
            if lw:
                need(lw)
        for w in writes:
            k = self._key(w)
            lw = self.last_write.get(k)
            if lw:
                need(lw)
            for tok in self.readers.get(k, ()):
                need(tok)
        return waits

    def _emit_waits(self, e, waits):
        wl = []
        for key, v in waits.items():
            if key[0] == "E" and key[1] == e:
                pass
            if self.watermark[e].get(key, 0) >= v:
                continue
            self.watermark[e][key] = v
            wl.append((key, v))
        return wl

    def op(self, e, fn, reads=(), writes=()):
        assert e in self.COMPUTE
        waits = self._collect(reads, writes)
        idx = self.seq[e] + 1
        self.seq[e] = idx
        for r in reads:
            self.readers.setdefault(self._key(r), []).append(("E", e, idx))
        for w in writes:
            k = self._key(w)
            self.last_write[k] = ("E", e, idx)
            self.readers[k] = []
        wl = self._emit_waits(e, waits)
        self.progs[e].append((wl, fn, e))

    def dma_group(self, q, items):
        assert q in self.QUEUES
        reads = []
        writes = []
        for (_o, _i, rds, wts) in items:
            reads += list(rds)
            writes += list(wts)
        waits = self._collect(reads, writes)
        si = self.qptr[q]
        self.qptr[q] = (si + 1) % self.POOL[q]
        gate = self.qtot[q][si]
        self.qtot[q][si] = gate + 16 * len(items)
        tot = self.qtot[q][si]
        for r in reads:
            self.readers.setdefault(self._key(r), []).append(("Q", q, si, tot))
        for w in writes:
            k = self._key(w)
            self.last_write[k] = ("Q", q, si, tot)
            self.readers[k] = []
        wl = self._emit_waits(q, waits)
        aps = [(o, i) for (o, i, _r, _w) in items]
        self.progs[q].append((wl, (si, gate, aps), q))
        return ("Q", q, si, tot)

    def dma(self, q, out_ap, in_ap, reads=(), writes=()):
        self.dma_group(q, [(out_ap, in_ap, reads, writes)])

    def finalize(self, out_keys):
        nc = self.nc
        sems = self.sems
        qsems = self.qsems

        def wait_tok(eng, key, v):
            if key[0] == "E":
                eng.wait_ge(sems[key[1]], v)
            else:
                eng.wait_ge(qsems[key[1]][key[2]], v)

        def runner(e):
            def run(eng):
                for wl, fn, kind in self.progs[e]:
                    for (key, v) in wl:
                        wait_tok(eng, key, v)
                    if e in self.COMPUTE:
                        inst = fn(eng)
                        inst.then_inc(sems[e], 1)
                    else:
                        si, gate, aps = fn
                        if gate > 0:
                            eng.wait_ge(qsems[e][si], gate)
                        for (o, i) in aps:
                            eng.dma_start(out=o, in_=i).then_inc(
                                qsems[e][si], 16)
                if e == "gd":
                    for si2 in range(self.POOL["gd"]):
                        if self.qtot["gd"][si2] > 0:
                            eng.wait_ge(qsems["gd"][si2], self.qtot["gd"][si2])
            return run

        with nc.Block() as blk:
            blk.sync(lambda eng: runner("y")(eng))
            blk.vector(lambda eng: runner("v")(eng))
            blk.scalar(lambda eng: runner("s")(eng))
            blk.tensor(lambda eng: runner("t")(eng))

            def gps(eng):
                runner("g")(eng)
                runner("gd")(eng)
            blk.gpsimd(gps)
        self.st.close()


def build_program(trunc=99):
    nc = bass.Bass("TRN2", target_bir_lowering=False)
    b = B(nc)

    def din(name, shape, dtype=F32):
        return nc.dram_tensor(name, list(shape), dtype, kind="ExternalInput").ap()

    H = {}
    H["x"] = din("x", [D, T])
    H["l0a_wqkv"] = din("l0a_wqkv", [D, 3 * D])
    for i, (a, c) in enumerate([(D, D), (D, 512), (512, 256), (256, 1)]):
        H[f"cn_w{i+1}"] = din(f"cn_w{i+1}", [a, c])
        H[f"cn_b{i+1}"] = din(f"cn_b{i+1}", [128, max(1, c // 128) if c >= 128 else 1])
    for i, (a, c) in enumerate([(D, 512), (512, 256), (256, 1)]):
        H[f"gc_w{i+1}"] = din(f"gc_w{i+1}", [a, c])
        H[f"gc_b{i+1}"] = din(f"gc_b{i+1}", [128, max(1, c // 128) if c >= 128 else 1])
    H["ar"] = din("ar", [1, 1])
    for w in ["wq", "wk", "wv", "wo"]:
        H[f"l0b_{w}"] = din(f"l0b_{w}", [D, D])
    for w in ["aq", "av", "ao"]:
        H[f"l0b_{w}"] = din(f"l0b_{w}", [D, 64])
    for w in ["bq", "bv", "bo"]:
        H[f"l0b_{w}"] = din(f"l0b_{w}", [64, D])
    for w in ["wq", "wk", "wv", "wo", "wfc"]:
        H[f"l1a_{w}"] = din(f"l1a_{w}", [D, D])
    for w in ["bq", "bk", "bv", "bo", "bfc"]:
        H[f"l1a_{w}"] = din(f"l1a_{w}", [128, 8])
    H["l1a_projT"] = din("l1a_projT", [64, 266])
    H["l1b_sfw"] = din("l1b_sfw", [D, D])
    H["l1b_mask"] = din("l1b_mask", [D, D])
    H["l1b_sfb"] = din("l1b_sfb", [128, 8])
    for w in ["wq", "wk", "wv", "wo"]:
        H[f"l1b_{w}"] = din(f"l1b_{w}", [D, D])
    for w in ["bq", "bk", "bv", "bo", "bo2"][:4]:
        H[f"l1b_{w}"] = din(f"l1b_{w}", [128, 8])
    H["l1b_ff1w"] = din("l1b_ff1w", [D, 4 * D])
    H["l1b_ff1b"] = din("l1b_ff1b", [128, 32])
    H["l1b_ff2w"] = din("l1b_ff2w", [4 * D, D])
    H["l1b_ff2b"] = din("l1b_ff2b", [128, 8])
    for g in ["g1", "b1", "g2", "b2", "g3", "b3"]:
        H[f"l1b_{g}"] = din(f"l1b_{g}", [128, 8])
    for w in ["wr", "wk", "wv"]:
        H[f"l1f_{w}"] = din(f"l1f_{w}", [D, D])
    H["l1f_mur"] = din("l1f_mur", [128, 8])
    H["l1f_muk"] = din("l1f_muk", [128, 8])
    H["c_ident_bf"] = din("c_ident_bf", [128, 128], BF16)
    H["c_ident_f32"] = din("c_ident_f32", [128, 128])
    H["c_ones_col"] = din("c_ones_col", [128, 1], BF16)
    H["c_ones_colf"] = din("c_ones_colf", [128, 1])
    H["c_ones_row"] = din("c_ones_row", [1, 128])
    H["c_sel2"] = din("c_sel2", [2, 128])
    H["c_trimask"] = din("c_trimask", [128, 4 * SPAN], BF16)
    H["c_slotmask"] = din("c_slotmask", [128, SPAN], BF16)
    H["c_csign"] = din("c_csign", [128, 4])
    H["c_ropecc"] = din("c_ropecc", [128, T])
    H["c_ropesc"] = din("c_ropesc", [128, T])
    H["c_ropec"] = din("c_ropec", [128, T])
    H["c_ropes"] = din("c_ropes", [128, T])

    out_ap = nc.dram_tensor("out", [D, T], F32, kind="ExternalOutput").ap()
    xspill = nc.dram_tensor("xspill", [D, T], F32, kind="Internal").ap()

    xs = [b.sbuf(f"xs{c}", [128, T], F32) for c in range(KC)]
    xln = [b.sbuf(f"xln{c}", [128, T], BF16) for c in range(KC)]
    ident_bf = b.sbuf("ident_bf", [128, 128], BF16)
    ident_f = b.sbuf("ident_f", [128, 128], F32)
    ones_col = b.sbuf("ones_col", [128, 1], BF16)
    ones_colf = b.sbuf("ones_colf", [128, 1], F32)
    ones_row = b.sbuf("ones_row", [1, 128], F32)
    sel2 = b.sbuf("sel2", [2, 128], F32)
    allones = b.sbuf("allones", [128, 128], F32)
    slotmask = b.sbuf("slotmask", [128, SPAN], BF16)
    csign = b.sbuf("csign", [128, 4], F32)
    lnrows_t = b.sbuf("lnrows_t", [128, T], F32)
    lnrow = lnrows_t[0:1, :]
    lnrow2 = lnrows_t[32:33, :]
    lnrow3 = lnrows_t[64:65, :]

    psA = b.psum("psA", [128, 512], F32)
    psB = b.psum("psB", [128, 512], F32)
    psS0 = b.psum("psS0", [128, 512], F32)
    psS1 = b.psum("psS1", [128, 512], F32)
    psP0 = b.psum("psP0", [128, 512], F32)
    psR = b.psum("psR", [128, 512], F32)
    psR2 = b.psum("psR2", [128, 512], F32)
    psT = b.psum("psT", [128, 1024], BF16)

    def _ao_ms(eng):
        return eng.memset(allones[:], 1.0)
    b.op("v", _ao_ms, reads=[], writes=[allones])
    for dst, src in [(ident_bf, H["c_ident_bf"]), (ident_f, H["c_ident_f32"]),
                     (ones_col, H["c_ones_col"]), (ones_colf, H["c_ones_colf"]),
                     (ones_row, H["c_ones_row"]), (sel2, H["c_sel2"]),
                     (slotmask, H["c_slotmask"]),
                     (csign, H["c_csign"])]:
        b.dma("gd", dst[:], src, writes=[dst])

    xv = H["x"].rearrange("(c p) t -> c p t", p=128)
    for c in range(KC):
        b.dma("gd", xs[c][:], xv[c], writes=[xs[c]])

    vop_early = []
    uid = [0]
    def U(pfx):
        uid[0] += 1
        return f"{pfx}{uid[0]}"

    dbg_outs = []
    def dbg(name, ap_list):
        if not MK_DEBUG or name not in MK_DEBUG.split(","):
            return
        for i, ap in enumerate(ap_list):
            p, f = ap.shape[0], ap.shape[1]
            dt = ap.dtype
            o = nc.dram_tensor(f"dbg_{name}_{i}", [p, f], dt, kind="ExternalOutput").ap()
            b.dma("gd", o, ap, reads=[ap])
            dbg_outs.append(f"dbg_{name}_{i}")

    def mm_group(ps_ap, items, reads, writes, extra=()):
        n = len(items)
        def fn(eng, items=tuple(items), ps=ps_ap, n=n):
            inst = None
            for i, (lt, rh) in enumerate(items):
                inst = eng.matmul(out=ps, lhsT=lt, rhs=rh,
                                  start=(i == 0), stop=(i == n - 1))
            return inst
        b.op("t", fn, reads=list(reads), writes=list(writes))

    def transpose(ps_ap, in_ap, ident_ap, reads, writes):
        def fn(eng, o=ps_ap, i=in_ap, idn=ident_ap):
            return eng.transpose(out=o, in_=i, identity=idn)
        b.op("t", fn, reads=list(reads), writes=list(writes))

    def act(out_ap, in_ap, func, reads, writes, bias=0.0, scale=1.0, accum=None):
        if func == ACT.Copy and not isinstance(bias, float):
            func = ACT.Identity
        def fn(eng, o=out_ap, i=in_ap, f=func, bi=bias, sc=scale, ac=accum):
            kw = {}
            if ac is not None:
                kw["accum_out"] = ac
            return eng.activation(out=o, in_=i, func=f, bias=bi, scale=sc, **kw)
        b.op("s", fn, reads=list(reads), writes=list(writes))

    def vop(name, out_ap, reads, writes, **kw):
        def fn(eng, nm=name, o=out_ap, kw=kw):
            if nm == "memset":
                return eng.memset(o, kw["constant"])
            return getattr(eng, nm)(out=o, **kw)
        b.op("v", fn, reads=list(reads), writes=list(writes))

    def scalar_copy(out_ap, in_ap, reads, writes):
        def fn(eng, o=out_ap, i=in_ap):
            return eng.copy(out=o, in_=i)
        b.op("s", fn, reads=list(reads), writes=list(writes))

    def cast_to(e, out_ap, in_ap, reads, writes):
        if e == "s":
            scalar_copy(out_ap, in_ap, reads, writes)
        else:
            vop("tensor_copy", out_ap, reads, writes, in_=in_ap)

    def bcast_row(ps_ap, row_ap, m, reads, base=0):
        def fn(eng, o=ps_ap, r=row_ap, m=m, base=base):
            return eng.matmul(out=o, lhsT=allones[base:base + 1, 0:m], rhs=r,
                              start=True, stop=True)
        b.op("t", fn, reads=[allones] + list(reads), writes=[ps_ap])

    wstage = [b.sbuf(f"wstage{i}", [128, 1024], F32) for i in range(2)]
    wtile = [b.sbuf(f"wtile{i}", [128, 1024], BF16) for i in range(2)]
    bias_sb = b.sbuf("bias_sb", [128, 64], F32)

    def load_bias(dram_ap, nft):
        b.dma("y", bias_sb[:, 0:nft], dram_ap[:, 0:nft], writes=[bias_sb])

    QROT = ["y", "gd"]

    def proj_fm(dst, src, w_dram, F_out, evac, src_rows=KC, w_row_off=0,
                extra_mm=None, spans=range(NSPAN), cast_e="v", src_get=None,
                src_reads=None):
        nft = F_out // 128
        wv = w_dram.rearrange("(kc p) f -> kc p f", p=128)
        if src_get is None:
            src_get = lambda kc, sp: src[kc][:, sp * SPAN:(sp + 1) * SPAN]
            src_reads = src
        spans = list(spans)
        banks = [psA, psB, psS0, psS1][:len(spans)]
        for ft in range(nft):
            sg = ft % 2
            b.dma_group(QROT[ft % 2],
                        [(wstage[sg][:, kc * 128:kc * 128 + 128],
                          wv[kc][:, ft * 128:(ft + 1) * 128], [], [wstage[sg]])
                         for kc in range(src_rows)])
            cast_to(cast_e, wtile[sg][:, 0:src_rows * 128], wstage[sg][:, 0:src_rows * 128],
                    reads=[wstage[sg]], writes=[wtile[sg]])
            ex_items = {j: [] for j in range(len(spans))}
            ex_reads = {j: [] for j in range(len(spans))}
            if extra_mm is not None:
                for j, sp in enumerate(spans):
                    for (lt, rh, rds) in extra_mm(ft, sp):
                        ex_items[j].append((lt, rh))
                        ex_reads[j] += rds
            def fn(eng, sg=sg, spans=tuple(spans), banks=tuple(banks),
                   ex=ex_items, ft=ft):
                inst = None
                for kc in range(src_rows):
                    for j, sp in enumerate(spans):
                        inst = eng.matmul(out=banks[j][0:128, 0:SPAN],
                                          lhsT=wtile[sg][:, kc * 128:kc * 128 + 128],
                                          rhs=src_get(kc, sp),
                                          start=(kc == 0),
                                          stop=(kc == src_rows - 1 and not ex[j]))
                for j in range(len(spans)):
                    for i2, (lt, rh) in enumerate(ex[j]):
                        inst = eng.matmul(out=banks[j][0:128, 0:SPAN], lhsT=lt, rhs=rh,
                                          start=False, stop=(i2 == len(ex[j]) - 1))
                return inst
            reads = [wtile[sg]] + list(src_reads)
            for j in range(len(spans)):
                reads += ex_reads[j]
            b.op("t", fn, reads=reads, writes=list(banks))
            for j, sp in enumerate(spans):
                evac(ft, sp, banks[j])

    sq_ps = [psS0, psS1, psP0, psR2]
    sqbuf = b.sbuf("sqbuf", [128, T], BF16)

    def emit_ln_stats(src_chunks, tag="ln"):
        tmp_bf = xln
        for c in range(KC):
            cast_to("v" if c % 2 == 0 else "s", tmp_bf[c][:], src_chunks[c][:],
                    reads=[src_chunks[c]], writes=[tmp_bf[c]])
        for c in range(KC):
            def sqf(eng, i=tmp_bf[c], o=sqbuf):
                return eng.activation(out=o[:], in_=i[:], func=ACT.Square)
            b.op("s", sqf, reads=[tmp_bf[c]], writes=[sqbuf])
            for sp in range(NSPAN):
                sl = slice(sp * SPAN, (sp + 1) * SPAN)
                ps = sq_ps[sp]
                def fn(eng, ps=ps, sl=sl, c=c):
                    return eng.matmul(out=ps[0:1, 0:SPAN], lhsT=ones_col[:],
                                      rhs=sqbuf[:, sl], start=(c == 0),
                                      stop=(c == KC - 1))
                b.op("t", fn, reads=[ones_col, sqbuf], writes=[ps])
        for sp in range(NSPAN):
            sl = slice(sp * SPAN, (sp + 1) * SPAN)
            items = [(ones_col[:], tmp_bf[c][:, sl]) for c in range(KC)]
            mm_group(psR[0:1, 0:SPAN], items,
                     reads=[ones_col] + list(tmp_bf), writes=[psR])
            act(lnrow3[0:1, sl], psR[0:1, 0:SPAN], ACT.Copy, reads=[psR],
                writes=[lnrow3], scale=1.0 / D)
            act(lnrow[0:1, sl], sq_ps[sp][0:1, 0:SPAN], ACT.Copy,
                reads=[sq_ps[sp]], writes=[lnrow], scale=1.0 / D)
        vop("tensor_mul", lnrow2[0:1, :], reads=[lnrow3], writes=[lnrow2],
            in0=lnrow3[0:1, :], in1=lnrow3[0:1, :])
        for sp in range(NSPAN):
            sl = slice(sp * SPAN, (sp + 1) * SPAN)
            scalar_copy(psR2[0:1, 0:SPAN], lnrow2[0:1, sl], reads=[lnrow2],
                        writes=[psR2])
            vop("tensor_sub", lnrow[0:1, sl], reads=[lnrow, psR2], writes=[lnrow],
                in0=lnrow[0:1, sl], in1=psR2[0:1, 0:SPAN])
        vop("tensor_scalar_add", lnrow[0:1, :], reads=[lnrow], writes=[lnrow],
            in0=lnrow[0:1, :], scalar1=1e-5)
        def rsq(eng):
            return eng.activation(out=lnrow[0:1, :], in_=lnrow[0:1, :],
                                  func=ACT.Sqrt)
        b.op("s", rsq, reads=[lnrow], writes=[lnrow])
        vop("reciprocal", lnrow[0:1, :], reads=[lnrow], writes=[lnrow],
            in_=lnrow[0:1, :])
        for sp in range(NSPAN):
            sl = slice(sp * SPAN, (sp + 1) * SPAN)
            scalar_copy(psR2[0:1, 0:SPAN], lnrow[0:1, sl], reads=[lnrow],
                        writes=[psR2])
            vop("tensor_mul", lnrow2[0:1, sl], reads=[lnrow3, psR2], writes=[lnrow2],
                in0=lnrow3[0:1, sl], in1=psR2[0:1, 0:SPAN])

    def emit_ln_apply(src_chunks, out_dst, spans=range(NSPAN), col0=0,
                      gamma=None, beta=None):
        for sp in spans:
            sl = slice(sp * SPAN, (sp + 1) * SPAN)
            osl = slice(sp * SPAN - col0, (sp + 1) * SPAN - col0)
            bcast_row(psR[0:128, 0:SPAN], lnrow[0:1, sl], 128, reads=[lnrow])
            bcast_row(psR2[0:128, 0:SPAN], lnrow2[0:1, sl], 128, reads=[lnrow2],
                      base=32)
            for c in range(KC):
                od = out_dst[c]
                vop("tensor_mul", od[:, osl], reads=[src_chunks[c], psR], writes=[od],
                    in0=src_chunks[c][:, sl], in1=psR[0:128, 0:SPAN])
                vop("tensor_sub", od[:, osl], reads=[od, psR2], writes=[od],
                    in0=od[:, osl], in1=psR2[0:128, 0:SPAN])
                if gamma is not None:
                    gsb, gcol = gamma
                    bsb, bcol = beta
                    vop("tensor_scalar", od[:, osl], reads=[od, gsb, bsb], writes=[od],
                        in0=od[:, osl], scalar1=gsb[:, gcol(c):gcol(c) + 1],
                        scalar2=bsb[:, bcol(c):bcol(c) + 1],
                        op0=ALU.mult, op1=ALU.add)

    def emit_layernorm(src_chunks, out_dst, gamma=None, beta=None, tag="ln"):
        emit_ln_stats(src_chunks, tag=tag)
        emit_ln_apply(src_chunks, out_dst, gamma=gamma, beta=beta)

    def l0_attn_a():
        with b.scope() as sc:
            trimask = b.sbuf_in(sc, "l0a_trimask", [128, 4 * SPAN], BF16)
            b.dma("gd", trimask[:], H["c_trimask"], writes=[trimask])
            emit_layernorm(xs, xln, tag="l0a_ln")
            dbg("xln", [xln[c][:] for c in range(KC)])
            qh = b.sbuf_in(sc, "l0a_q", [128, T], BF16)
            kh = b.sbuf_in(sc, "l0a_k", [128, T], BF16)
            va = [b.sbuf_in(sc, f"l0a_va{t}", [128, 130], BF16) for t in range(16)]
            pt = b.sbuf_in(sc, "l0a_pt", [128, 16 * SPAN], BF16)
            denP = b.sbuf_in(sc, "l0a_denP", [128, T // 16, 2], F32) if False else b.sbuf_in(sc, "l0a_denP", [64, T], F32)
            den0 = denP[0:1, :]
            den1 = denP[32:33, :]
            ao = b.sbuf_in(sc, "l0a_ao", [128, T], BF16)
            wq = H["l0a_wqkv"]
            for hg in range(8):
                def evac_q(ft, sp, ps, dst=qh):
                    act(dst[:, sp * SPAN:(sp + 1) * SPAN], ps[0:128, 0:SPAN], ACT.Copy,
                        reads=[ps], writes=[dst], scale=0.125)
                def evac_k(ft, sp, ps, dst=kh):
                    act(dst[:, sp * SPAN:(sp + 1) * SPAN], ps[0:128, 0:SPAN], ACT.Copy,
                        reads=[ps], writes=[dst])
                proj_fm([qh], xln, wq[:, hg * 128:(hg + 1) * 128], 128, evac_q)
                proj_fm([kh], xln, wq[:, D + hg * 128:D + (hg + 1) * 128], 128, evac_k)
                if hg == 0:
                    dbg("qh", [qh[:]])
                    dbg("kh", [kh[:]])
                sg = hg % 2
                wvv = wq.rearrange("(kc p) f -> kc p f", p=128)
                b.dma_group("y", [(wstage[sg][:, kc * 128:kc * 128 + 128],
                                   wvv[kc][:, 2 * D + hg * 128:2 * D + hg * 128 + 128],
                                   [], [wstage[sg]]) for kc in range(KC)])
                cast_to("v", wtile[sg][:, 0:KC * 128], wstage[sg][:, 0:KC * 128],
                        reads=[wstage[sg]], writes=[wtile[sg]])
                for tt in range(16):
                    ps = psA if tt % 2 == 0 else psB
                    items = [(xln[kc][:, tt * 128:tt * 128 + 128],
                              wtile[sg][:, kc * 128:kc * 128 + 128]) for kc in range(KC)]
                    mm_group(ps[0:128, 0:128], items,
                             reads=[wtile[sg]] + list(xln), writes=[ps])
                    scalar_copy(va[tt][:, 0:64], ps[0:128, 0:64], reads=[ps], writes=[va[tt]])
                    scalar_copy(va[tt][:, 65:129], ps[0:128, 64:128], reads=[ps], writes=[va[tt]])
                    vop("memset", va[tt][:, 64:65], reads=[], writes=[va[tt]], constant=1.0)
                    vop("memset", va[tt][:, 129:130], reads=[], writes=[va[tt]], constant=1.0)
                if hg == 0:
                    dbg("va", [va[0][:], va[1][:]])
                for hh in range(2):
                    h = hg * 2 + hh
                    qsl = qh[hh * 64:(hh + 1) * 64, :]
                    ksl = kh[hh * 64:(hh + 1) * 64, :]
                    vcol = hh * 65
                    for bb in range(NSPAN):
                        qs = slice(bb * SPAN, (bb + 1) * SPAN)
                        nchunks = 4 * bb + 4
                        for cc in range(nchunks):
                            ps = psS0 if cc % 2 == 0 else psS1
                            mm_group(ps[0:128, 0:SPAN],
                                     [(ksl[:, cc * 128:cc * 128 + 128], qsl[:, qs])],
                                     reads=[kh, qh], writes=[ps])
                            u = cc - 4 * bb
                            if 0 <= u <= 3:
                                vop("tensor_add", ps[0:128, 0:SPAN], reads=[ps, trimask],
                                    writes=[ps], in0=ps[0:128, 0:SPAN],
                                    in1=trimask[:, u * SPAN:(u + 1) * SPAN])
                            act(pt[:, cc * SPAN:(cc + 1) * SPAN], ps[0:128, 0:SPAN],
                                ACT.Exp, reads=[ps], writes=[pt])
                        items = [(va[cc][:, vcol:vcol + 65],
                                  pt[:, cc * SPAN:(cc + 1) * SPAN]) for cc in range(nchunks)]
                        mm_group(psP0[0:65, 0:SPAN], items,
                                 reads=[pt] + [va[cc] for cc in range(nchunks)],
                                 writes=[psP0])
                        if hg == 0 and hh == 0 and bb == 0:
                            dbg("pt", [pt[:, 0:4 * SPAN]])
                        scalar_copy(ao[hh * 64:hh * 64 + 64, qs], psP0[0:64, 0:SPAN],
                                    reads=[psP0], writes=[ao])
                        dh_ = den0 if hh == 0 else den1
                        scalar_copy(dh_[0:1, qs], psP0[64:65, 0:SPAN],
                                    reads=[psP0], writes=[dh_])
                if hg == 0:
                    dbg("ao", [ao[:]])
                    dbg("den", [den0[:], den1[:]])
                vop("reciprocal", den0[0:1, :], reads=[den0], writes=[den0], in_=den0[0:1, :])
                vop("reciprocal", den1[0:1, :], reads=[den1], writes=[den1], in_=den1[0:1, :])
                for sp in range(NSPAN):
                    sl = slice(sp * SPAN, (sp + 1) * SPAN)
                    def fn0(eng, sl=sl):
                        return eng.matmul(out=psR[0:64, 0:SPAN], lhsT=ones_row[:, 0:64],
                                          rhs=den0[0:1, sl], start=True, stop=True)
                    b.op("t", fn0, reads=[ones_row, den0], writes=[psR])
                    def fn1(eng, sl=sl):
                        return eng.matmul(out=psR[64:128, 0:SPAN],
                                          lhsT=allones[32:33, 0:64],
                                          rhs=den1[0:1, sl], start=True, stop=True)
                    b.op("t", fn1, reads=[allones, den1], writes=[psR])
                    vop("tensor_mul", ao[:, sl], reads=[ao, psR], writes=[ao],
                        in0=ao[:, sl], in1=psR[0:128, 0:SPAN])
                    vop("tensor_add", xs[hg][:, sl], reads=[xs[hg], ao], writes=[xs[hg]],
                        in0=xs[hg][:, sl], in1=ao[:, sl])
                if hg == 0:
                    dbg("aon", [ao[:]])
                    dbg("rden", [den0[:]])

    def l0_ffn(tag):
        TH = T // 2
        with b.scope() as sc:
            emit_ln_stats(xs, tag=tag)
            of = [b.sbuf_in(sc, f"{tag}_of{c}", [128, TH], F32) for c in range(KC)]
            ob = [xln[c] for c in range(KC)]
            h1 = [xln[c] for c in range(KC)]
            h2 = [b.sbuf_in(sc, f"{tag}_h2{c}", [128, TH], BF16) for c in range(4)]
            h3 = [b.sbuf_in(sc, f"{tag}_h3{c}", [128, TH], BF16) for c in range(2)]
            gsb = b.sbuf_in(sc, f"{tag}_gsb", [128, 8], F32)
            g1 = b.sbuf_in(sc, f"{tag}_g1", [128, 4], F32)
            g2 = b.sbuf_in(sc, f"{tag}_g2", [128, 2], F32)
            gfv = b.sbuf_in(sc, f"{tag}_gfv", [1, 2], F32)
            arow = b.sbuf_in(sc, f"{tag}_arow", [1, 4], F32)
            smsb = b.sbuf_in(sc, f"{tag}_smsb", [128, 1], F32)
            tw = [b.sbuf_in(sc, f"{tag}_tw{i}", [128, SPAN], F32) for i in range(2)]
            gsc = b.sbuf_in(sc, f"{tag}_gsc", [128, T], F32)

            for c in range(KC):
                for sp in range(NSPAN):
                    sl = slice(sp * SPAN, (sp + 1) * SPAN)
                    bcast_row(psR[0:128, 0:SPAN], lnrow[0:1, sl], 128, reads=[lnrow])
                    vop("tensor_mul", gsc[:, sl], reads=[xs[c], psR], writes=[gsc],
                        in0=xs[c][:, sl], in1=psR[0:128, 0:SPAN])
                vop("reduce_sum", gsb[:, c:c + 1], reads=[gsc], writes=[gsb],
                    in_=gsc[:], axis=AX)
            vop("reduce_sum", smsb[0:1, 0:1], reads=[lnrow2], writes=[smsb],
                in_=lnrow2[0:1, :], axis=AX)
            def pbc(eng):
                return eng.matmul(out=psB[0:128, 0:1], lhsT=ones_row[:],
                                  rhs=smsb[0:1, 0:1], start=True, stop=True)
            b.op("t", pbc, reads=[smsb, ones_row], writes=[psB])
            vop("scalar_tensor_tensor", gsb[:], reads=[gsb, psB], writes=[gsb],
                in0=gsb[:], scalar=1.0, in1=psB[0:128, 0:1].broadcast_to([128, 8]),
                op0=ALU.mult, op1=ALU.subtract)
            vop("tensor_scalar", gsb[:], reads=[gsb], writes=[gsb],
                in0=gsb[:], scalar1=1.0 / T, scalar2=None, op0=ALU.mult)

            for nm, col in [("gc_b1", 0), ("gc_b2", 8), ("gc_b3", 12)]:
                w = 4 if nm == "gc_b1" else (2 if nm == "gc_b2" else 1)
                b.dma("y", bias_sb[:, col:col + w], H[nm][:, 0:w], writes=[bias_sb])
            wv1 = H["gc_w1"].rearrange("(kc p) f -> kc p f", p=128)
            for ft in range(4):
                sg = ft % 2
                b.dma_group("y", [(wstage[sg][:, kc * 128:kc * 128 + 128],
                                   wv1[kc][:, ft * 128:(ft + 1) * 128], [], [wstage[sg]])
                                  for kc in range(KC)])
                cast_to("v", wtile[sg][:, 0:KC * 128], wstage[sg][:, 0:KC * 128],
                        reads=[wstage[sg]], writes=[wtile[sg]])
                gb = b.sbuf
                def gcast(eng, sg=sg):
                    return eng.tensor_copy(out=gsc[:, 0:8], in_=gsb[:, 0:8])
                vop("tensor_copy", sqbuf[:, 0:8], reads=[gsb], writes=[sqbuf],
                    in_=gsb[:, 0:8])
                items = [(wtile[sg][:, kc * 128:kc * 128 + 128], sqbuf[:, kc:kc + 1])
                         for kc in range(KC)]
                mm_group(psA[0:128, 0:1], items, reads=[wtile[sg], sqbuf], writes=[psA])
                act(g1[:, ft:ft + 1], psA[0:128, 0:1], ACT.Gelu, reads=[psA, bias_sb],
                    writes=[g1], bias=bias_sb[:, ft:ft + 1])
            vop("tensor_copy", sqbuf[:, 8:12], reads=[g1], writes=[sqbuf],
                in_=g1[:, 0:4])
            wv2 = H["gc_w2"].rearrange("(kc p) f -> kc p f", p=128)
            for ft in range(2):
                sg = ft % 2
                b.dma_group("y", [(wstage[sg][:, kc * 128:kc * 128 + 128],
                                   wv2[kc][:, ft * 128:(ft + 1) * 128], [], [wstage[sg]])
                                  for kc in range(4)])
                cast_to("v", wtile[sg][:, 0:4 * 128], wstage[sg][:, 0:4 * 128],
                        reads=[wstage[sg]], writes=[wtile[sg]])
                items = [(wtile[sg][:, kc * 128:kc * 128 + 128], sqbuf[:, 8 + kc:9 + kc])
                         for kc in range(4)]
                mm_group(psA[0:128, 0:1], items, reads=[wtile[sg], sqbuf], writes=[psA])
                act(g2[:, ft:ft + 1], psA[0:128, 0:1], ACT.Gelu, reads=[psA, bias_sb],
                    writes=[g2], bias=bias_sb[:, 8 + ft:9 + ft])
            vop("tensor_copy", sqbuf[:, 12:14], reads=[g2], writes=[sqbuf],
                in_=g2[:, 0:2])
            wv3 = H["gc_w3"].rearrange("(kc p) f -> kc p f", p=128)
            sg = 0
            b.dma_group("y", [(wstage[sg][:, kc * 128:kc * 128 + 1],
                               wv3[kc][:, 0:1], [], [wstage[sg]])
                              for kc in range(2)])
            cast_to("v", wtile[sg][:, 0:2 * 128], wstage[sg][:, 0:2 * 128],
                    reads=[wstage[sg]], writes=[wtile[sg]])
            items = [(wtile[sg][:, kc * 128:kc * 128 + 1], sqbuf[:, 12 + kc:13 + kc])
                     for kc in range(2)]
            mm_group(psA[0:1, 0:1], items, reads=[wtile[sg], sqbuf], writes=[psA])
            def sig(eng):
                return eng.activation(out=gfv[0:1, 0:1], in_=psA[0:1, 0:1],
                                      func=ACT.Sigmoid,
                                      bias=bias_sb[0:1, 12:13])
            b.op("s", sig, reads=[psA, bias_sb], writes=[gfv])

            b.dma("y", arow[0:1, 0:1], H["ar"], writes=[arow])
            vop("tensor_scalar_mul", arow[0:1, 1:2], reads=[arow], writes=[arow],
                in0=arow[0:1, 0:1], scalar1=0.6)
            vop("tensor_mul", arow[0:1, 2:3], reads=[arow, gfv], writes=[arow],
                in0=arow[0:1, 0:1], in1=gfv[0:1, 0:1])
            vop("tensor_scalar", arow[0:1, 2:3], reads=[arow], writes=[arow],
                in0=arow[0:1, 2:3], scalar1=1.4, scalar2=0.1, op0=ALU.mult,
                op1=ALU.add)
            vop("tensor_sub", arow[0:1, 2:3], reads=[arow], writes=[arow],
                in0=arow[0:1, 2:3], in1=arow[0:1, 0:1])

            for nm, col, w in [("cn_b1", 16, 8), ("cn_b2", 24, 4), ("cn_b3", 28, 2)]:
                b.dma("y", bias_sb[:, col:col + w], H[nm][:, 0:w], writes=[bias_sb])
            b.dma("y", arow[0:1, 3:4], H["cn_b4"][0:1, 0:1], writes=[arow])

            pairs = [(0, 4, 1.0, -1.0), (1, 5, 1.0, -1.0),
                     (2, 6, -1.0, 1.0), (3, 7, -1.0, 1.0)]
            for half in range(2):
                spans = [2 * half, 2 * half + 1]
                col0 = half * TH
                emit_ln_apply(xs, of, spans=spans, col0=col0)
                for cyc in range(3):
                    for c in range(KC):
                        cast_to("v" if c % 2 == 0 else "s", ob[c][:, 0:TH], of[c][:],
                                reads=[of[c]], writes=[ob[c]])
                    def ev1(ft, sp, ps, half=half):
                        act(h1[ft][:, TH + (sp - 2 * half) * SPAN:TH + (sp - 2 * half + 1) * SPAN],
                            ps[0:128, 0:SPAN], ACT.Gelu, reads=[ps, bias_sb],
                            writes=[h1[ft]], bias=bias_sb[:, 16 + ft:17 + ft])
                    proj_fm(None, None, H["cn_w1"], 1024, ev1, spans=spans,
                            src_get=lambda kc, sp, half=half: ob[kc][:, (sp - 2 * half) * SPAN:(sp - 2 * half + 1) * SPAN],
                            src_reads=ob)
                    def ev2(ft, sp, ps, half=half):
                        act(h2[ft][:, (sp - 2 * half) * SPAN:(sp - 2 * half + 1) * SPAN],
                            ps[0:128, 0:SPAN], ACT.Gelu, reads=[ps, bias_sb],
                            writes=[h2[ft]], bias=bias_sb[:, 24 + ft:25 + ft])
                    proj_fm(None, None, H["cn_w2"], 512, ev2, spans=spans,
                            src_get=lambda kc, sp, half=half: h1[kc][:, TH + (sp - 2 * half) * SPAN:TH + (sp - 2 * half + 1) * SPAN],
                            src_reads=h1)
                    def ev3(ft, sp, ps, half=half):
                        act(h3[ft][:, (sp - 2 * half) * SPAN:(sp - 2 * half + 1) * SPAN],
                            ps[0:128, 0:SPAN], ACT.Gelu, reads=[ps, bias_sb],
                            writes=[h3[ft]], bias=bias_sb[:, 28 + ft:29 + ft])
                    proj_fm(None, None, H["cn_w3"], 256, ev3, spans=spans,
                            src_rows=4,
                            src_get=lambda kc, sp, half=half: h2[kc][:, (sp - 2 * half) * SPAN:(sp - 2 * half + 1) * SPAN],
                            src_reads=h2)
                    wv4 = H["cn_w4"].rearrange("(kc p) f -> kc p f", p=128)
                    sg4 = 0
                    b.dma_group("y", [(wstage[sg4][:, 1024 - 2 + kc:1024 - 1 + kc],
                                       wv4[kc][:, 0:1], [], [wstage[sg4]])
                                      for kc in range(2)])
                    cast_to("v", wtile[sg4][:, 1022:1024], wstage[sg4][:, 1022:1024],
                            reads=[wstage[sg4]], writes=[wtile[sg4]])
                    for sp in spans:
                        lsl = slice((sp - 2 * half) * SPAN, (sp - 2 * half + 1) * SPAN)
                        items = [(wtile[sg4][:, 1022 + kc:1023 + kc], h3[kc][:, lsl])
                                 for kc in range(2)]
                        mm_group(psP0[0:1, 0:SPAN], items,
                                 reads=[wtile[sg4]] + h3, writes=[psP0])
                        def sig4(eng, sp=sp):
                            return eng.activation(
                                out=lnrow3[0:1, sp * SPAN:(sp + 1) * SPAN],
                                in_=psP0[0:1, 0:SPAN], func=ACT.Sigmoid,
                                bias=arow[0:1, 3:4])
                        b.op("s", sig4, reads=[psP0, arow], writes=[lnrow3])
                        vop("tensor_scalar", lnrow3[0:1, sp * SPAN:(sp + 1) * SPAN],
                            reads=[lnrow3, arow], writes=[lnrow3],
                            in0=lnrow3[0:1, sp * SPAN:(sp + 1) * SPAN],
                            scalar1=arow[0:1, 1:2], scalar2=arow[0:1, 2:3],
                            op0=ALU.mult, op1=ALU.add)
                        bcast_row(psR2[0:128, 0:SPAN],
                                  lnrow3[0:1, sp * SPAN:(sp + 1) * SPAN], 128,
                                  reads=[lnrow3], base=64)
                        for (p, qq, s1, s2) in pairs:
                            vop("scalar_tensor_tensor", tw[0][:], reads=[of[qq], psR2],
                                writes=[tw[0]], in0=of[qq][:, lsl], scalar=s1,
                                in1=psR2[0:128, 0:SPAN], op0=ALU.mult, op1=ALU.mult)
                            vop("scalar_tensor_tensor", tw[1][:], reads=[of[p], psR2],
                                writes=[tw[1]], in0=of[p][:, lsl], scalar=s2,
                                in1=psR2[0:128, 0:SPAN], op0=ALU.mult, op1=ALU.mult)
                            vop("tensor_add", of[p][:, lsl], reads=[of[p], tw[0]],
                                writes=[of[p]], in0=of[p][:, lsl], in1=tw[0][:])
                            vop("tensor_add", of[qq][:, lsl], reads=[of[qq], tw[1]],
                                writes=[of[qq]], in0=of[qq][:, lsl], in1=tw[1][:])
                for c in range(KC):
                    gsl = slice(col0, col0 + TH)
                    vop("tensor_add", xs[c][:, gsl], reads=[xs[c], of[c]],
                        writes=[xs[c]], in0=xs[c][:, gsl], in1=of[c][:])

    def l0_attn_b(tag="b0"):
        csap = csign[:].unsqueeze(1).broadcast_to([128, SPAN // 4, 4])
        with b.scope() as sc:
            emit_layernorm(xs, xln)
            cc = b.sbuf_in(sc, f"{tag}_cc", [128, T], F32)
            scs = b.sbuf_in(sc, f"{tag}_scs", [128, T], F32)
            b.dma("gd", cc[:], H["c_ropecc"], writes=[cc])
            b.dma("gd", scs[:], H["c_ropesc"], writes=[scs])
            aof = [b.sbuf_in(sc, f"{tag}_ao{c}", [128, T], BF16) for c in range(KC)]
            t1 = b.sbuf_in(sc, f"{tag}_t1", [128, T], BF16)
            bsm = b.sbuf_in(sc, f"{tag}_bsm", [128, 2 * D], BF16)
            rows4 = b.sbuf_in(sc, f"{tag}_rows4", [128, T], F32)
            expA = rows4[0:1, :]
            expB = rows4[32:33, :]
            denA = rows4[64:65, :]
            denB = rows4[0:1, :]

            for nm, r0, c0 in [("bq", 0, 0), ("bo", 0, D), ("bv", 64, 0)]:
                b.dma("y", wstage[0][0:64, 0:D], H[f"l0b_{nm}"], writes=[wstage[0]])
                cast_to("v", bsm[r0:r0 + 64, c0:c0 + D], wstage[0][0:64, 0:D],
                        reads=[wstage[0]], writes=[bsm])

            for i, nm in enumerate(["aq", "av"]):
                wvv = H[f"l0b_{nm}"].rearrange("(kc p) f -> kc p f", p=128)
                b.dma_group("y", [(wstage[1][:, kc * 64:kc * 64 + 64], wvv[kc],
                                   [], [wstage[1]]) for kc in range(KC)])
                cast_to("v", wtile[1][:, 0:KC * 64], wstage[1][:, 0:KC * 64],
                        reads=[wstage[1]], writes=[wtile[1]])
                for sp in range(NSPAN):
                    sl = slice(sp * SPAN, (sp + 1) * SPAN)
                    items = [(wtile[1][:, kc * 64:kc * 64 + 64], xln[kc][:, sl])
                             for kc in range(KC)]
                    mm_group(psA[0:64, 0:SPAN], items, reads=[wtile[1]] + xln,
                             writes=[psA])
                    scalar_copy(t1[i * 64:(i + 1) * 64, sl], psA[0:64, 0:SPAN],
                                reads=[psA], writes=[t1])

            for hg in range(KC):
                hsl = slice(hg * 128, (hg + 1) * 128)
                with b.scope() as hs:
                    fullK = b.sbuf_in(hs, f"{tag}_fk{hg}", [128, SPAN], BF16)
                    fullV = b.sbuf_in(hs, f"{tag}_fv{hg}", [128, SPAN], BF16)
                    cpv = b.sbuf_in(hs, f"{tag}_cpv{hg}", [128, T], BF16)
                    fvt = b.sbuf_in(hs, f"{tag}_fvt{hg}", [128, 2 * 260], BF16)
                    pts = b.sbuf_in(hs, f"{tag}_pts{hg}", [128, 4 * SPAN], BF16)
                    cptmp = b.sbuf_in(hs, f"{tag}_cptmp{hg}", [64, SPAN], F32)
                    sh = sqbuf

                    def do_proj(wnm, dst, lora_i, rope):
                        def evq(ft, sp, ps, dst=dst):
                            act(dst[:, sp * SPAN:(sp + 1) * SPAN], ps[0:128, 0:SPAN],
                                ACT.Copy, reads=[ps], writes=[dst])
                        ex = None
                        if lora_i is not None:
                            def ex(ft, sp, li=lora_i):
                                r0 = 0 if li == 0 else 64
                                return [(bsm[r0:r0 + 64, hg * 128:(hg + 1) * 128],
                                         t1[r0:r0 + 64, sp * SPAN:(sp + 1) * SPAN],
                                         [bsm, t1])]
                        proj_fm(None, xln, H[f"l0b_{wnm}"][:, hsl], 128, evq,
                                extra_mm=ex)
                        if rope:
                            b.dma_group("y", [
                                (sh[0:32, :], dst[32:64, :], [dst], [sh]),
                                (sh[32:64, :], dst[0:32, :], [dst], [sh]),
                                (sh[64:96, :], dst[96:128, :], [dst], [sh]),
                                (sh[96:128, :], dst[64:96, :], [dst], [sh])])
                            vop("tensor_mul", dst[:], reads=[dst, cc], writes=[dst],
                                in0=dst[:], in1=cc[:])
                            vop("scalar_tensor_tensor", sh[:], reads=[sh, scs],
                                writes=[sh], in0=sh[:], scalar=1.0, in1=scs[:],
                                op0=ALU.mult, op1=ALU.mult)
                            vop("tensor_add", dst[:], reads=[dst, sh], writes=[dst],
                                in0=dst[:], in1=sh[:])
                        else:
                            def csm(eng, d=dst):
                                return eng.tensor_mul(
                                    out=d[:].rearrange("p (s f) -> p s f", f=4),
                                    in0=d[:].rearrange("p (s f) -> p s f", f=4),
                                    in1=csign[:].unsqueeze(1).broadcast_to([128, T // 4, 4]))
                            b.op("v", csm, reads=[dst, csign], writes=[dst])

                    def slot_sum(srcb, dstb):
                        v3 = srcb[:].rearrange("p (s f) -> p s f", f=4)
                        vop("tensor_add", dstb[:].rearrange("p (s o) -> p s o", o=1),
                            reads=[srcb], writes=[dstb], in0=v3[:, :, 0:1], in1=v3[:, :, 1:2])
                        vop("tensor_add", dstb[:].rearrange("p (s o) -> p s o", o=1),
                            reads=[srcb, dstb], writes=[dstb],
                            in0=dstb[:].rearrange("p (s o) -> p s o", o=1), in1=v3[:, :, 2:3])
                        vop("tensor_add", dstb[:].rearrange("p (s o) -> p s o", o=1),
                            reads=[srcb, dstb], writes=[dstb],
                            in0=dstb[:].rearrange("p (s o) -> p s o", o=1), in1=v3[:, :, 3:4])

                    def slot_cumsum(srcb, cpb):
                        vop("tensor_copy", cpb[:], reads=[srcb], writes=[cpb], in_=srcb[:])
                        c3 = cpb[:].rearrange("p (s f) -> p s f", f=4)
                        s3 = srcb[:].rearrange("p (s f) -> p s f", f=4)
                        for j in range(1, 4):
                            vop("tensor_add", c3[:, :, j:j + 1], reads=[srcb, cpb],
                                writes=[cpb], in0=c3[:, :, j - 1:j], in1=s3[:, :, j:j + 1])

                    with b.scope() as vsc:
                        vs = b.sbuf_in(vsc, f"{tag}_vs{hg}", [128, T], BF16)
                        do_proj("wv", vs, 1, False)
                        slot_sum(vs, fullV)
                        slot_cumsum(vs, cpv)
                        for hh in range(2):
                            hr = slice(hh * 64, hh * 64 + 64)
                            for cs_ in range(4):
                                transpose(psT[0:128, 0:64], fullV[hr, cs_ * 128:(cs_ + 1) * 128],
                                          ident_bf[hr, hh * 64:hh * 64 + 64],
                                          reads=[fullV, ident_bf], writes=[psT])
                                col = hh * 260 + cs_ * 65
                                scalar_copy(fvt[:, col:col + 64], psT[0:128, 0:64],
                                            reads=[psT], writes=[fvt])
                                vop("memset", fvt[:, col + 64:col + 65], reads=[],
                                    writes=[fvt], constant=1.0)
                    aq = b.sbuf_in(hs, f"{tag}_aq{hg}", [128, T], BF16)
                    ks = aof[hg]
                    m2 = pts
                    do_proj("wq", aq, 0, True)
                    do_proj("wk", ks, None, True)
                    slot_sum(ks, fullK)
                    vop("tensor_copy", m2[:], reads=[ks], writes=[m2], in_=ks[:])
                    c3 = m2[:].rearrange("p (s f) -> p s f", f=4)
                    s3 = ks[:].rearrange("p (s f) -> p s f", f=4)
                    for j in range(1, 4):
                        vop("tensor_add", c3[:, :, j:j + 1], reads=[ks, m2],
                            writes=[m2], in0=c3[:, :, j - 1:j], in1=s3[:, :, j:j + 1])
                    vop("tensor_mul", sh[:], reads=[aq, m2], writes=[sh],
                        in0=aq[:], in1=m2[:])
                    for hh, exr in [(0, expA), (1, expB)]:
                        hr = slice(hh * 64, hh * 64 + 64)
                        for sp in range(NSPAN):
                            sl = slice(sp * SPAN, (sp + 1) * SPAN)
                            mm_group(psR2[0:1, 0:SPAN],
                                     [(ones_col[hr, :], sh[hr, sl])],
                                     reads=[ones_col, sh], writes=[psR2])
                            act(exr[0:1, sl], psR2[0:1, 0:SPAN], ACT.Exp,
                                reads=[psR2], writes=[exr], scale=1.0 / 16.0)
                    for hh, exr, dnr in [(0, expA, denA), (1, expB, denB)]:
                        hr = slice(hh * 64, hh * 64 + 64)
                        for bb in range(NSPAN):
                            bsl = slice(bb * SPAN, (bb + 1) * SPAN)
                            for ccn in range(bb + 1):
                                ps = psS0 if ccn % 2 == 0 else psS1
                                mm_group(ps[0:128, 0:SPAN],
                                         [(fullK[hr, ccn * 128:(ccn + 1) * 128], aq[hr, bsl])],
                                         reads=[fullK, aq], writes=[ps])
                                if ccn == bb:
                                    vop("tensor_add", ps[0:128, 0:SPAN], reads=[ps, slotmask],
                                        writes=[ps], in0=ps[0:128, 0:SPAN], in1=slotmask[:])
                                act(pts[:, ccn * SPAN:(ccn + 1) * SPAN], ps[0:128, 0:SPAN],
                                    ACT.Exp, reads=[ps], writes=[pts], scale=1.0 / 16.0)
                            items = [(fvt[:, hh * 260 + ccn * 65:hh * 260 + ccn * 65 + 65],
                                      pts[:, ccn * SPAN:(ccn + 1) * SPAN])
                                     for ccn in range(bb + 1)]
                            mm_group(psP0[0:65, 0:SPAN], items, reads=[fvt, pts],
                                     writes=[psP0])
                            bcast_row(psR[0:64, 0:SPAN], exr[0:1, bsl], 64,
                                      reads=[exr], base=(0 if hh == 0 else 32))
                            vop("tensor_mul", cptmp[:], reads=[cpv, psR], writes=[cptmp],
                                in0=cpv[hr, bsl], in1=psR[0:64, 0:SPAN])
                            vop("tensor_add", psP0[0:64, 0:SPAN], reads=[psP0, cptmp],
                                writes=[psP0], in0=psP0[0:64, 0:SPAN], in1=cptmp[:])
                            vop("tensor_add", psP0[64:65, 0:SPAN], reads=[psP0, exr],
                                writes=[psP0], in0=psP0[64:65, 0:SPAN], in1=exr[0:1, bsl])
                            scalar_copy(aof[hg][hr, bsl], psP0[0:64, 0:SPAN],
                                        reads=[psP0], writes=[aof[hg]])
                            scalar_copy(dnr[0:1, bsl], psP0[64:65, 0:SPAN],
                                        reads=[psP0], writes=[dnr])
                    vop("reciprocal", rows4[0:65, :], reads=[rows4], writes=[rows4],
                        in_=rows4[0:65, :])
                    for sp in range(NSPAN):
                        sl = slice(sp * SPAN, (sp + 1) * SPAN)
                        def f0(eng, sl=sl):
                            return eng.matmul(out=psR[0:64, 0:SPAN],
                                              lhsT=allones[64:65, 0:64],
                                              rhs=denA[0:1, sl], start=True, stop=True)
                        b.op("t", f0, reads=[allones, denA], writes=[psR])
                        def f1(eng, sl=sl):
                            return eng.matmul(out=psR[64:128, 0:SPAN],
                                              lhsT=allones[0:1, 0:64],
                                              rhs=denB[0:1, sl], start=True, stop=True)
                        b.op("t", f1, reads=[allones, denB], writes=[psR])
                        vop("tensor_mul", aof[hg][:, sl], reads=[aof[hg], psR],
                            writes=[aof[hg]], in0=aof[hg][:, sl], in1=psR[0:128, 0:SPAN])
                        def csm2(eng, sl=sl, hg=hg):
                            return eng.tensor_mul(
                                out=aof[hg][:, sl].rearrange("p (s f) -> p s f", f=4),
                                in0=aof[hg][:, sl].rearrange("p (s f) -> p s f", f=4),
                                in1=csap)
                        b.op("v", csm2, reads=[aof[hg], csign], writes=[aof[hg]])
            wvv = H["l0b_ao"].rearrange("(kc p) f -> kc p f", p=128)
            b.dma_group("y", [(wstage[1][:, kc * 64:kc * 64 + 64], wvv[kc],
                               [], [wstage[1]]) for kc in range(KC)])
            cast_to("v", wtile[1][:, 0:KC * 64], wstage[1][:, 0:KC * 64],
                    reads=[wstage[1]], writes=[wtile[1]])
            for sp in range(NSPAN):
                sl = slice(sp * SPAN, (sp + 1) * SPAN)
                items = [(wtile[1][:, kc * 64:kc * 64 + 64], aof[kc][:, sl])
                         for kc in range(KC)]
                mm_group(psA[0:64, 0:SPAN], items, reads=[wtile[1]] + aof, writes=[psA])
                scalar_copy(t1[0:64, sl], psA[0:64, 0:SPAN], reads=[psA], writes=[t1])
            def evr(ft, sp, ps):
                sl = slice(sp * SPAN, (sp + 1) * SPAN)
                vop("tensor_add", xs[ft][:, sl], reads=[xs[ft], ps], writes=[xs[ft]],
                    in0=xs[ft][:, sl], in1=ps[0:128, 0:SPAN])
            def exo(ft, sp):
                return [(bsm[0:64, D + ft * 128:D + (ft + 1) * 128],
                         t1[0:64, sp * SPAN:(sp + 1) * SPAN], [bsm, t1])]
            proj_fm(None, aof, H["l0b_wo"], D, evr, extra_mm=exo)

    def l1_ffn(tag):
        with b.scope() as sc:
            emit_layernorm(xs, xln)
            musb = b.sbuf_in(sc, f"{tag}_mu", [128, 32], F32)
            b.dma("y", musb[:, 0:8], H["l1f_mur"][:, 0:8], writes=[musb])
            b.dma("y", musb[:, 8:16], H["l1f_muk"][:, 0:8], writes=[musb])
            vop("tensor_scalar", musb[:, 16:32], reads=[musb], writes=[musb],
                in0=musb[:, 0:16], scalar1=-1.0, scalar2=1.0, op0=ALU.mult, op1=ALU.add)
            r = [b.sbuf_in(sc, f"{tag}_r{c}", [128, T], BF16) for c in range(KC)]

            def mix(dst, mucol):
                for c in range(KC):
                    vop("tensor_scalar_mul", dst[c][:], reads=[xln[c], musb],
                        writes=[dst[c]], in0=xln[c][:], scalar1=musb[:, mucol + c:mucol + c + 1])
                    vop("scalar_tensor_tensor", dst[c][:, 1:T], reads=[xln[c], musb, dst[c]],
                        writes=[dst[c]], in0=xln[c][:, 0:T - 1],
                        scalar=musb[:, 16 + mucol + c:17 + mucol + c],
                        in1=dst[c][:, 1:T], op0=ALU.mult, op1=ALU.add)

            with b.scope() as sa:
                xr = [b.sbuf_in(sa, f"{tag}_xr{c}", [128, T], BF16) for c in range(KC)]
                mix(xr, 0)
                def evr(ft, sp, ps):
                    act(r[ft][:, sp * SPAN:(sp + 1) * SPAN], ps[0:128, 0:SPAN],
                        ACT.Sigmoid, reads=[ps], writes=[r[ft]])
                proj_fm(None, xr, H["l1f_wr"], D, evr)
            with b.scope() as sb_:
                xk = [b.sbuf_in(sb_, f"{tag}_xk{c}", [128, T], BF16) for c in range(KC)]
                mix(xk, 8)
                ka = xln
                def evk(ft, sp, ps):
                    act(ka[ft][:, sp * SPAN:(sp + 1) * SPAN], ps[0:128, 0:SPAN],
                        ACT.Relu, reads=[ps], writes=[ka[ft]])
                proj_fm(None, xk, H["l1f_wk"], D, evk)
            for c in range(KC):
                def sqk(eng, c=c):
                    return eng.activation(out=xln[c][:], in_=xln[c][:], func=ACT.Square)
                b.op("s", sqk, reads=[xln[c]], writes=[xln[c]])
            tws = b.sbuf_in(sc, f"{tag}_tws", [128, SPAN], F32)
            def evv(ft, sp, ps):
                sl = slice(sp * SPAN, (sp + 1) * SPAN)
                vop("tensor_mul", tws[:], reads=[r[ft], ps], writes=[tws],
                    in0=r[ft][:, sl], in1=ps[0:128, 0:SPAN])
                vop("tensor_add", xs[ft][:, sl], reads=[xs[ft], tws], writes=[xs[ft]],
                    in0=xs[ft][:, sl], in1=tws[:])
            proj_fm(None, xln, H["l1f_wv"], D, evv)

    def l1_attn_a(tag="p1"):
        M = 266
        DN = float(64.0 ** -0.25)
        RATIO = float(266.0 ** -0.5)
        EPS = 1e-4
        with b.scope() as sc:
            emit_layernorm(xs, xln)
            aof = [b.sbuf_in(sc, f"{tag}_ao{c}", [128, T], BF16) for c in range(KC)]
            pj2 = b.sbuf_in(sc, f"{tag}_pj2", [128, M], BF16)
            b.dma("y", wstage[0][0:64, 0:M], H["l1a_projT"], writes=[wstage[0]])
            cast_to("v", pj2[0:64, 0:M], wstage[0][0:64, 0:M], reads=[wstage[0]], writes=[pj2])
            cast_to("v", pj2[64:128, 0:M], wstage[0][0:64, 0:M], reads=[wstage[0]], writes=[pj2])
            for nm, col in [("bq", 0), ("bk", 8), ("bv", 16), ("bo", 24), ("bfc", 32)]:
                b.dma("y", bias_sb[:, col:col + 8], H[f"l1a_{nm}"][:, 0:8], writes=[bias_sb])
            vop("tensor_scalar_mul", bias_sb[:, 0:16], reads=[bias_sb], writes=[bias_sb],
                in0=bias_sb[:, 0:16], scalar1=DN)
            denP2 = b.sbuf_in(sc, f"{tag}_den", [64, T], F32)
            dnA = denP2[0:1, :]
            dnB = denP2[32:33, :]
            bqrow = b.sbuf_in(sc, f"{tag}_bqrow", [1, T], F32)
            for hg in range(KC):
                hsl = slice(hg * 128, (hg + 1) * 128)
                with b.scope() as hs:
                    qb = b.sbuf_in(hs, f"{tag}_qb{hg}", [128, T], BF16)
                    kb = b.sbuf_in(hs, f"{tag}_kb{hg}", [128, T], BF16)
                    va = [b.sbuf_in(hs, f"{tag}_va{hg}_{t}", [128, 130], BF16)
                          for t in range(16)]
                    kp = b.sbuf_in(hs, f"{tag}_kp{hg}", [128, 16 * M], BF16)
                    qpt0 = b.sbuf_in(hs, f"{tag}_qpt0{hg}", [128, T], BF16)
                    qpt1 = b.sbuf_in(hs, f"{tag}_qpt1{hg}", [128, T], BF16)
                    qpt2 = b.sbuf_in(hs, f"{tag}_qpt2{hg}", [10, T], BF16)
                    stat = b.sbuf_in(hs, f"{tag}_st{hg}", [128, 64], F32)
                    trow = b.sbuf_in(hs, f"{tag}_tr{hg}", [16, 128], F32)
                    ctx = b.sbuf_in(hs, f"{tag}_ctx{hg}", [65, M], BF16)
                    ctxa = [b.sbuf_in(hs, f"{tag}_cx{hg}_{i}", [128, 65], BF16)
                            for i in range(3)]
                    def evq(ft, sp, ps, dst=qb, bc=0):
                        act(dst[:, sp * SPAN:(sp + 1) * SPAN], ps[0:128, 0:SPAN],
                            ACT.Copy, reads=[ps, bias_sb], writes=[dst], scale=DN,
                            bias=bias_sb[:, bc + hg:bc + hg + 1])
                    def evk_(ft, sp, ps, dst=kb, bc=8):
                        act(dst[:, sp * SPAN:(sp + 1) * SPAN], ps[0:128, 0:SPAN],
                            ACT.Copy, reads=[ps, bias_sb], writes=[dst], scale=DN,
                            bias=bias_sb[:, bc + hg:bc + hg + 1])
                    proj_fm(None, xln, H["l1a_wq"][:, hsl], 128, evq)
                    proj_fm(None, xln, H["l1a_wk"][:, hsl], 128, evk_)
                    sg = hg % 2
                    wvv = H["l1a_wv"].rearrange("(kc p) f -> kc p f", p=128)
                    b.dma_group("y", [(wstage[sg][:, kc * 128:kc * 128 + 128],
                                       wvv[kc][:, hsl], [], [wstage[sg]])
                                      for kc in range(KC)])
                    cast_to("v", wtile[sg][:, 0:KC * 128], wstage[sg][:, 0:KC * 128],
                            reads=[wstage[sg]], writes=[wtile[sg]])
                    transpose(psA[0:1, 0:128], bias_sb[:, 16 + hg:17 + hg],
                              ident_f[:], reads=[bias_sb, ident_f], writes=[psA])
                    scalar_copy(bqrow[0:1, 0:128], psA[0:1, 0:128], reads=[psA],
                                writes=[bqrow])
                    for tt in range(16):
                        ps = psB if tt % 2 == 0 else psP0
                        items = [(xln[kc][:, tt * 128:tt * 128 + 128],
                                  wtile[sg][:, kc * 128:kc * 128 + 128]) for kc in range(KC)]
                        items.append((allones[0:1, 0:128], bqrow[0:1, 0:128]))
                        mm_group(ps[0:128, 0:128], items,
                                 reads=[wtile[sg], bqrow] + list(xln), writes=[ps])
                        scalar_copy(va[tt][:, 0:64], ps[0:128, 0:64], reads=[ps], writes=[va[tt]])
                        scalar_copy(va[tt][:, 65:129], ps[0:128, 64:128], reads=[ps], writes=[va[tt]])
                        vop("memset", va[tt][:, 64:65], reads=[], writes=[va[tt]], constant=1.0)
                        vop("memset", va[tt][:, 129:130], reads=[], writes=[va[tt]], constant=1.0)
                    for hh in range(2):
                        h0 = hh * 64
                        hr = slice(h0, h0 + 64)
                        def sqq(eng, s=qb):
                            return eng.activation(out=sqbuf[:], in_=s[:], func=ACT.Square)
                        b.op("s", sqq, reads=[qb], writes=[sqbuf])
                        for nt in range(16):
                            mm_group(psS0[0:128, 0:1],
                                     [(sqbuf[hr, nt * 128:(nt + 1) * 128], ones_col[hr, :])],
                                     reads=[sqbuf, ones_col], writes=[psS0])
                            scalar_copy(stat[:, nt:nt + 1], psS0[0:128, 0:1],
                                        reads=[psS0], writes=[stat])
                            mm_group(psS1[0:128, 0:M],
                                     [(qb[hr, nt * 128:(nt + 1) * 128], pj2[hr, :])],
                                     reads=[qb, pj2], writes=[psS1])
                            vop("reduce_max", stat[:, 16 + nt:17 + nt], reads=[psS1],
                                writes=[stat], in_=psS1[0:128, 0:M], axis=AX)
                        def sqk2(eng, s=kb):
                            return eng.activation(out=sqbuf[:], in_=s[:], func=ACT.Square)
                        b.op("s", sqk2, reads=[kb], writes=[sqbuf])
                        for nt in range(16):
                            mm_group(psS0[0:128, 0:1],
                                     [(sqbuf[hr, nt * 128:(nt + 1) * 128], ones_col[hr, :])],
                                     reads=[sqbuf, ones_col], writes=[psS0])
                            scalar_copy(stat[:, 32 + nt:33 + nt], psS0[0:128, 0:1],
                                        reads=[psS0], writes=[stat])
                            mm_group(psS1[0:128, 0:M],
                                     [(kb[hr, nt * 128:(nt + 1) * 128], pj2[hr, :])],
                                     reads=[kb, pj2], writes=[psS1])
                            vop("reduce_max", stat[:, 48 + nt:49 + nt], reads=[psS1],
                                writes=[stat], in_=psS1[0:128, 0:M], axis=AX)
                        vop("reduce_max", stat[:, 48:49], reads=[stat], writes=[stat],
                            in_=stat[:, 48:64], axis=AX)
                        transpose(psA[0:1, 0:128], stat[:, 48:49], ident_f[:],
                                  reads=[stat, ident_f], writes=[psA])
                        vop("reduce_max", psA[0:1, 0:1], reads=[psA], writes=[psA],
                            in_=psA[0:1, 0:128], axis=AX)
                        scalar_copy(bqrow[0:1, 256:257], psA[0:1, 0:1], reads=[psA],
                                    writes=[bqrow])
                        def mxb(eng):
                            return eng.matmul(out=psR2[0:128, 0:1], lhsT=ones_row[:],
                                              rhs=bqrow[0:1, 256:257], start=True, stop=True)
                        b.op("t", mxb, reads=[ones_row, bqrow], writes=[psR2])
                        vop("scalar_tensor_tensor", stat[:, 32:48], reads=[stat, psR2],
                            writes=[stat], in0=stat[:, 32:48], scalar=-0.5,
                            in1=psR2[0:128, 0:1].broadcast_to([128, 16]),
                            op0=ALU.mult, op1=ALU.subtract)
                        vop("tensor_scalar_mul", stat[:, 0:16], reads=[stat], writes=[stat],
                            in0=stat[:, 0:16], scalar1=0.5)
                        vop("tensor_add", stat[:, 0:16], reads=[stat], writes=[stat],
                            in0=stat[:, 0:16], in1=stat[:, 16:32])
                        vop("tensor_scalar_mul", stat[:, 0:16], reads=[stat], writes=[stat],
                            in0=stat[:, 0:16], scalar1=-1.0)
                        for nt in range(16):
                            ps = psS0 if nt % 2 == 0 else psS1
                            mm_group(ps[0:128, 0:M],
                                     [(kb[hr, nt * 128:(nt + 1) * 128], pj2[hr, :])],
                                     reads=[kb, pj2], writes=[ps])
                            act(kp[:, nt * M:(nt + 1) * M], ps[0:128, 0:M], ACT.Exp,
                                reads=[ps, stat], writes=[kp],
                                bias=stat[:, 32 + nt:33 + nt])
                        vop("tensor_scalar", kp[:], reads=[kp], writes=[kp],
                            in0=kp[:], scalar1=RATIO, scalar2=RATIO * EPS,
                            op0=ALU.mult, op1=ALU.add)
                        transpose(psA[0:16, 0:128], stat[:, 0:16], ident_f[:],
                                  reads=[stat, ident_f], writes=[psA])
                        scalar_copy(trow[0:16, :], psA[0:16, 0:128], reads=[psA],
                                    writes=[trow])
                        b.dma_group("y", [(bqrow[0:1, nt * 128:(nt + 1) * 128],
                                           trow[nt:nt + 1, 0:128], [trow], [bqrow])
                                          for nt in range(16)])
                        qpt = [qpt0, qpt1, qpt2]
                        for mc in range(3):
                            mw = 128 if mc < 2 else 10
                            for sp in range(NSPAN):
                                sl = slice(sp * SPAN, (sp + 1) * SPAN)
                                ps = psS0 if sp % 2 == 0 else psS1
                                items = [(pj2[hr, mc * 128:mc * 128 + mw], qb[hr, sl]),
                                         (allones[0:1, 0:mw], bqrow[0:1, sl])]
                                mm_group(ps[0:mw, 0:SPAN], items,
                                         reads=[pj2, qb, allones, bqrow], writes=[ps])
                                act(qpt[mc][0:mw, sl], ps[0:mw, 0:SPAN], ACT.Exp,
                                    reads=[ps], writes=[qpt[mc]])
                            vop("tensor_scalar", qpt[mc][0:mw, :], reads=[qpt[mc]],
                                writes=[qpt[mc]], in0=qpt[mc][0:mw, :], scalar1=RATIO,
                                scalar2=RATIO * EPS, op0=ALU.mult, op1=ALU.add)
                        items = [(va[nt][:, hh * 65:hh * 65 + 65], kp[:, nt * M:(nt + 1) * M])
                                 for nt in range(16)]
                        mm_group(psP0[0:65, 0:M], items,
                                 reads=[kp] + va, writes=[psP0])
                        scalar_copy(ctx[0:65, 0:M], psP0[0:65, 0:M], reads=[psP0], writes=[ctx])
                        for mc in range(3):
                            mw = 128 if mc < 2 else 10
                            transpose(psT[0:mw, 0:65], ctx[0:65, mc * 128:mc * 128 + mw],
                                      ident_bf[0:65, 0:65], reads=[ctx, ident_bf], writes=[psT])
                            scalar_copy(ctxa[mc][0:mw, :], psT[0:mw, 0:65],
                                        reads=[psT], writes=[ctxa[mc]])
                        dn_r = dnA if hh == 0 else dnB
                        for sp in range(NSPAN):
                            sl = slice(sp * SPAN, (sp + 1) * SPAN)
                            items = [(ctxa[mc][0:(128 if mc < 2 else 10), :],
                                      qpt[mc][0:(128 if mc < 2 else 10), sl])
                                     for mc in range(3)]
                            mm_group(psP0[0:65, 0:SPAN], items,
                                     reads=[qpt0, qpt1, qpt2] + ctxa, writes=[psP0])
                            scalar_copy(aof[hg][hr, sl], psP0[0:64, 0:SPAN],
                                        reads=[psP0], writes=[aof[hg]])
                            scalar_copy(dn_r[0:1, sl], psP0[64:65, 0:SPAN],
                                        reads=[psP0], writes=[denP2])
                    vop("reciprocal", denP2[0:33, :], reads=[denP2], writes=[denP2],
                        in_=denP2[0:33, :])
                    for sp in range(NSPAN):
                        sl = slice(sp * SPAN, (sp + 1) * SPAN)
                        def g0(eng, sl=sl):
                            return eng.matmul(out=psR[0:64, 0:SPAN], lhsT=allones[0:1, 0:64],
                                              rhs=dnA[0:1, sl], start=True, stop=True)
                        b.op("t", g0, reads=[allones, denP2], writes=[psR])
                        def g1(eng, sl=sl):
                            return eng.matmul(out=psR[64:128, 0:SPAN], lhsT=allones[32:33, 0:64],
                                              rhs=dnB[0:1, sl], start=True, stop=True)
                        b.op("t", g1, reads=[allones, denP2], writes=[psR])
                        vop("tensor_mul", aof[hg][:, sl], reads=[aof[hg], psR],
                            writes=[aof[hg]], in0=aof[hg][:, sl], in1=psR[0:128, 0:SPAN])
            def evo(ft, sp, ps):
                act(xln[ft][:, sp * SPAN:(sp + 1) * SPAN], ps[0:128, 0:SPAN],
                    ACT.Copy, reads=[ps, bias_sb], writes=[xln[ft]],
                    bias=bias_sb[:, 24 + ft:25 + ft])
            proj_fm(None, aof, H["l1a_wo"], D, evo)
            tws2 = b.sbuf_in(sc, f"{tag}_tws", [128, SPAN], BF16)
            def evf(ft, sp, ps):
                sl = slice(sp * SPAN, (sp + 1) * SPAN)
                act(tws2[:], ps[0:128, 0:SPAN], ACT.Relu, reads=[ps, bias_sb],
                    writes=[tws2], bias=bias_sb[:, 32 + ft:33 + ft])
                vop("tensor_add", xs[ft][:, sl], reads=[xs[ft], tws2], writes=[xs[ft]],
                    in0=xs[ft][:, sl], in1=tws2[:])
            proj_fm(None, xln, H["l1a_wfc"], D, evf)

    def l1_attn_b(tag="q1"):
        xsv = xspill.rearrange("(c p) t -> c p t", p=128)
        with b.scope() as sc:
            b.dma_group("gd", [(xsv[c], xs[c][:], [xs[c]], []) for c in range(KC)])
            emit_layernorm(xs, xs)
            trimask = b.sbuf_in(sc, f"{tag}_tm", [128, 4 * SPAN], BF16)
            b.dma("gd", trimask[:], H["c_trimask"], writes=[trimask])
            gam = b.sbuf_in(sc, f"{tag}_gam", [128, 48], F32)
            for i, g in enumerate(["g1", "b1", "g2", "b2", "g3", "b3"]):
                b.dma("y", gam[:, i * 8:(i + 1) * 8], H[f"l1b_{g}"][:, 0:8], writes=[gam])

            def gamma_of(i):
                return ((gam, lambda c, i=i: 2 * i * 8 + c),
                        (gam, lambda c, i=i: (2 * i + 1) * 8 + c))

            aof = [b.sbuf_in(sc, f"{tag}_ao{c}", [128, T], BF16) for c in range(KC)]
            tws3 = b.sbuf_in(sc, f"{tag}_tws", [128, SPAN], BF16)

            g1, b1_ = gamma_of(0)
            emit_layernorm(xs, xln, gamma=g1, beta=b1_)
            b.dma("y", bias_sb[:, 0:8], H["l1b_sfb"][:, 0:8], writes=[bias_sb])
            mstage = b.sbuf_in(sc, f"{tag}_mst", [128, 1024], F32)
            wvv = H["l1b_sfw"].rearrange("(kc p) f -> kc p f", p=128)
            mvv = H["l1b_mask"].rearrange("(kc p) f -> kc p f", p=128)
            for ft in range(KC):
                sg = ft % 2
                b.dma_group("y", [(wstage[sg][:, kc * 128:kc * 128 + 128],
                                   wvv[kc][:, ft * 128:(ft + 1) * 128], [], [wstage[sg]])
                                  for kc in range(KC)])
                b.dma_group("y", [(mstage[:, kc * 128:kc * 128 + 128],
                                   mvv[kc][:, ft * 128:(ft + 1) * 128], [], [mstage])
                                  for kc in range(KC)])
                vop("tensor_mul", wstage[sg][:], reads=[wstage[sg], mstage],
                    writes=[wstage[sg]], in0=wstage[sg][:], in1=mstage[:])
                cast_to("v", wtile[sg][:, 0:KC * 128], wstage[sg][:, 0:KC * 128],
                        reads=[wstage[sg]], writes=[wtile[sg]])
                for sp in range(NSPAN):
                    ps = psA if sp % 2 == 0 else psB
                    sl = slice(sp * SPAN, (sp + 1) * SPAN)
                    items = [(wtile[sg][:, kc * 128:kc * 128 + 128], xln[kc][:, sl])
                             for kc in range(KC)]
                    mm_group(ps[0:128, 0:SPAN], items, reads=[wtile[sg]] + xln, writes=[ps])
                    act(tws3[:], ps[0:128, 0:SPAN], ACT.Copy, reads=[ps, bias_sb],
                        writes=[tws3], bias=bias_sb[:, ft:ft + 1])
                    vop("tensor_add", xs[ft][:, sl], reads=[xs[ft], tws3], writes=[xs[ft]],
                        in0=xs[ft][:, sl], in1=tws3[:])

            g2, b2_ = gamma_of(1)
            emit_layernorm(xs, xln, gamma=g2, beta=b2_)
            for nm, col in [("bq", 8), ("bk", 16), ("bv", 24), ("bo", 32)]:
                b.dma("y", bias_sb[:, col:col + 8], H[f"l1b_{nm}"][:, 0:8], writes=[bias_sb])
            vop("tensor_scalar_mul", bias_sb[:, 8:16], reads=[bias_sb], writes=[bias_sb],
                in0=bias_sb[:, 8:16], scalar1=0.125)
            denP3 = b.sbuf_in(sc, f"{tag}_den", [64, T], F32)
            dn0 = denP3[0:1, :]
            dn1 = denP3[32:33, :]
            for hg in range(KC):
                hsl = slice(hg * 128, (hg + 1) * 128)
                with b.scope() as hs:
                    qh = b.sbuf_in(hs, f"{tag}_q{hg}", [128, T], BF16)
                    kh = b.sbuf_in(hs, f"{tag}_k{hg}", [128, T], BF16)
                    va = [b.sbuf_in(hs, f"{tag}_va{hg}_{t}", [128, 130], BF16)
                          for t in range(16)]
                    pt = b.sbuf_in(hs, f"{tag}_pt{hg}", [128, 16 * SPAN], BF16)
                    def evq(ft, sp, ps):
                        act(qh[:, sp * SPAN:(sp + 1) * SPAN], ps[0:128, 0:SPAN],
                            ACT.Copy, reads=[ps, bias_sb], writes=[qh], scale=0.125,
                            bias=bias_sb[:, 8 + hg:9 + hg])
                    def evk(ft, sp, ps):
                        act(kh[:, sp * SPAN:(sp + 1) * SPAN], ps[0:128, 0:SPAN],
                            ACT.Copy, reads=[ps, bias_sb], writes=[kh],
                            bias=bias_sb[:, 16 + hg:17 + hg])
                    proj_fm(None, xln, H["l1b_wq"][:, hsl], 128, evq)
                    proj_fm(None, xln, H["l1b_wk"][:, hsl], 128, evk)
                    sg = hg % 2
                    wvv2 = H["l1b_wv"].rearrange("(kc p) f -> kc p f", p=128)
                    b.dma_group("y", [(wstage[sg][:, kc * 128:kc * 128 + 128],
                                       wvv2[kc][:, hsl], [], [wstage[sg]])
                                      for kc in range(KC)])
                    cast_to("v", wtile[sg][:, 0:KC * 128], wstage[sg][:, 0:KC * 128],
                            reads=[wstage[sg]], writes=[wtile[sg]])
                    transpose(psA[0:1, 0:128], bias_sb[:, 24 + hg:25 + hg],
                              ident_f[:], reads=[bias_sb, ident_f], writes=[psA])
                    scalar_copy(dn0[0:1, 0:128], psA[0:1, 0:128], reads=[psA],
                                writes=[denP3])
                    for tt in range(16):
                        ps = psB if tt % 2 == 0 else psP0
                        items = [(xln[kc][:, tt * 128:tt * 128 + 128],
                                  wtile[sg][:, kc * 128:kc * 128 + 128]) for kc in range(KC)]
                        items.append((allones[0:1, 0:128], dn0[0:1, 0:128]))
                        mm_group(ps[0:128, 0:128], items,
                                 reads=[wtile[sg], denP3] + list(xln), writes=[ps])
                        scalar_copy(va[tt][:, 0:64], ps[0:128, 0:64], reads=[ps], writes=[va[tt]])
                        scalar_copy(va[tt][:, 65:129], ps[0:128, 64:128], reads=[ps], writes=[va[tt]])
                        vop("memset", va[tt][:, 64:65], reads=[], writes=[va[tt]], constant=1.0)
                        vop("memset", va[tt][:, 129:130], reads=[], writes=[va[tt]], constant=1.0)
                    for hh in range(2):
                        hr = slice(hh * 64, hh * 64 + 64)
                        dnr = dn0 if hh == 0 else dn1
                        for bb in range(NSPAN):
                            bsl = slice(bb * SPAN, (bb + 1) * SPAN)
                            nch = 4 * bb + 4
                            for ccn in range(nch):
                                ps = psS0 if ccn % 2 == 0 else psS1
                                mm_group(ps[0:128, 0:SPAN],
                                         [(kh[hr, ccn * 128:(ccn + 1) * 128], qh[hr, bsl])],
                                         reads=[kh, qh], writes=[ps])
                                u = ccn - 4 * bb
                                if 0 <= u <= 3:
                                    vop("tensor_add", ps[0:128, 0:SPAN], reads=[ps, trimask],
                                        writes=[ps], in0=ps[0:128, 0:SPAN],
                                        in1=trimask[:, u * SPAN:(u + 1) * SPAN])
                                act(pt[:, ccn * SPAN:(ccn + 1) * SPAN], ps[0:128, 0:SPAN],
                                    ACT.Relu, reads=[ps], writes=[pt])
                            items = [(va[ccn][:, hh * 65:hh * 65 + 65],
                                      pt[:, ccn * SPAN:(ccn + 1) * SPAN])
                                     for ccn in range(nch)]
                            mm_group(psP0[0:65, 0:SPAN], items,
                                     reads=[pt] + va, writes=[psP0])
                            scalar_copy(aof[hg][hr, bsl], psP0[0:64, 0:SPAN],
                                        reads=[psP0], writes=[aof[hg]])
                            scalar_copy(dnr[0:1, bsl], psP0[64:65, 0:SPAN],
                                        reads=[psP0], writes=[denP3])
                    vop("tensor_scalar_add", denP3[0:33, :], reads=[denP3], writes=[denP3],
                        in0=denP3[0:33, :], scalar1=1e-9)
                    vop("reciprocal", denP3[0:33, :], reads=[denP3], writes=[denP3],
                        in_=denP3[0:33, :])
                    for sp in range(NSPAN):
                        sl = slice(sp * SPAN, (sp + 1) * SPAN)
                        def g0(eng, sl=sl):
                            return eng.matmul(out=psR[0:64, 0:SPAN], lhsT=allones[0:1, 0:64],
                                              rhs=dn0[0:1, sl], start=True, stop=True)
                        b.op("t", g0, reads=[allones, denP3], writes=[psR])
                        def g1_(eng, sl=sl):
                            return eng.matmul(out=psR[64:128, 0:SPAN], lhsT=allones[32:33, 0:64],
                                              rhs=dn1[0:1, sl], start=True, stop=True)
                        b.op("t", g1_, reads=[allones, denP3], writes=[psR])
                        vop("tensor_mul", aof[hg][:, sl], reads=[aof[hg], psR],
                            writes=[aof[hg]], in0=aof[hg][:, sl], in1=psR[0:128, 0:SPAN])
            def evo2(ft, sp, ps):
                sl = slice(sp * SPAN, (sp + 1) * SPAN)
                act(tws3[:], ps[0:128, 0:SPAN], ACT.Copy, reads=[ps, bias_sb],
                    writes=[tws3], bias=bias_sb[:, 32 + ft:33 + ft])
                vop("tensor_add", xs[ft][:, sl], reads=[xs[ft], tws3], writes=[xs[ft]],
                    in0=xs[ft][:, sl], in1=tws3[:])
            proj_fm(None, aof, H["l1b_wo"], D, evo2)

            g3, b3_ = gamma_of(2)
            emit_layernorm(xs, xln, gamma=g3, beta=b3_)
            b.dma("y", bias_sb[:, 0:32], H["l1b_ff1b"][:, 0:32], writes=[bias_sb])
            b.dma("y", bias_sb[:, 32:40], H["l1b_ff2b"][:, 0:8], writes=[bias_sb])
            hsp = b.sbuf_in(sc, f"{tag}_hsp", [128, 32 * SPAN], BF16)
            w2v = H["l1b_ff2w"].rearrange("(kc p) f -> kc p f", p=128)
            for sp in range(NSPAN):
                sl = slice(sp * SPAN, (sp + 1) * SPAN)
                def evh(ft, sp2, ps, sp=sp):
                    act(hsp[:, ft * SPAN:(ft + 1) * SPAN], ps[0:128, 0:SPAN],
                        ACT.Relu, reads=[ps, bias_sb], writes=[hsp],
                        bias=bias_sb[:, ft:ft + 1])
                proj_fm(None, xln, H["l1b_ff1w"], 4 * D, evh, spans=[sp])
                for ft in range(KC):
                    for kq in range(4):
                        sg = kq % 2
                        b.dma_group("y", [(wstage[sg][:, kc * 128:kc * 128 + 128],
                                           w2v[kq * 8 + kc][:, ft * 128:(ft + 1) * 128],
                                           [], [wstage[sg]]) for kc in range(KC)])
                        cast_to("v", wtile[sg][:, 0:KC * 128], wstage[sg][:, 0:KC * 128],
                                reads=[wstage[sg]], writes=[wtile[sg]])
                        def fn(eng, sg=sg, kq=kq, ft=ft, sl=sl):
                            inst = None
                            for kc in range(KC):
                                inst = eng.matmul(
                                    out=psA[0:128, 0:SPAN],
                                    lhsT=wtile[sg][:, kc * 128:kc * 128 + 128],
                                    rhs=hsp[:, (kq * 8 + kc) * SPAN:(kq * 8 + kc + 1) * SPAN],
                                    start=(kq == 0 and kc == 0),
                                    stop=(kq == 3 and kc == KC - 1))
                            return inst
                        b.op("t", fn, reads=[wtile[sg], hsp], writes=[psA])
                    act(tws3[:], psA[0:128, 0:SPAN], ACT.Copy, reads=[psA, bias_sb],
                        writes=[tws3], bias=bias_sb[:, 32 + ft:33 + ft])
                    vop("tensor_add", xs[ft][:, sl], reads=[xs[ft], tws3], writes=[xs[ft]],
                        in0=xs[ft][:, sl], in1=tws3[:])

            for c in range(KC):
                for h in range(2):
                    ob = wstage[h]
                    hc = slice(h * 1024, (h + 1) * 1024)
                    b.dma("y", ob[:], xsv[c][:, hc], writes=[ob])
                    vop("tensor_add", xs[c][:, hc], reads=[xs[c], ob], writes=[xs[c]],
                        in0=xs[c][:, hc], in1=ob[:])

    def store_out(src_chunks):
        ov = out_ap.rearrange("(c p) t -> c p t", p=128)
        for c in range(KC):
            b.dma("gd", ov[c], src_chunks[c][:], reads=[src_chunks[c]])

    if trunc >= 1:
        l0_attn_a()
    if trunc >= 2:
        l0_ffn("f0a")
    if trunc >= 3:
        l0_attn_b()
    if trunc >= 4:
        l0_ffn("f0b")
    if trunc >= 5:
        l1_attn_a()
    if trunc >= 6:
        l1_ffn("f1a")
    if trunc >= 7:
        l1_attn_b()
    if trunc >= 8:
        l1_ffn("f1b")
    store_out(xs)
    b.finalize(["out"])
    return nc


def _consts():
    c = {}
    c["c_ident_bf"] = np.eye(128, dtype=np.float32)
    c["c_ident_f32"] = np.eye(128, dtype=np.float32)
    c["c_ones_col"] = np.ones((128, 1), dtype=np.float32)
    c["c_ones_colf"] = np.ones((128, 1), dtype=np.float32)
    c["c_ones_row"] = np.ones((1, 128), dtype=np.float32)
    sel2 = np.zeros((2, 128), dtype=np.float32)
    sel2[0, 0:64] = 1.0
    sel2[1, 64:128] = 1.0
    c["c_sel2"] = sel2
    tm = np.zeros((128, 4 * SPAN), dtype=np.float32)
    r = np.arange(128)[:, None]
    col = np.arange(SPAN)[None, :]
    for u in range(4):
        tm[:, u * SPAN:(u + 1) * SPAN] = np.where(col < r + 128 * u, NEG, 0.0)
    c["c_trimask"] = tm
    c["c_slotmask"] = np.where(col < 4 * (r + 1), NEG, 0.0).astype(np.float32)
    had = np.array([[1, 1, 1, 1], [1, -1, 1, -1], [1, 1, -1, -1], [1, -1, -1, 1]],
                   dtype=np.float32)
    cdma = np.tile(had, (1, 16))
    cs = np.zeros((128, 4), dtype=np.float32)
    for p in range(128):
        cs[p, :] = cdma[:, p % 64]
    c["c_csign"] = cs
    base = 10000.0
    dh = 64
    inv = 1.0 / base ** (np.arange(0, dh, 2, dtype=np.float32) / dh)
    pos = np.arange(T, dtype=np.float32)
    fr = pos[:, None] * inv[None, :]
    emb = np.concatenate([fr, fr], axis=-1)
    cosv = np.cos(emb).T
    sinv = np.sin(emb).T
    cos2 = np.tile(cosv, (2, 1)).astype(np.float32)
    sin2 = np.tile(sinv, (2, 1)).astype(np.float32)
    csT = np.zeros((128, T), dtype=np.float32)
    for p in range(128):
        csT[p, :] = cdma[np.arange(T) % 4, p % 64]
    c["c_ropec"] = cos2
    c["c_ropes"] = sin2
    rsign = np.where((np.arange(128) % 64) < 32, -1.0, 1.0).astype(np.float32)[:, None]
    c["c_ropecc"] = cos2 * csT
    c["c_ropesc"] = sin2 * csT * rsign
    return c


_BF16_INPUTS = {"c_ident_bf", "c_ones_col", "c_trimask", "c_slotmask"}


def _to_ml_bf16(a):
    import ml_dtypes
    return np.ascontiguousarray(a).astype(ml_dtypes.bfloat16)


def kernel(**inputs):
    trunc = TRUNC
    nc = build_program(trunc)
    consts = _consts()

    def TT(a):
        return np.ascontiguousarray(np.asarray(a).T.astype(np.float32))

    def BIAS(a, nft=None):
        a = np.asarray(a).astype(np.float32)
        n = a.shape[0] // 128
        return np.ascontiguousarray(a.reshape(n, 128).T) if a.shape[0] >= 128 else \
            np.ascontiguousarray(a.reshape(1, -1)).T.reshape(128, -1) if False else \
            np.pad(a.reshape(1, -1), ((0, 0), (0, 0))).T.reshape(-1, 1) * np.ones((1, 1), np.float32)

    def BIAS128(a):
        a = np.asarray(a).astype(np.float32)
        if a.size % 128 == 0 and a.size >= 128:
            return np.ascontiguousarray(a.reshape(-1, 128).T)
        out = np.zeros((128, 1), dtype=np.float32)
        out[:a.size, 0] = a
        return out

    base = {}
    base["l0a_wqkv"] = np.ascontiguousarray(inputs["L0_attn_a_W_qkv"].astype(np.float32))
    for i in range(1, 5):
        base[f"cn_w{i}"] = np.ascontiguousarray(inputs[f"L0_ffn_cn_w{i}"].astype(np.float32))
        base[f"cn_b{i}"] = BIAS128(inputs[f"L0_ffn_cn_b{i}"])
    for i in range(1, 4):
        base[f"gc_w{i}"] = np.ascontiguousarray(inputs[f"L0_ffn_gc_w{i}"].astype(np.float32))
        base[f"gc_b{i}"] = BIAS128(inputs[f"L0_ffn_gc_b{i}"])
    base["ar"] = inputs["L0_ffn_adaptive_range"].reshape(1, 1).astype(np.float32)
    base["l0b_wq"] = np.ascontiguousarray(inputs["L0_attn_b_Wq"].astype(np.float32))
    base["l0b_wk"] = np.ascontiguousarray(inputs["L0_attn_b_Wk"].astype(np.float32))
    base["l0b_wv"] = np.ascontiguousarray(inputs["L0_attn_b_Wv"].astype(np.float32))
    base["l0b_wo"] = np.ascontiguousarray(inputs["L0_attn_b_Wo"].astype(np.float32))
    base["l0b_aq"] = np.ascontiguousarray(inputs["L0_attn_b_Aq"].astype(np.float32))
    base["l0b_av"] = np.ascontiguousarray(inputs["L0_attn_b_Av"].astype(np.float32))
    base["l0b_ao"] = np.ascontiguousarray(inputs["L0_attn_b_Ao"].astype(np.float32))
    base["l0b_bq"] = np.ascontiguousarray(inputs["L0_attn_b_Bq"].astype(np.float32))
    base["l0b_bv"] = np.ascontiguousarray(inputs["L0_attn_b_Bv"].astype(np.float32))
    base["l0b_bo"] = np.ascontiguousarray(inputs["L0_attn_b_Bo"].astype(np.float32))
    for w in ["Wq", "Wk", "Wv", "Wo", "Wfc"]:
        base[f"l1a_{w.lower()}"] = TT(inputs[f"L1_attn_a_{w}"])
    for w in ["bq", "bk", "bv", "bo", "bfc"]:
        base[f"l1a_{w}"] = BIAS128(inputs[f"L1_attn_a_{w}"])
    base["l1a_projT"] = TT(inputs["L1_attn_a_proj"])
    base["l1b_sfw"] = TT(inputs["L1_attn_b_sf_w"])
    base["l1b_mask"] = TT(inputs["L1_attn_b_mask"])
    base["l1b_sfb"] = BIAS128(inputs["L1_attn_b_sf_b"])
    for w in ["wq", "wk", "wv", "wo"]:
        base[f"l1b_{w}"] = TT(inputs[f"L1_attn_b_{w}"])
        base[f"l1b_b{w[1]}"] = BIAS128(inputs[f"L1_attn_b_b{w[1]}"])
    base["l1b_ff1w"] = TT(inputs["L1_attn_b_ff1_w"])
    base["l1b_ff1b"] = BIAS128(inputs["L1_attn_b_ff1_b"])
    base["l1b_ff2w"] = TT(inputs["L1_attn_b_ff2_w"])
    base["l1b_ff2b"] = BIAS128(inputs["L1_attn_b_ff2_b"])
    for g in ["g1", "b1", "g2", "b2", "g3", "b3"]:
        base[f"l1b_{g}"] = BIAS128(inputs[f"L1_attn_b_{g}"])
    base["l1f_wr"] = TT(inputs["L1_ffn_W_r"])
    base["l1f_wk"] = TT(inputs["L1_ffn_W_k"])
    base["l1f_wv"] = TT(inputs["L1_ffn_W_v"])
    base["l1f_mur"] = BIAS128(inputs["L1_ffn_mu_r"])
    base["l1f_muk"] = BIAS128(inputs["L1_ffn_mu_k"])
    base.update(consts)
    for k in list(base.keys()):
        if k in _BF16_INPUTS:
            base[k] = _to_ml_bf16(base[k])

    x = np.asarray(inputs["L0_attn_a_x"]).astype(np.float32)
    in_maps = []
    for core in range(8):
        m = dict(base)
        m["x"] = np.ascontiguousarray(x[core].T)
        in_maps.append(m)

    res = run_bass_kernel_spmd(nc, in_maps, core_ids=list(range(8)))
    if MK_DEBUG:
        import __main__
        __main__.MK_DBG_RESULTS = res.results
    outs = [np.asarray(r["out"]).T for r in res.results]
    return np.stack(outs, axis=0).astype(np.float32)


if __name__ == "__main__":
    print("build test")
    nc = build_program(1)
    print("instructions:", nc.n_instructions() if hasattr(nc, "n_instructions") else "?")


# version 04
# speedup vs baseline: 1.0228x; 1.0228x over previous
import os
import numpy as np
from contextlib import ExitStack

import concourse.bass as bass
import concourse.mybir as mybir
from concourse.bass_utils import run_bass_kernel_spmd

F32 = mybir.dt.float32
BF16 = mybir.dt.bfloat16
AX = mybir.AxisListType.X
ALU = mybir.AluOpType
ACT = mybir.ActivationFunctionType

T = 2048
D = 1024
KC = 8
NSPAN = 4
SPAN = 512
NH = 16
DH = 64
NEG = -1.0e30

TRUNC = int(os.environ.get("MK_TRUNC", "99"))
MK_DEBUG = os.environ.get("MK_DEBUG", "")


class B:
    COMPUTE = ("t", "v", "s", "g")
    QUEUES = ("y", "gd")
    POOL = {"y": 20, "gd": 8}

    def __init__(self, nc):
        self.nc = nc
        self.st = ExitStack()
        self.sems = {e: self.st.enter_context(nc.semaphore(f"sem_{e}"))
                     for e in self.COMPUTE}
        self.qsems = {}
        self.qtot = {}
        self.qptr = {}
        for q in self.QUEUES:
            self.qsems[q] = [self.st.enter_context(nc.semaphore(f"qs_{q}{i}"))
                             for i in range(self.POOL[q])]
            self.qtot[q] = [0] * self.POOL[q]
            self.qptr[q] = 0
        self.seq = {e: 0 for e in self.COMPUTE + self.QUEUES}
        self.progs = {e: [] for e in self.COMPUTE + self.QUEUES}
        self.last_write = {}
        self.readers = {}
        self.watermark = {e: {} for e in self.COMPUTE + self.QUEUES}
        self._names = set()
        self._sbufs = {}
        self.pending_free = {}

    def sbuf(self, name, shape, dtype):
        assert name not in self._names, name
        self._names.add(name)
        h = self.st.enter_context(self.nc.sbuf_tensor(name, list(shape), dtype))
        return h

    def psum(self, name, shape, dtype):
        h = self.st.enter_context(self.nc.psum_tensor(name, list(shape), dtype))
        return h

    def scope(self):
        return ExitStack()

    def sbuf_in(self, stack, name, shape, dtype):
        assert name not in self._names, name
        self._names.add(name)
        h = stack.enter_context(self.nc.sbuf_tensor(name, list(shape), dtype))
        pend = getattr(self, "pending_free", None)
        if pend:
            self.readers[name] = [key + (v,) for key, v in pend.items()]
        def _cleanup(nm=name):
            self._names.discard(nm)
            pendd = self.pending_free
            lw = self.last_write.pop(nm, None)
            toks = list(self.readers.pop(nm, ()) or ())
            if lw is not None:
                toks.append(lw)
            for tok in toks:
                key, v = tok[:-1], tok[-1]
                if pendd.get(key, 0) < v:
                    pendd[key] = v
        stack.callback(_cleanup)
        return h

    @staticmethod
    def _key(x):
        t = getattr(x, "tensor", x)
        return t.name

    def _collect(self, reads, writes):
        waits = {}
        def need(tok):
            key = tok[:-1]
            v = tok[-1]
            if key not in waits or v > waits[key]:
                waits[key] = v
        for r in reads:
            lw = self.last_write.get(self._key(r))
            if lw:
                need(lw)
        for w in writes:
            k = self._key(w)
            lw = self.last_write.get(k)
            if lw:
                need(lw)
            for tok in self.readers.get(k, ()):
                need(tok)
        return waits

    def _emit_waits(self, e, waits):
        wl = []
        for key, v in waits.items():
            if key[0] == "E" and key[1] == e:
                pass
            if self.watermark[e].get(key, 0) >= v:
                continue
            self.watermark[e][key] = v
            wl.append((key, v))
        return wl

    def op(self, e, fn, reads=(), writes=()):
        assert e in self.COMPUTE
        waits = self._collect(reads, writes)
        idx = self.seq[e] + 1
        self.seq[e] = idx
        for r in reads:
            self.readers.setdefault(self._key(r), []).append(("E", e, idx))
        for w in writes:
            k = self._key(w)
            self.last_write[k] = ("E", e, idx)
            self.readers[k] = []
        wl = self._emit_waits(e, waits)
        self.progs[e].append((wl, fn, e))

    def dma_group(self, q, items):
        assert q in self.QUEUES
        reads = []
        writes = []
        for (_o, _i, rds, wts) in items:
            reads += list(rds)
            writes += list(wts)
        waits = self._collect(reads, writes)
        si = self.qptr[q]
        self.qptr[q] = (si + 1) % self.POOL[q]
        gate = self.qtot[q][si]
        self.qtot[q][si] = gate + 16 * len(items)
        tot = self.qtot[q][si]
        for r in reads:
            self.readers.setdefault(self._key(r), []).append(("Q", q, si, tot))
        for w in writes:
            k = self._key(w)
            self.last_write[k] = ("Q", q, si, tot)
            self.readers[k] = []
        wl = self._emit_waits(q, waits)
        aps = [(o, i) for (o, i, _r, _w) in items]
        self.progs[q].append((wl, (si, gate, aps), q))
        return ("Q", q, si, tot)

    def dma(self, q, out_ap, in_ap, reads=(), writes=()):
        self.dma_group(q, [(out_ap, in_ap, reads, writes)])

    def finalize(self, out_keys):
        nc = self.nc
        sems = self.sems
        qsems = self.qsems

        def wait_tok(eng, key, v):
            if key[0] == "E":
                eng.wait_ge(sems[key[1]], v)
            else:
                eng.wait_ge(qsems[key[1]][key[2]], v)

        def runner(e):
            def run(eng):
                for wl, fn, kind in self.progs[e]:
                    for (key, v) in wl:
                        wait_tok(eng, key, v)
                    if e in self.COMPUTE:
                        inst = fn(eng)
                        inst.then_inc(sems[e], 1)
                    else:
                        si, gate, aps = fn
                        if gate > 0:
                            eng.wait_ge(qsems[e][si], gate)
                        for (o, i) in aps:
                            eng.dma_start(out=o, in_=i).then_inc(
                                qsems[e][si], 16)
                if e == "gd":
                    for si2 in range(self.POOL["gd"]):
                        if self.qtot["gd"][si2] > 0:
                            eng.wait_ge(qsems["gd"][si2], self.qtot["gd"][si2])
            return run

        with nc.Block() as blk:
            blk.sync(lambda eng: runner("y")(eng))
            blk.vector(lambda eng: runner("v")(eng))
            blk.scalar(lambda eng: runner("s")(eng))
            blk.tensor(lambda eng: runner("t")(eng))

            def gps(eng):
                runner("g")(eng)
                runner("gd")(eng)
            blk.gpsimd(gps)
        self.st.close()


def build_program(trunc=99):
    nc = bass.Bass("TRN2", target_bir_lowering=False)
    b = B(nc)

    def din(name, shape, dtype=F32):
        return nc.dram_tensor(name, list(shape), dtype, kind="ExternalInput").ap()

    H = {}
    H["x"] = din("x", [D, T])
    H["l0a_wqkv"] = din("l0a_wqkv", [D, 3 * D])
    for i, (a, c) in enumerate([(D, D), (D, 512), (512, 256), (256, 1)]):
        H[f"cn_w{i+1}"] = din(f"cn_w{i+1}", [a, c])
        H[f"cn_b{i+1}"] = din(f"cn_b{i+1}", [128, max(1, c // 128) if c >= 128 else 1])
    for i, (a, c) in enumerate([(D, 512), (512, 256), (256, 1)]):
        H[f"gc_w{i+1}"] = din(f"gc_w{i+1}", [a, c])
        H[f"gc_b{i+1}"] = din(f"gc_b{i+1}", [128, max(1, c // 128) if c >= 128 else 1])
    H["ar"] = din("ar", [1, 1])
    for w in ["wq", "wk", "wv", "wo"]:
        H[f"l0b_{w}"] = din(f"l0b_{w}", [D, D])
    for w in ["aq", "av", "ao"]:
        H[f"l0b_{w}"] = din(f"l0b_{w}", [D, 64])
    for w in ["bq", "bv", "bo"]:
        H[f"l0b_{w}"] = din(f"l0b_{w}", [64, D])
    for w in ["wq", "wk", "wv", "wo", "wfc"]:
        H[f"l1a_{w}"] = din(f"l1a_{w}", [D, D])
    for w in ["bq", "bk", "bv", "bo", "bfc"]:
        H[f"l1a_{w}"] = din(f"l1a_{w}", [128, 8])
    H["l1a_projT"] = din("l1a_projT", [64, 266])
    H["l1b_sfw"] = din("l1b_sfw", [D, D])
    H["l1b_mask"] = din("l1b_mask", [D, D])
    H["l1b_sfb"] = din("l1b_sfb", [128, 8])
    for w in ["wq", "wk", "wv", "wo"]:
        H[f"l1b_{w}"] = din(f"l1b_{w}", [D, D])
    for w in ["bq", "bk", "bv", "bo", "bo2"][:4]:
        H[f"l1b_{w}"] = din(f"l1b_{w}", [128, 8])
    H["l1b_ff1w"] = din("l1b_ff1w", [D, 4 * D])
    H["l1b_ff1b"] = din("l1b_ff1b", [128, 32])
    H["l1b_ff2w"] = din("l1b_ff2w", [4 * D, D])
    H["l1b_ff2b"] = din("l1b_ff2b", [128, 8])
    for g in ["g1", "b1", "g2", "b2", "g3", "b3"]:
        H[f"l1b_{g}"] = din(f"l1b_{g}", [128, 8])
    for w in ["wr", "wk", "wv"]:
        H[f"l1f_{w}"] = din(f"l1f_{w}", [D, D])
    H["l1f_mur"] = din("l1f_mur", [128, 8])
    H["l1f_muk"] = din("l1f_muk", [128, 8])
    H["c_ident_bf"] = din("c_ident_bf", [128, 128], BF16)
    H["c_ident_f32"] = din("c_ident_f32", [128, 128])
    H["c_ones_col"] = din("c_ones_col", [128, 1], BF16)
    H["c_ones_colf"] = din("c_ones_colf", [128, 1])
    H["c_ones_row"] = din("c_ones_row", [1, 128])
    H["c_sel2"] = din("c_sel2", [2, 128])
    H["c_trimask"] = din("c_trimask", [128, 4 * SPAN], BF16)
    H["c_slotmask"] = din("c_slotmask", [128, SPAN], BF16)
    H["c_csign"] = din("c_csign", [128, 4])
    H["c_ropecc"] = din("c_ropecc", [128, T])
    H["c_ropesc"] = din("c_ropesc", [128, T])
    H["c_ropec"] = din("c_ropec", [128, T])
    H["c_ropes"] = din("c_ropes", [128, T])

    out_ap = nc.dram_tensor("out", [D, T], F32, kind="ExternalOutput").ap()
    xspill = nc.dram_tensor("xspill", [D, T], F32, kind="Internal").ap()

    xs = [b.sbuf(f"xs{c}", [128, T], F32) for c in range(KC)]
    xln = [b.sbuf(f"xln{c}", [128, T], BF16) for c in range(KC)]
    ident_bf = b.sbuf("ident_bf", [128, 128], BF16)
    ident_f = b.sbuf("ident_f", [128, 128], F32)
    ones_col = b.sbuf("ones_col", [128, 1], BF16)
    ones_colf = b.sbuf("ones_colf", [128, 1], F32)
    ones_row = b.sbuf("ones_row", [1, 128], F32)
    sel2 = b.sbuf("sel2", [2, 128], F32)
    allones = b.sbuf("allones", [128, 128], F32)
    slotmask = b.sbuf("slotmask", [128, SPAN], BF16)
    csign = b.sbuf("csign", [128, 4], F32)
    lnrows_t = b.sbuf("lnrows_t", [128, T], F32)
    lnrow = lnrows_t[0:1, :]
    lnrow2 = lnrows_t[32:33, :]
    lnrow3 = lnrows_t[64:65, :]

    psA = b.psum("psA", [128, 512], F32)
    psS01 = b.psum("psS01", [128, 1024], F32)
    psS23 = b.psum("psS23", [128, 1024], F32)
    psS0 = psS01[0:128, 0:512]
    psS1 = psS23[0:128, 0:512]
    psB = psS01[0:128, 512:1024]
    psR2 = psS23[0:128, 512:1024]
    psP0 = b.psum("psP0", [128, 512], F32)
    psR = b.psum("psR", [128, 512], F32)
    psT = b.psum("psT", [128, 1024], BF16)

    def _ao_ms(eng):
        return eng.memset(allones[:], 1.0)
    b.op("v", _ao_ms, reads=[], writes=[allones])
    for dst, src in [(ident_bf, H["c_ident_bf"]), (ident_f, H["c_ident_f32"]),
                     (ones_col, H["c_ones_col"]), (ones_colf, H["c_ones_colf"]),
                     (ones_row, H["c_ones_row"]), (sel2, H["c_sel2"]),
                     (slotmask, H["c_slotmask"]),
                     (csign, H["c_csign"])]:
        b.dma("gd", dst[:], src, writes=[dst])

    xv = H["x"].rearrange("(c p) t -> c p t", p=128)
    for c in range(KC):
        b.dma("gd", xs[c][:], xv[c], writes=[xs[c]])

    vop_early = []
    uid = [0]
    def U(pfx):
        uid[0] += 1
        return f"{pfx}{uid[0]}"

    dbg_outs = []
    def dbg(name, ap_list):
        if not MK_DEBUG or name not in MK_DEBUG.split(","):
            return
        for i, ap in enumerate(ap_list):
            p, f = ap.shape[0], ap.shape[1]
            dt = ap.dtype
            o = nc.dram_tensor(f"dbg_{name}_{i}", [p, f], dt, kind="ExternalOutput").ap()
            b.dma("gd", o, ap, reads=[ap])
            dbg_outs.append(f"dbg_{name}_{i}")

    def mm_group(ps_ap, items, reads, writes, extra=()):
        n = len(items)
        def fn(eng, items=tuple(items), ps=ps_ap, n=n):
            inst = None
            for i, (lt, rh) in enumerate(items):
                inst = eng.matmul(out=ps, lhsT=lt, rhs=rh,
                                  start=(i == 0), stop=(i == n - 1))
            return inst
        b.op("t", fn, reads=list(reads), writes=list(writes))

    def transpose(ps_ap, in_ap, ident_ap, reads, writes):
        def fn(eng, o=ps_ap, i=in_ap, idn=ident_ap):
            return eng.transpose(out=o, in_=i, identity=idn)
        b.op("t", fn, reads=list(reads), writes=list(writes))

    def act(out_ap, in_ap, func, reads, writes, bias=0.0, scale=1.0, accum=None):
        if func == ACT.Copy and not isinstance(bias, float):
            func = ACT.Identity
        def fn(eng, o=out_ap, i=in_ap, f=func, bi=bias, sc=scale, ac=accum):
            kw = {}
            if ac is not None:
                kw["accum_out"] = ac
            return eng.activation(out=o, in_=i, func=f, bias=bi, scale=sc, **kw)
        b.op("s", fn, reads=list(reads), writes=list(writes))

    def vop(name, out_ap, reads, writes, **kw):
        def fn(eng, nm=name, o=out_ap, kw=kw):
            if nm == "memset":
                return eng.memset(o, kw["constant"])
            return getattr(eng, nm)(out=o, **kw)
        b.op("v", fn, reads=list(reads), writes=list(writes))

    def scalar_copy(out_ap, in_ap, reads, writes):
        def fn(eng, o=out_ap, i=in_ap):
            return eng.copy(out=o, in_=i)
        b.op("s", fn, reads=list(reads), writes=list(writes))

    def cast_to(e, out_ap, in_ap, reads, writes):
        if e == "s":
            scalar_copy(out_ap, in_ap, reads, writes)
        else:
            vop("tensor_copy", out_ap, reads, writes, in_=in_ap)

    def bcast_row(ps_ap, row_ap, m, reads, base=0):
        def fn(eng, o=ps_ap, r=row_ap, m=m, base=base):
            return eng.matmul(out=o, lhsT=allones[base:base + 1, 0:m], rhs=r,
                              start=True, stop=True)
        b.op("t", fn, reads=[allones] + list(reads), writes=[ps_ap])

    wstage = [b.sbuf(f"wstage{i}", [128, 1024], F32) for i in range(2)]
    wtile = [b.sbuf(f"wtile{i}", [128, 1024], BF16) for i in range(2)]
    bias_sb = b.sbuf("bias_sb", [128, 64], F32)

    def load_bias(dram_ap, nft):
        b.dma("y", bias_sb[:, 0:nft], dram_ap[:, 0:nft], writes=[bias_sb])

    QROT = ["y", "gd"]

    def proj_fm(dst, src, w_dram, F_out, evac, src_rows=KC, w_row_off=0,
                extra_mm=None, spans=range(NSPAN), cast_e="v", src_get=None,
                src_reads=None):
        nft = F_out // 128
        wv = w_dram.rearrange("(kc p) f -> kc p f", p=128)
        if src_get is None:
            src_get = lambda kc, sp: src[kc][:, sp * SPAN:(sp + 1) * SPAN]
            src_reads = src
        spans = list(spans)
        banks = [psA, psB, psS0, psS1][:len(spans)]
        for ft in range(nft):
            sg = ft % 2
            b.dma_group(QROT[ft % 2],
                        [(wstage[sg][:, kc * 128:kc * 128 + 128],
                          wv[kc][:, ft * 128:(ft + 1) * 128], [], [wstage[sg]])
                         for kc in range(src_rows)])
            cast_to(cast_e, wtile[sg][:, 0:src_rows * 128], wstage[sg][:, 0:src_rows * 128],
                    reads=[wstage[sg]], writes=[wtile[sg]])
            ex_items = {j: [] for j in range(len(spans))}
            ex_reads = {j: [] for j in range(len(spans))}
            if extra_mm is not None:
                for j, sp in enumerate(spans):
                    for (lt, rh, rds) in extra_mm(ft, sp):
                        ex_items[j].append((lt, rh))
                        ex_reads[j] += rds
            def fn(eng, sg=sg, spans=tuple(spans), banks=tuple(banks),
                   ex=ex_items, ft=ft):
                inst = None
                for kc in range(src_rows):
                    for j, sp in enumerate(spans):
                        inst = eng.matmul(out=banks[j][0:128, 0:SPAN],
                                          lhsT=wtile[sg][:, kc * 128:kc * 128 + 128],
                                          rhs=src_get(kc, sp),
                                          start=(kc == 0),
                                          stop=(kc == src_rows - 1 and not ex[j]))
                for j in range(len(spans)):
                    for i2, (lt, rh) in enumerate(ex[j]):
                        inst = eng.matmul(out=banks[j][0:128, 0:SPAN], lhsT=lt, rhs=rh,
                                          start=False, stop=(i2 == len(ex[j]) - 1))
                return inst
            reads = [wtile[sg]] + list(src_reads)
            for j in range(len(spans)):
                reads += ex_reads[j]
            b.op("t", fn, reads=reads, writes=list(banks))
            for j, sp in enumerate(spans):
                evac(ft, sp, banks[j])

    sq_ps = [psS0, psS1, psP0, psR2]
    sqbuf = b.sbuf("sqbuf", [128, T], BF16)

    def emit_ln_stats(src_chunks, tag="ln"):
        tmp_bf = xln
        for c in range(KC):
            cast_to("v" if c % 2 == 0 else "s", tmp_bf[c][:], src_chunks[c][:],
                    reads=[src_chunks[c]], writes=[tmp_bf[c]])
        for c in range(KC):
            def sqf(eng, i=tmp_bf[c], o=sqbuf):
                return eng.activation(out=o[:], in_=i[:], func=ACT.Square)
            b.op("s", sqf, reads=[tmp_bf[c]], writes=[sqbuf])
            for sp in range(NSPAN):
                sl = slice(sp * SPAN, (sp + 1) * SPAN)
                ps = sq_ps[sp]
                def fn(eng, ps=ps, sl=sl, c=c):
                    return eng.matmul(out=ps[0:1, 0:SPAN], lhsT=ones_col[:],
                                      rhs=sqbuf[:, sl], start=(c == 0),
                                      stop=(c == KC - 1))
                b.op("t", fn, reads=[ones_col, sqbuf], writes=[ps])
        for sp in range(NSPAN):
            sl = slice(sp * SPAN, (sp + 1) * SPAN)
            items = [(ones_col[:], tmp_bf[c][:, sl]) for c in range(KC)]
            mm_group(psR[0:1, 0:SPAN], items,
                     reads=[ones_col] + list(tmp_bf), writes=[psR])
            act(lnrow3[0:1, sl], psR[0:1, 0:SPAN], ACT.Copy, reads=[psR],
                writes=[lnrow3], scale=1.0 / D)
            act(lnrow[0:1, sl], sq_ps[sp][0:1, 0:SPAN], ACT.Copy,
                reads=[sq_ps[sp]], writes=[lnrow], scale=1.0 / D)
        vop("tensor_mul", lnrow2[0:1, :], reads=[lnrow3], writes=[lnrow2],
            in0=lnrow3[0:1, :], in1=lnrow3[0:1, :])
        for sp in range(NSPAN):
            sl = slice(sp * SPAN, (sp + 1) * SPAN)
            scalar_copy(psR2[0:1, 0:SPAN], lnrow2[0:1, sl], reads=[lnrow2],
                        writes=[psR2])
            vop("tensor_sub", lnrow[0:1, sl], reads=[lnrow, psR2], writes=[lnrow],
                in0=lnrow[0:1, sl], in1=psR2[0:1, 0:SPAN])
        vop("tensor_scalar_add", lnrow[0:1, :], reads=[lnrow], writes=[lnrow],
            in0=lnrow[0:1, :], scalar1=1e-5)
        def rsq(eng):
            return eng.activation(out=lnrow[0:1, :], in_=lnrow[0:1, :],
                                  func=ACT.Sqrt)
        b.op("s", rsq, reads=[lnrow], writes=[lnrow])
        vop("reciprocal", lnrow[0:1, :], reads=[lnrow], writes=[lnrow],
            in_=lnrow[0:1, :])
        for sp in range(NSPAN):
            sl = slice(sp * SPAN, (sp + 1) * SPAN)
            scalar_copy(psR2[0:1, 0:SPAN], lnrow[0:1, sl], reads=[lnrow],
                        writes=[psR2])
            vop("tensor_mul", lnrow2[0:1, sl], reads=[lnrow3, psR2], writes=[lnrow2],
                in0=lnrow3[0:1, sl], in1=psR2[0:1, 0:SPAN])

    def emit_ln_apply(src_chunks, out_dst, spans=range(NSPAN), col0=0,
                      gamma=None, beta=None):
        for sp in spans:
            sl = slice(sp * SPAN, (sp + 1) * SPAN)
            osl = slice(sp * SPAN - col0, (sp + 1) * SPAN - col0)
            bcast_row(psR[0:128, 0:SPAN], lnrow[0:1, sl], 128, reads=[lnrow])
            bcast_row(psR2[0:128, 0:SPAN], lnrow2[0:1, sl], 128, reads=[lnrow2],
                      base=32)
            for c in range(KC):
                od = out_dst[c]
                vop("tensor_mul", od[:, osl], reads=[src_chunks[c], psR], writes=[od],
                    in0=src_chunks[c][:, sl], in1=psR[0:128, 0:SPAN])
                vop("tensor_sub", od[:, osl], reads=[od, psR2], writes=[od],
                    in0=od[:, osl], in1=psR2[0:128, 0:SPAN])
                if gamma is not None:
                    gsb, gcol = gamma
                    bsb, bcol = beta
                    vop("tensor_scalar", od[:, osl], reads=[od, gsb, bsb], writes=[od],
                        in0=od[:, osl], scalar1=gsb[:, gcol(c):gcol(c) + 1],
                        scalar2=bsb[:, bcol(c):bcol(c) + 1],
                        op0=ALU.mult, op1=ALU.add)

    def emit_layernorm(src_chunks, out_dst, gamma=None, beta=None, tag="ln"):
        emit_ln_stats(src_chunks, tag=tag)
        emit_ln_apply(src_chunks, out_dst, gamma=gamma, beta=beta)

    def l0_attn_a():
        with b.scope() as sc:
            trimask = b.sbuf_in(sc, "l0a_trimask", [128, 4 * SPAN], BF16)
            b.dma("gd", trimask[:], H["c_trimask"], writes=[trimask])
            emit_layernorm(xs, xln, tag="l0a_ln")
            dbg("xln", [xln[c][:] for c in range(KC)])
            qh = b.sbuf_in(sc, "l0a_q", [128, T], BF16)
            kh = b.sbuf_in(sc, "l0a_k", [128, T], BF16)
            va = [b.sbuf_in(sc, f"l0a_va{t}", [128, 130], BF16) for t in range(16)]
            pt = b.sbuf_in(sc, "l0a_pt", [128, 16 * SPAN], BF16)
            denP = b.sbuf_in(sc, "l0a_denP", [128, T // 16, 2], F32) if False else b.sbuf_in(sc, "l0a_denP", [64, T], F32)
            den0 = denP[0:1, :]
            den1 = denP[32:33, :]
            ao = b.sbuf_in(sc, "l0a_ao", [128, T], BF16)
            wq = H["l0a_wqkv"]
            for hg in range(8):
                def evac_q(ft, sp, ps, dst=qh):
                    act(dst[:, sp * SPAN:(sp + 1) * SPAN], ps[0:128, 0:SPAN], ACT.Copy,
                        reads=[ps], writes=[dst], scale=0.125)
                def evac_k(ft, sp, ps, dst=kh):
                    act(dst[:, sp * SPAN:(sp + 1) * SPAN], ps[0:128, 0:SPAN], ACT.Copy,
                        reads=[ps], writes=[dst])
                proj_fm([qh], xln, wq[:, hg * 128:(hg + 1) * 128], 128, evac_q)
                proj_fm([kh], xln, wq[:, D + hg * 128:D + (hg + 1) * 128], 128, evac_k)
                if hg == 0:
                    dbg("qh", [qh[:]])
                    dbg("kh", [kh[:]])
                sg = hg % 2
                wvv = wq.rearrange("(kc p) f -> kc p f", p=128)
                b.dma_group("y", [(wstage[sg][:, kc * 128:kc * 128 + 128],
                                   wvv[kc][:, 2 * D + hg * 128:2 * D + hg * 128 + 128],
                                   [], [wstage[sg]]) for kc in range(KC)])
                cast_to("v", wtile[sg][:, 0:KC * 128], wstage[sg][:, 0:KC * 128],
                        reads=[wstage[sg]], writes=[wtile[sg]])
                for tt in range(16):
                    ps = psA if tt % 2 == 0 else psB
                    items = [(xln[kc][:, tt * 128:tt * 128 + 128],
                              wtile[sg][:, kc * 128:kc * 128 + 128]) for kc in range(KC)]
                    mm_group(ps[0:128, 0:128], items,
                             reads=[wtile[sg]] + list(xln), writes=[ps])
                    scalar_copy(va[tt][:, 0:64], ps[0:128, 0:64], reads=[ps], writes=[va[tt]])
                    scalar_copy(va[tt][:, 65:129], ps[0:128, 64:128], reads=[ps], writes=[va[tt]])
                    vop("memset", va[tt][:, 64:65], reads=[], writes=[va[tt]], constant=1.0)
                    vop("memset", va[tt][:, 129:130], reads=[], writes=[va[tt]], constant=1.0)
                if hg == 0:
                    dbg("va", [va[0][:], va[1][:]])
                for hh in range(2):
                    h = hg * 2 + hh
                    qsl = qh[hh * 64:(hh + 1) * 64, :]
                    ksl = kh[hh * 64:(hh + 1) * 64, :]
                    vcol = hh * 65
                    for bb in range(NSPAN):
                        qs = slice(bb * SPAN, (bb + 1) * SPAN)
                        nchunks = 4 * bb + 4
                        for cc0 in range(0, nchunks, 2):
                            ps = psS01 if (cc0 // 2) % 2 == 0 else psS23
                            def qkf(eng, cc0=cc0, qs=qs, ps=ps, ksl=ksl, qsl=qsl):
                                i1 = eng.matmul(out=ps[0:128, 0:SPAN],
                                                lhsT=ksl[:, cc0 * 128:cc0 * 128 + 128],
                                                rhs=qsl[:, qs], start=True, stop=True)
                                i1 = eng.matmul(out=ps[0:128, SPAN:2 * SPAN],
                                                lhsT=ksl[:, (cc0 + 1) * 128:(cc0 + 2) * 128],
                                                rhs=qsl[:, qs], start=True, stop=True)
                                return i1
                            b.op("t", qkf, reads=[kh, qh], writes=[ps])
                            for j in range(2):
                                u = cc0 + j - 4 * bb
                                if 0 <= u <= 3:
                                    vop("tensor_add", ps[0:128, j * SPAN:(j + 1) * SPAN],
                                        reads=[ps, trimask], writes=[ps],
                                        in0=ps[0:128, j * SPAN:(j + 1) * SPAN],
                                        in1=trimask[:, u * SPAN:(u + 1) * SPAN])
                            act(pt[:, cc0 * SPAN:(cc0 + 2) * SPAN], ps[0:128, 0:2 * SPAN],
                                ACT.Exp, reads=[ps], writes=[pt])
                        items = [(va[cc][:, vcol:vcol + 65],
                                  pt[:, cc * SPAN:(cc + 1) * SPAN]) for cc in range(nchunks)]
                        mm_group(psP0[0:65, 0:SPAN], items,
                                 reads=[pt] + [va[cc] for cc in range(nchunks)],
                                 writes=[psP0])
                        if hg == 0 and hh == 0 and bb == 0:
                            dbg("pt", [pt[:, 0:4 * SPAN]])
                        scalar_copy(ao[hh * 64:hh * 64 + 64, qs], psP0[0:64, 0:SPAN],
                                    reads=[psP0], writes=[ao])
                        dh_ = den0 if hh == 0 else den1
                        scalar_copy(dh_[0:1, qs], psP0[64:65, 0:SPAN],
                                    reads=[psP0], writes=[dh_])
                if hg == 0:
                    dbg("ao", [ao[:]])
                    dbg("den", [den0[:], den1[:]])
                vop("reciprocal", den0[0:1, :], reads=[den0], writes=[den0], in_=den0[0:1, :])
                vop("reciprocal", den1[0:1, :], reads=[den1], writes=[den1], in_=den1[0:1, :])
                for sp in range(NSPAN):
                    sl = slice(sp * SPAN, (sp + 1) * SPAN)
                    def fn0(eng, sl=sl):
                        return eng.matmul(out=psR[0:64, 0:SPAN], lhsT=ones_row[:, 0:64],
                                          rhs=den0[0:1, sl], start=True, stop=True)
                    b.op("t", fn0, reads=[ones_row, den0], writes=[psR])
                    def fn1(eng, sl=sl):
                        return eng.matmul(out=psR[64:128, 0:SPAN],
                                          lhsT=allones[32:33, 0:64],
                                          rhs=den1[0:1, sl], start=True, stop=True)
                    b.op("t", fn1, reads=[allones, den1], writes=[psR])
                    vop("tensor_mul", ao[:, sl], reads=[ao, psR], writes=[ao],
                        in0=ao[:, sl], in1=psR[0:128, 0:SPAN])
                    vop("tensor_add", xs[hg][:, sl], reads=[xs[hg], ao], writes=[xs[hg]],
                        in0=xs[hg][:, sl], in1=ao[:, sl])
                if hg == 0:
                    dbg("aon", [ao[:]])
                    dbg("rden", [den0[:]])

    def l0_ffn(tag):
        TH = T // 2
        with b.scope() as sc:
            emit_ln_stats(xs, tag=tag)
            of = [b.sbuf_in(sc, f"{tag}_of{c}", [128, TH], F32) for c in range(KC)]
            ob = [xln[c] for c in range(KC)]
            h1 = [xln[c] for c in range(KC)]
            h2 = [b.sbuf_in(sc, f"{tag}_h2{c}", [128, TH], BF16) for c in range(4)]
            h3 = [b.sbuf_in(sc, f"{tag}_h3{c}", [128, TH], BF16) for c in range(2)]
            gsb = b.sbuf_in(sc, f"{tag}_gsb", [128, 8], F32)
            g1 = b.sbuf_in(sc, f"{tag}_g1", [128, 4], F32)
            g2 = b.sbuf_in(sc, f"{tag}_g2", [128, 2], F32)
            gfv = b.sbuf_in(sc, f"{tag}_gfv", [1, 2], F32)
            arow = b.sbuf_in(sc, f"{tag}_arow", [1, 4], F32)
            smsb = b.sbuf_in(sc, f"{tag}_smsb", [128, 1], F32)
            tw = [b.sbuf_in(sc, f"{tag}_tw{i}", [128, SPAN], F32) for i in range(2)]
            gsc = b.sbuf_in(sc, f"{tag}_gsc", [128, T], F32)

            for c in range(KC):
                for sp in range(NSPAN):
                    sl = slice(sp * SPAN, (sp + 1) * SPAN)
                    bcast_row(psR[0:128, 0:SPAN], lnrow[0:1, sl], 128, reads=[lnrow])
                    vop("tensor_mul", gsc[:, sl], reads=[xs[c], psR], writes=[gsc],
                        in0=xs[c][:, sl], in1=psR[0:128, 0:SPAN])
                vop("reduce_sum", gsb[:, c:c + 1], reads=[gsc], writes=[gsb],
                    in_=gsc[:], axis=AX)
            vop("reduce_sum", smsb[0:1, 0:1], reads=[lnrow2], writes=[smsb],
                in_=lnrow2[0:1, :], axis=AX)
            def pbc(eng):
                return eng.matmul(out=psB[0:128, 0:1], lhsT=ones_row[:],
                                  rhs=smsb[0:1, 0:1], start=True, stop=True)
            b.op("t", pbc, reads=[smsb, ones_row], writes=[psB])
            vop("scalar_tensor_tensor", gsb[:], reads=[gsb, psB], writes=[gsb],
                in0=gsb[:], scalar=1.0, in1=psB[0:128, 0:1].broadcast_to([128, 8]),
                op0=ALU.mult, op1=ALU.subtract)
            vop("tensor_scalar", gsb[:], reads=[gsb], writes=[gsb],
                in0=gsb[:], scalar1=1.0 / T, scalar2=None, op0=ALU.mult)

            for nm, col in [("gc_b1", 0), ("gc_b2", 8), ("gc_b3", 12)]:
                w = 4 if nm == "gc_b1" else (2 if nm == "gc_b2" else 1)
                b.dma("y", bias_sb[:, col:col + w], H[nm][:, 0:w], writes=[bias_sb])
            wv1 = H["gc_w1"].rearrange("(kc p) f -> kc p f", p=128)
            for ft in range(4):
                sg = ft % 2
                b.dma_group("y", [(wstage[sg][:, kc * 128:kc * 128 + 128],
                                   wv1[kc][:, ft * 128:(ft + 1) * 128], [], [wstage[sg]])
                                  for kc in range(KC)])
                cast_to("v", wtile[sg][:, 0:KC * 128], wstage[sg][:, 0:KC * 128],
                        reads=[wstage[sg]], writes=[wtile[sg]])
                gb = b.sbuf
                def gcast(eng, sg=sg):
                    return eng.tensor_copy(out=gsc[:, 0:8], in_=gsb[:, 0:8])
                vop("tensor_copy", sqbuf[:, 0:8], reads=[gsb], writes=[sqbuf],
                    in_=gsb[:, 0:8])
                items = [(wtile[sg][:, kc * 128:kc * 128 + 128], sqbuf[:, kc:kc + 1])
                         for kc in range(KC)]
                mm_group(psA[0:128, 0:1], items, reads=[wtile[sg], sqbuf], writes=[psA])
                act(g1[:, ft:ft + 1], psA[0:128, 0:1], ACT.Gelu, reads=[psA, bias_sb],
                    writes=[g1], bias=bias_sb[:, ft:ft + 1])
            vop("tensor_copy", sqbuf[:, 8:12], reads=[g1], writes=[sqbuf],
                in_=g1[:, 0:4])
            wv2 = H["gc_w2"].rearrange("(kc p) f -> kc p f", p=128)
            for ft in range(2):
                sg = ft % 2
                b.dma_group("y", [(wstage[sg][:, kc * 128:kc * 128 + 128],
                                   wv2[kc][:, ft * 128:(ft + 1) * 128], [], [wstage[sg]])
                                  for kc in range(4)])
                cast_to("v", wtile[sg][:, 0:4 * 128], wstage[sg][:, 0:4 * 128],
                        reads=[wstage[sg]], writes=[wtile[sg]])
                items = [(wtile[sg][:, kc * 128:kc * 128 + 128], sqbuf[:, 8 + kc:9 + kc])
                         for kc in range(4)]
                mm_group(psA[0:128, 0:1], items, reads=[wtile[sg], sqbuf], writes=[psA])
                act(g2[:, ft:ft + 1], psA[0:128, 0:1], ACT.Gelu, reads=[psA, bias_sb],
                    writes=[g2], bias=bias_sb[:, 8 + ft:9 + ft])
            vop("tensor_copy", sqbuf[:, 12:14], reads=[g2], writes=[sqbuf],
                in_=g2[:, 0:2])
            wv3 = H["gc_w3"].rearrange("(kc p) f -> kc p f", p=128)
            sg = 0
            b.dma_group("y", [(wstage[sg][:, kc * 128:kc * 128 + 1],
                               wv3[kc][:, 0:1], [], [wstage[sg]])
                              for kc in range(2)])
            cast_to("v", wtile[sg][:, 0:2 * 128], wstage[sg][:, 0:2 * 128],
                    reads=[wstage[sg]], writes=[wtile[sg]])
            items = [(wtile[sg][:, kc * 128:kc * 128 + 1], sqbuf[:, 12 + kc:13 + kc])
                     for kc in range(2)]
            mm_group(psA[0:1, 0:1], items, reads=[wtile[sg], sqbuf], writes=[psA])
            def sig(eng):
                return eng.activation(out=gfv[0:1, 0:1], in_=psA[0:1, 0:1],
                                      func=ACT.Sigmoid,
                                      bias=bias_sb[0:1, 12:13])
            b.op("s", sig, reads=[psA, bias_sb], writes=[gfv])

            b.dma("y", arow[0:1, 0:1], H["ar"], writes=[arow])
            vop("tensor_scalar_mul", arow[0:1, 1:2], reads=[arow], writes=[arow],
                in0=arow[0:1, 0:1], scalar1=0.6)
            vop("tensor_mul", arow[0:1, 2:3], reads=[arow, gfv], writes=[arow],
                in0=arow[0:1, 0:1], in1=gfv[0:1, 0:1])
            vop("tensor_scalar", arow[0:1, 2:3], reads=[arow], writes=[arow],
                in0=arow[0:1, 2:3], scalar1=1.4, scalar2=0.1, op0=ALU.mult,
                op1=ALU.add)
            vop("tensor_sub", arow[0:1, 2:3], reads=[arow], writes=[arow],
                in0=arow[0:1, 2:3], in1=arow[0:1, 0:1])

            for nm, col, w in [("cn_b1", 16, 8), ("cn_b2", 24, 4), ("cn_b3", 28, 2)]:
                b.dma("y", bias_sb[:, col:col + w], H[nm][:, 0:w], writes=[bias_sb])
            b.dma("y", arow[0:1, 3:4], H["cn_b4"][0:1, 0:1], writes=[arow])

            pairs = [(0, 4, 1.0, -1.0), (1, 5, 1.0, -1.0),
                     (2, 6, -1.0, 1.0), (3, 7, -1.0, 1.0)]
            for half in range(2):
                spans = [2 * half, 2 * half + 1]
                col0 = half * TH
                emit_ln_apply(xs, of, spans=spans, col0=col0)
                for cyc in range(3):
                    for c in range(KC):
                        cast_to("v" if c % 2 == 0 else "s", ob[c][:, 0:TH], of[c][:],
                                reads=[of[c]], writes=[ob[c]])
                    def ev1(ft, sp, ps, half=half):
                        act(h1[ft][:, TH + (sp - 2 * half) * SPAN:TH + (sp - 2 * half + 1) * SPAN],
                            ps[0:128, 0:SPAN], ACT.Gelu, reads=[ps, bias_sb],
                            writes=[h1[ft]], bias=bias_sb[:, 16 + ft:17 + ft])
                    proj_fm(None, None, H["cn_w1"], 1024, ev1, spans=spans,
                            src_get=lambda kc, sp, half=half: ob[kc][:, (sp - 2 * half) * SPAN:(sp - 2 * half + 1) * SPAN],
                            src_reads=ob)
                    def ev2(ft, sp, ps, half=half):
                        act(h2[ft][:, (sp - 2 * half) * SPAN:(sp - 2 * half + 1) * SPAN],
                            ps[0:128, 0:SPAN], ACT.Gelu, reads=[ps, bias_sb],
                            writes=[h2[ft]], bias=bias_sb[:, 24 + ft:25 + ft])
                    proj_fm(None, None, H["cn_w2"], 512, ev2, spans=spans,
                            src_get=lambda kc, sp, half=half: h1[kc][:, TH + (sp - 2 * half) * SPAN:TH + (sp - 2 * half + 1) * SPAN],
                            src_reads=h1)
                    def ev3(ft, sp, ps, half=half):
                        act(h3[ft][:, (sp - 2 * half) * SPAN:(sp - 2 * half + 1) * SPAN],
                            ps[0:128, 0:SPAN], ACT.Gelu, reads=[ps, bias_sb],
                            writes=[h3[ft]], bias=bias_sb[:, 28 + ft:29 + ft])
                    proj_fm(None, None, H["cn_w3"], 256, ev3, spans=spans,
                            src_rows=4,
                            src_get=lambda kc, sp, half=half: h2[kc][:, (sp - 2 * half) * SPAN:(sp - 2 * half + 1) * SPAN],
                            src_reads=h2)
                    wv4 = H["cn_w4"].rearrange("(kc p) f -> kc p f", p=128)
                    sg4 = 0
                    b.dma_group("y", [(wstage[sg4][:, 1024 - 2 + kc:1024 - 1 + kc],
                                       wv4[kc][:, 0:1], [], [wstage[sg4]])
                                      for kc in range(2)])
                    cast_to("v", wtile[sg4][:, 1022:1024], wstage[sg4][:, 1022:1024],
                            reads=[wstage[sg4]], writes=[wtile[sg4]])
                    for sp in spans:
                        lsl = slice((sp - 2 * half) * SPAN, (sp - 2 * half + 1) * SPAN)
                        items = [(wtile[sg4][:, 1022 + kc:1023 + kc], h3[kc][:, lsl])
                                 for kc in range(2)]
                        mm_group(psP0[0:1, 0:SPAN], items,
                                 reads=[wtile[sg4]] + h3, writes=[psP0])
                        def sig4(eng, sp=sp):
                            return eng.activation(
                                out=lnrow3[0:1, sp * SPAN:(sp + 1) * SPAN],
                                in_=psP0[0:1, 0:SPAN], func=ACT.Sigmoid,
                                bias=arow[0:1, 3:4])
                        b.op("s", sig4, reads=[psP0, arow], writes=[lnrow3])
                        vop("tensor_scalar", lnrow3[0:1, sp * SPAN:(sp + 1) * SPAN],
                            reads=[lnrow3, arow], writes=[lnrow3],
                            in0=lnrow3[0:1, sp * SPAN:(sp + 1) * SPAN],
                            scalar1=arow[0:1, 1:2], scalar2=arow[0:1, 2:3],
                            op0=ALU.mult, op1=ALU.add)
                        bcast_row(psR2[0:128, 0:SPAN],
                                  lnrow3[0:1, sp * SPAN:(sp + 1) * SPAN], 128,
                                  reads=[lnrow3], base=64)
                        for (p, qq, s1, s2) in pairs:
                            vop("scalar_tensor_tensor", tw[0][:], reads=[of[qq], psR2],
                                writes=[tw[0]], in0=of[qq][:, lsl], scalar=s1,
                                in1=psR2[0:128, 0:SPAN], op0=ALU.mult, op1=ALU.mult)
                            vop("scalar_tensor_tensor", tw[1][:], reads=[of[p], psR2],
                                writes=[tw[1]], in0=of[p][:, lsl], scalar=s2,
                                in1=psR2[0:128, 0:SPAN], op0=ALU.mult, op1=ALU.mult)
                            vop("tensor_add", of[p][:, lsl], reads=[of[p], tw[0]],
                                writes=[of[p]], in0=of[p][:, lsl], in1=tw[0][:])
                            vop("tensor_add", of[qq][:, lsl], reads=[of[qq], tw[1]],
                                writes=[of[qq]], in0=of[qq][:, lsl], in1=tw[1][:])
                for c in range(KC):
                    gsl = slice(col0, col0 + TH)
                    vop("tensor_add", xs[c][:, gsl], reads=[xs[c], of[c]],
                        writes=[xs[c]], in0=xs[c][:, gsl], in1=of[c][:])

    def l0_attn_b(tag="b0"):
        csap = csign[:].unsqueeze(1).broadcast_to([128, SPAN // 4, 4])
        with b.scope() as sc:
            emit_layernorm(xs, xln)
            cc = b.sbuf_in(sc, f"{tag}_cc", [128, T], F32)
            scs = b.sbuf_in(sc, f"{tag}_scs", [128, T], F32)
            b.dma("gd", cc[:], H["c_ropecc"], writes=[cc])
            b.dma("gd", scs[:], H["c_ropesc"], writes=[scs])
            aof = [b.sbuf_in(sc, f"{tag}_ao{c}", [128, T], BF16) for c in range(KC)]
            t1 = b.sbuf_in(sc, f"{tag}_t1", [128, T], BF16)
            bsm = b.sbuf_in(sc, f"{tag}_bsm", [128, 2 * D], BF16)
            rows4 = b.sbuf_in(sc, f"{tag}_rows4", [128, T], F32)
            expA = rows4[0:1, :]
            expB = rows4[32:33, :]
            denA = rows4[64:65, :]
            denB = rows4[0:1, :]

            for nm, r0, c0 in [("bq", 0, 0), ("bo", 0, D), ("bv", 64, 0)]:
                b.dma("y", wstage[0][0:64, 0:D], H[f"l0b_{nm}"], writes=[wstage[0]])
                cast_to("v", bsm[r0:r0 + 64, c0:c0 + D], wstage[0][0:64, 0:D],
                        reads=[wstage[0]], writes=[bsm])

            for i, nm in enumerate(["aq", "av"]):
                wvv = H[f"l0b_{nm}"].rearrange("(kc p) f -> kc p f", p=128)
                b.dma_group("y", [(wstage[1][:, kc * 64:kc * 64 + 64], wvv[kc],
                                   [], [wstage[1]]) for kc in range(KC)])
                cast_to("v", wtile[1][:, 0:KC * 64], wstage[1][:, 0:KC * 64],
                        reads=[wstage[1]], writes=[wtile[1]])
                for sp in range(NSPAN):
                    sl = slice(sp * SPAN, (sp + 1) * SPAN)
                    items = [(wtile[1][:, kc * 64:kc * 64 + 64], xln[kc][:, sl])
                             for kc in range(KC)]
                    mm_group(psA[0:64, 0:SPAN], items, reads=[wtile[1]] + xln,
                             writes=[psA])
                    scalar_copy(t1[i * 64:(i + 1) * 64, sl], psA[0:64, 0:SPAN],
                                reads=[psA], writes=[t1])

            for hg in range(KC):
                hsl = slice(hg * 128, (hg + 1) * 128)
                with b.scope() as hs:
                    fullK = b.sbuf_in(hs, f"{tag}_fk{hg}", [128, SPAN], BF16)
                    fullV = b.sbuf_in(hs, f"{tag}_fv{hg}", [128, SPAN], BF16)
                    cpv = b.sbuf_in(hs, f"{tag}_cpv{hg}", [128, T], BF16)
                    fvt = b.sbuf_in(hs, f"{tag}_fvt{hg}", [128, 2 * 260], BF16)
                    pts = b.sbuf_in(hs, f"{tag}_pts{hg}", [128, 4 * SPAN], BF16)
                    cptmp = b.sbuf_in(hs, f"{tag}_cptmp{hg}", [64, SPAN], F32)
                    sh = sqbuf

                    def do_proj(wnm, dst, lora_i, rope):
                        def evq(ft, sp, ps, dst=dst):
                            act(dst[:, sp * SPAN:(sp + 1) * SPAN], ps[0:128, 0:SPAN],
                                ACT.Copy, reads=[ps], writes=[dst])
                        ex = None
                        if lora_i is not None:
                            def ex(ft, sp, li=lora_i):
                                r0 = 0 if li == 0 else 64
                                return [(bsm[r0:r0 + 64, hg * 128:(hg + 1) * 128],
                                         t1[r0:r0 + 64, sp * SPAN:(sp + 1) * SPAN],
                                         [bsm, t1])]
                        proj_fm(None, xln, H[f"l0b_{wnm}"][:, hsl], 128, evq,
                                extra_mm=ex)
                        if rope:
                            b.dma_group("y", [
                                (sh[0:32, :], dst[32:64, :], [dst], [sh]),
                                (sh[32:64, :], dst[0:32, :], [dst], [sh]),
                                (sh[64:96, :], dst[96:128, :], [dst], [sh]),
                                (sh[96:128, :], dst[64:96, :], [dst], [sh])])
                            vop("tensor_mul", dst[:], reads=[dst, cc], writes=[dst],
                                in0=dst[:], in1=cc[:])
                            vop("scalar_tensor_tensor", sh[:], reads=[sh, scs],
                                writes=[sh], in0=sh[:], scalar=1.0, in1=scs[:],
                                op0=ALU.mult, op1=ALU.mult)
                            vop("tensor_add", dst[:], reads=[dst, sh], writes=[dst],
                                in0=dst[:], in1=sh[:])
                        else:
                            def csm(eng, d=dst):
                                return eng.tensor_mul(
                                    out=d[:].rearrange("p (s f) -> p s f", f=4),
                                    in0=d[:].rearrange("p (s f) -> p s f", f=4),
                                    in1=csign[:].unsqueeze(1).broadcast_to([128, T // 4, 4]))
                            b.op("v", csm, reads=[dst, csign], writes=[dst])

                    def slot_sum(srcb, dstb):
                        v3 = srcb[:].rearrange("p (s f) -> p s f", f=4)
                        vop("tensor_add", dstb[:].rearrange("p (s o) -> p s o", o=1),
                            reads=[srcb], writes=[dstb], in0=v3[:, :, 0:1], in1=v3[:, :, 1:2])
                        vop("tensor_add", dstb[:].rearrange("p (s o) -> p s o", o=1),
                            reads=[srcb, dstb], writes=[dstb],
                            in0=dstb[:].rearrange("p (s o) -> p s o", o=1), in1=v3[:, :, 2:3])
                        vop("tensor_add", dstb[:].rearrange("p (s o) -> p s o", o=1),
                            reads=[srcb, dstb], writes=[dstb],
                            in0=dstb[:].rearrange("p (s o) -> p s o", o=1), in1=v3[:, :, 3:4])

                    def slot_cumsum(srcb, cpb):
                        vop("tensor_copy", cpb[:], reads=[srcb], writes=[cpb], in_=srcb[:])
                        c3 = cpb[:].rearrange("p (s f) -> p s f", f=4)
                        s3 = srcb[:].rearrange("p (s f) -> p s f", f=4)
                        for j in range(1, 4):
                            vop("tensor_add", c3[:, :, j:j + 1], reads=[srcb, cpb],
                                writes=[cpb], in0=c3[:, :, j - 1:j], in1=s3[:, :, j:j + 1])

                    with b.scope() as vsc:
                        vs = b.sbuf_in(vsc, f"{tag}_vs{hg}", [128, T], BF16)
                        do_proj("wv", vs, 1, False)
                        slot_sum(vs, fullV)
                        slot_cumsum(vs, cpv)
                        for hh in range(2):
                            hr = slice(hh * 64, hh * 64 + 64)
                            for cs_ in range(4):
                                transpose(psT[0:128, 0:64], fullV[hr, cs_ * 128:(cs_ + 1) * 128],
                                          ident_bf[hr, hh * 64:hh * 64 + 64],
                                          reads=[fullV, ident_bf], writes=[psT])
                                col = hh * 260 + cs_ * 65
                                scalar_copy(fvt[:, col:col + 64], psT[0:128, 0:64],
                                            reads=[psT], writes=[fvt])
                                vop("memset", fvt[:, col + 64:col + 65], reads=[],
                                    writes=[fvt], constant=1.0)
                    aq = b.sbuf_in(hs, f"{tag}_aq{hg}", [128, T], BF16)
                    ks = aof[hg]
                    m2 = pts
                    do_proj("wq", aq, 0, True)
                    do_proj("wk", ks, None, True)
                    slot_sum(ks, fullK)
                    vop("tensor_copy", m2[:], reads=[ks], writes=[m2], in_=ks[:])
                    c3 = m2[:].rearrange("p (s f) -> p s f", f=4)
                    s3 = ks[:].rearrange("p (s f) -> p s f", f=4)
                    for j in range(1, 4):
                        vop("tensor_add", c3[:, :, j:j + 1], reads=[ks, m2],
                            writes=[m2], in0=c3[:, :, j - 1:j], in1=s3[:, :, j:j + 1])
                    vop("tensor_mul", sh[:], reads=[aq, m2], writes=[sh],
                        in0=aq[:], in1=m2[:])
                    for hh, exr in [(0, expA), (1, expB)]:
                        hr = slice(hh * 64, hh * 64 + 64)
                        for sp in range(NSPAN):
                            sl = slice(sp * SPAN, (sp + 1) * SPAN)
                            mm_group(psR2[0:1, 0:SPAN],
                                     [(ones_col[hr, :], sh[hr, sl])],
                                     reads=[ones_col, sh], writes=[psR2])
                            act(exr[0:1, sl], psR2[0:1, 0:SPAN], ACT.Exp,
                                reads=[psR2], writes=[exr], scale=1.0 / 16.0)
                    for hh, exr, dnr in [(0, expA, denA), (1, expB, denB)]:
                        hr = slice(hh * 64, hh * 64 + 64)
                        for bb in range(NSPAN):
                            bsl = slice(bb * SPAN, (bb + 1) * SPAN)
                            for cc0 in range(0, bb + 1, 2):
                                nj = min(2, bb + 1 - cc0)
                                ps = psS01 if (cc0 // 2) % 2 == 0 else psS23
                                def qkf(eng, cc0=cc0, bsl=bsl, ps=ps, hr=hr, nj=nj,
                                        fullK=fullK, aq=aq):
                                    i1 = None
                                    for j in range(nj):
                                        i1 = eng.matmul(
                                            out=ps[0:128, j * SPAN:(j + 1) * SPAN],
                                            lhsT=fullK[hr, (cc0 + j) * 128:(cc0 + j + 1) * 128],
                                            rhs=aq[hr, bsl], start=True, stop=True)
                                    return i1
                                b.op("t", qkf, reads=[fullK, aq], writes=[ps])
                                for j in range(nj):
                                    if cc0 + j == bb:
                                        vop("tensor_add", ps[0:128, j * SPAN:(j + 1) * SPAN],
                                            reads=[ps, slotmask], writes=[ps],
                                            in0=ps[0:128, j * SPAN:(j + 1) * SPAN],
                                            in1=slotmask[:])
                                act(pts[:, cc0 * SPAN:(cc0 + nj) * SPAN],
                                    ps[0:128, 0:nj * SPAN],
                                    ACT.Exp, reads=[ps], writes=[pts], scale=1.0 / 16.0)
                            items = [(fvt[:, hh * 260 + ccn * 65:hh * 260 + ccn * 65 + 65],
                                      pts[:, ccn * SPAN:(ccn + 1) * SPAN])
                                     for ccn in range(bb + 1)]
                            mm_group(psP0[0:65, 0:SPAN], items, reads=[fvt, pts],
                                     writes=[psP0])
                            bcast_row(psR[0:64, 0:SPAN], exr[0:1, bsl], 64,
                                      reads=[exr], base=(0 if hh == 0 else 32))
                            vop("tensor_mul", cptmp[:], reads=[cpv, psR], writes=[cptmp],
                                in0=cpv[hr, bsl], in1=psR[0:64, 0:SPAN])
                            vop("tensor_add", psP0[0:64, 0:SPAN], reads=[psP0, cptmp],
                                writes=[psP0], in0=psP0[0:64, 0:SPAN], in1=cptmp[:])
                            vop("tensor_add", psP0[64:65, 0:SPAN], reads=[psP0, exr],
                                writes=[psP0], in0=psP0[64:65, 0:SPAN], in1=exr[0:1, bsl])
                            scalar_copy(aof[hg][hr, bsl], psP0[0:64, 0:SPAN],
                                        reads=[psP0], writes=[aof[hg]])
                            scalar_copy(dnr[0:1, bsl], psP0[64:65, 0:SPAN],
                                        reads=[psP0], writes=[dnr])
                    vop("reciprocal", rows4[0:65, :], reads=[rows4], writes=[rows4],
                        in_=rows4[0:65, :])
                    for sp in range(NSPAN):
                        sl = slice(sp * SPAN, (sp + 1) * SPAN)
                        def f0(eng, sl=sl):
                            return eng.matmul(out=psR[0:64, 0:SPAN],
                                              lhsT=allones[64:65, 0:64],
                                              rhs=denA[0:1, sl], start=True, stop=True)
                        b.op("t", f0, reads=[allones, denA], writes=[psR])
                        def f1(eng, sl=sl):
                            return eng.matmul(out=psR[64:128, 0:SPAN],
                                              lhsT=allones[0:1, 0:64],
                                              rhs=denB[0:1, sl], start=True, stop=True)
                        b.op("t", f1, reads=[allones, denB], writes=[psR])
                        vop("tensor_mul", aof[hg][:, sl], reads=[aof[hg], psR],
                            writes=[aof[hg]], in0=aof[hg][:, sl], in1=psR[0:128, 0:SPAN])
                        def csm2(eng, sl=sl, hg=hg):
                            return eng.tensor_mul(
                                out=aof[hg][:, sl].rearrange("p (s f) -> p s f", f=4),
                                in0=aof[hg][:, sl].rearrange("p (s f) -> p s f", f=4),
                                in1=csap)
                        b.op("v", csm2, reads=[aof[hg], csign], writes=[aof[hg]])
            wvv = H["l0b_ao"].rearrange("(kc p) f -> kc p f", p=128)
            b.dma_group("y", [(wstage[1][:, kc * 64:kc * 64 + 64], wvv[kc],
                               [], [wstage[1]]) for kc in range(KC)])
            cast_to("v", wtile[1][:, 0:KC * 64], wstage[1][:, 0:KC * 64],
                    reads=[wstage[1]], writes=[wtile[1]])
            for sp in range(NSPAN):
                sl = slice(sp * SPAN, (sp + 1) * SPAN)
                items = [(wtile[1][:, kc * 64:kc * 64 + 64], aof[kc][:, sl])
                         for kc in range(KC)]
                mm_group(psA[0:64, 0:SPAN], items, reads=[wtile[1]] + aof, writes=[psA])
                scalar_copy(t1[0:64, sl], psA[0:64, 0:SPAN], reads=[psA], writes=[t1])
            def evr(ft, sp, ps):
                sl = slice(sp * SPAN, (sp + 1) * SPAN)
                vop("tensor_add", xs[ft][:, sl], reads=[xs[ft], ps], writes=[xs[ft]],
                    in0=xs[ft][:, sl], in1=ps[0:128, 0:SPAN])
            def exo(ft, sp):
                return [(bsm[0:64, D + ft * 128:D + (ft + 1) * 128],
                         t1[0:64, sp * SPAN:(sp + 1) * SPAN], [bsm, t1])]
            proj_fm(None, aof, H["l0b_wo"], D, evr, extra_mm=exo)

    def l1_ffn(tag):
        with b.scope() as sc:
            emit_layernorm(xs, xln)
            musb = b.sbuf_in(sc, f"{tag}_mu", [128, 32], F32)
            b.dma("y", musb[:, 0:8], H["l1f_mur"][:, 0:8], writes=[musb])
            b.dma("y", musb[:, 8:16], H["l1f_muk"][:, 0:8], writes=[musb])
            vop("tensor_scalar", musb[:, 16:32], reads=[musb], writes=[musb],
                in0=musb[:, 0:16], scalar1=-1.0, scalar2=1.0, op0=ALU.mult, op1=ALU.add)
            r = [b.sbuf_in(sc, f"{tag}_r{c}", [128, T], BF16) for c in range(KC)]

            def mix(dst, mucol):
                for c in range(KC):
                    vop("tensor_scalar_mul", dst[c][:], reads=[xln[c], musb],
                        writes=[dst[c]], in0=xln[c][:], scalar1=musb[:, mucol + c:mucol + c + 1])
                    vop("scalar_tensor_tensor", dst[c][:, 1:T], reads=[xln[c], musb, dst[c]],
                        writes=[dst[c]], in0=xln[c][:, 0:T - 1],
                        scalar=musb[:, 16 + mucol + c:17 + mucol + c],
                        in1=dst[c][:, 1:T], op0=ALU.mult, op1=ALU.add)

            with b.scope() as sa:
                xr = [b.sbuf_in(sa, f"{tag}_xr{c}", [128, T], BF16) for c in range(KC)]
                mix(xr, 0)
                def evr(ft, sp, ps):
                    act(r[ft][:, sp * SPAN:(sp + 1) * SPAN], ps[0:128, 0:SPAN],
                        ACT.Sigmoid, reads=[ps], writes=[r[ft]])
                proj_fm(None, xr, H["l1f_wr"], D, evr)
            with b.scope() as sb_:
                xk = [b.sbuf_in(sb_, f"{tag}_xk{c}", [128, T], BF16) for c in range(KC)]
                mix(xk, 8)
                ka = xln
                def evk(ft, sp, ps):
                    act(ka[ft][:, sp * SPAN:(sp + 1) * SPAN], ps[0:128, 0:SPAN],
                        ACT.Relu, reads=[ps], writes=[ka[ft]])
                proj_fm(None, xk, H["l1f_wk"], D, evk)
            for c in range(KC):
                def sqk(eng, c=c):
                    return eng.activation(out=xln[c][:], in_=xln[c][:], func=ACT.Square)
                b.op("s", sqk, reads=[xln[c]], writes=[xln[c]])
            tws = b.sbuf_in(sc, f"{tag}_tws", [128, SPAN], F32)
            def evv(ft, sp, ps):
                sl = slice(sp * SPAN, (sp + 1) * SPAN)
                vop("tensor_mul", tws[:], reads=[r[ft], ps], writes=[tws],
                    in0=r[ft][:, sl], in1=ps[0:128, 0:SPAN])
                vop("tensor_add", xs[ft][:, sl], reads=[xs[ft], tws], writes=[xs[ft]],
                    in0=xs[ft][:, sl], in1=tws[:])
            proj_fm(None, xln, H["l1f_wv"], D, evv)

    def l1_attn_a(tag="p1"):
        M = 266
        DN = float(64.0 ** -0.25)
        RATIO = float(266.0 ** -0.5)
        EPS = 1e-4
        with b.scope() as sc:
            emit_layernorm(xs, xln)
            aof = [b.sbuf_in(sc, f"{tag}_ao{c}", [128, T], BF16) for c in range(KC)]
            pj2 = b.sbuf_in(sc, f"{tag}_pj2", [128, M], BF16)
            b.dma("y", wstage[0][0:64, 0:M], H["l1a_projT"], writes=[wstage[0]])
            cast_to("v", pj2[0:64, 0:M], wstage[0][0:64, 0:M], reads=[wstage[0]], writes=[pj2])
            cast_to("v", pj2[64:128, 0:M], wstage[0][0:64, 0:M], reads=[wstage[0]], writes=[pj2])
            for nm, col in [("bq", 0), ("bk", 8), ("bv", 16), ("bo", 24), ("bfc", 32)]:
                b.dma("y", bias_sb[:, col:col + 8], H[f"l1a_{nm}"][:, 0:8], writes=[bias_sb])
            vop("tensor_scalar_mul", bias_sb[:, 0:16], reads=[bias_sb], writes=[bias_sb],
                in0=bias_sb[:, 0:16], scalar1=DN)
            denP2 = b.sbuf_in(sc, f"{tag}_den", [64, T], F32)
            dnA = denP2[0:1, :]
            dnB = denP2[32:33, :]
            bqrow = b.sbuf_in(sc, f"{tag}_bqrow", [1, T], F32)
            for hg in range(KC):
                hsl = slice(hg * 128, (hg + 1) * 128)
                with b.scope() as hs:
                    qb = b.sbuf_in(hs, f"{tag}_qb{hg}", [128, T], BF16)
                    kb = b.sbuf_in(hs, f"{tag}_kb{hg}", [128, T], BF16)
                    va = [b.sbuf_in(hs, f"{tag}_va{hg}_{t}", [128, 130], BF16)
                          for t in range(16)]
                    kp = b.sbuf_in(hs, f"{tag}_kp{hg}", [128, 16 * M], BF16)
                    qpt0 = b.sbuf_in(hs, f"{tag}_qpt0{hg}", [128, T], BF16)
                    qpt1 = b.sbuf_in(hs, f"{tag}_qpt1{hg}", [128, T], BF16)
                    qpt2 = b.sbuf_in(hs, f"{tag}_qpt2{hg}", [10, T], BF16)
                    stat = b.sbuf_in(hs, f"{tag}_st{hg}", [128, 64], F32)
                    trow = b.sbuf_in(hs, f"{tag}_tr{hg}", [16, 128], F32)
                    ctx = b.sbuf_in(hs, f"{tag}_ctx{hg}", [65, M], BF16)
                    ctxa = [b.sbuf_in(hs, f"{tag}_cx{hg}_{i}", [128, 65], BF16)
                            for i in range(3)]
                    def evq(ft, sp, ps, dst=qb, bc=0):
                        act(dst[:, sp * SPAN:(sp + 1) * SPAN], ps[0:128, 0:SPAN],
                            ACT.Copy, reads=[ps, bias_sb], writes=[dst], scale=DN,
                            bias=bias_sb[:, bc + hg:bc + hg + 1])
                    def evk_(ft, sp, ps, dst=kb, bc=8):
                        act(dst[:, sp * SPAN:(sp + 1) * SPAN], ps[0:128, 0:SPAN],
                            ACT.Copy, reads=[ps, bias_sb], writes=[dst], scale=DN,
                            bias=bias_sb[:, bc + hg:bc + hg + 1])
                    proj_fm(None, xln, H["l1a_wq"][:, hsl], 128, evq)
                    proj_fm(None, xln, H["l1a_wk"][:, hsl], 128, evk_)
                    sg = hg % 2
                    wvv = H["l1a_wv"].rearrange("(kc p) f -> kc p f", p=128)
                    b.dma_group("y", [(wstage[sg][:, kc * 128:kc * 128 + 128],
                                       wvv[kc][:, hsl], [], [wstage[sg]])
                                      for kc in range(KC)])
                    cast_to("v", wtile[sg][:, 0:KC * 128], wstage[sg][:, 0:KC * 128],
                            reads=[wstage[sg]], writes=[wtile[sg]])
                    transpose(psA[0:1, 0:128], bias_sb[:, 16 + hg:17 + hg],
                              ident_f[:], reads=[bias_sb, ident_f], writes=[psA])
                    scalar_copy(bqrow[0:1, 0:128], psA[0:1, 0:128], reads=[psA],
                                writes=[bqrow])
                    for tt in range(16):
                        ps = psB if tt % 2 == 0 else psP0
                        items = [(xln[kc][:, tt * 128:tt * 128 + 128],
                                  wtile[sg][:, kc * 128:kc * 128 + 128]) for kc in range(KC)]
                        items.append((allones[0:1, 0:128], bqrow[0:1, 0:128]))
                        mm_group(ps[0:128, 0:128], items,
                                 reads=[wtile[sg], bqrow] + list(xln), writes=[ps])
                        scalar_copy(va[tt][:, 0:64], ps[0:128, 0:64], reads=[ps], writes=[va[tt]])
                        scalar_copy(va[tt][:, 65:129], ps[0:128, 64:128], reads=[ps], writes=[va[tt]])
                        vop("memset", va[tt][:, 64:65], reads=[], writes=[va[tt]], constant=1.0)
                        vop("memset", va[tt][:, 129:130], reads=[], writes=[va[tt]], constant=1.0)
                    for hh in range(2):
                        h0 = hh * 64
                        hr = slice(h0, h0 + 64)
                        def sqq(eng, s=qb):
                            return eng.activation(out=sqbuf[:], in_=s[:], func=ACT.Square)
                        b.op("s", sqq, reads=[qb], writes=[sqbuf])
                        for nt in range(16):
                            mm_group(psS0[0:128, 0:1],
                                     [(sqbuf[hr, nt * 128:(nt + 1) * 128], ones_col[hr, :])],
                                     reads=[sqbuf, ones_col], writes=[psS0])
                            scalar_copy(stat[:, nt:nt + 1], psS0[0:128, 0:1],
                                        reads=[psS0], writes=[stat])
                            mm_group(psS1[0:128, 0:M],
                                     [(qb[hr, nt * 128:(nt + 1) * 128], pj2[hr, :])],
                                     reads=[qb, pj2], writes=[psS1])
                            vop("reduce_max", stat[:, 16 + nt:17 + nt], reads=[psS1],
                                writes=[stat], in_=psS1[0:128, 0:M], axis=AX)
                        def sqk2(eng, s=kb):
                            return eng.activation(out=sqbuf[:], in_=s[:], func=ACT.Square)
                        b.op("s", sqk2, reads=[kb], writes=[sqbuf])
                        for nt in range(16):
                            mm_group(psS0[0:128, 0:1],
                                     [(sqbuf[hr, nt * 128:(nt + 1) * 128], ones_col[hr, :])],
                                     reads=[sqbuf, ones_col], writes=[psS0])
                            scalar_copy(stat[:, 32 + nt:33 + nt], psS0[0:128, 0:1],
                                        reads=[psS0], writes=[stat])
                            mm_group(psS1[0:128, 0:M],
                                     [(kb[hr, nt * 128:(nt + 1) * 128], pj2[hr, :])],
                                     reads=[kb, pj2], writes=[psS1])
                            vop("reduce_max", stat[:, 48 + nt:49 + nt], reads=[psS1],
                                writes=[stat], in_=psS1[0:128, 0:M], axis=AX)
                        vop("reduce_max", stat[:, 48:49], reads=[stat], writes=[stat],
                            in_=stat[:, 48:64], axis=AX)
                        transpose(psA[0:1, 0:128], stat[:, 48:49], ident_f[:],
                                  reads=[stat, ident_f], writes=[psA])
                        vop("reduce_max", psA[0:1, 0:1], reads=[psA], writes=[psA],
                            in_=psA[0:1, 0:128], axis=AX)
                        scalar_copy(bqrow[0:1, 256:257], psA[0:1, 0:1], reads=[psA],
                                    writes=[bqrow])
                        def mxb(eng):
                            return eng.matmul(out=psR2[0:128, 0:1], lhsT=ones_row[:],
                                              rhs=bqrow[0:1, 256:257], start=True, stop=True)
                        b.op("t", mxb, reads=[ones_row, bqrow], writes=[psR2])
                        vop("scalar_tensor_tensor", stat[:, 32:48], reads=[stat, psR2],
                            writes=[stat], in0=stat[:, 32:48], scalar=-0.5,
                            in1=psR2[0:128, 0:1].broadcast_to([128, 16]),
                            op0=ALU.mult, op1=ALU.subtract)
                        vop("tensor_scalar_mul", stat[:, 0:16], reads=[stat], writes=[stat],
                            in0=stat[:, 0:16], scalar1=0.5)
                        vop("tensor_add", stat[:, 0:16], reads=[stat], writes=[stat],
                            in0=stat[:, 0:16], in1=stat[:, 16:32])
                        vop("tensor_scalar_mul", stat[:, 0:16], reads=[stat], writes=[stat],
                            in0=stat[:, 0:16], scalar1=-1.0)
                        for nt in range(16):
                            ps = psS0 if nt % 2 == 0 else psS1
                            mm_group(ps[0:128, 0:M],
                                     [(kb[hr, nt * 128:(nt + 1) * 128], pj2[hr, :])],
                                     reads=[kb, pj2], writes=[ps])
                            act(kp[:, nt * M:(nt + 1) * M], ps[0:128, 0:M], ACT.Exp,
                                reads=[ps, stat], writes=[kp],
                                bias=stat[:, 32 + nt:33 + nt])
                        vop("tensor_scalar", kp[:], reads=[kp], writes=[kp],
                            in0=kp[:], scalar1=RATIO, scalar2=RATIO * EPS,
                            op0=ALU.mult, op1=ALU.add)
                        transpose(psA[0:16, 0:128], stat[:, 0:16], ident_f[:],
                                  reads=[stat, ident_f], writes=[psA])
                        scalar_copy(trow[0:16, :], psA[0:16, 0:128], reads=[psA],
                                    writes=[trow])
                        b.dma_group("y", [(bqrow[0:1, nt * 128:(nt + 1) * 128],
                                           trow[nt:nt + 1, 0:128], [trow], [bqrow])
                                          for nt in range(16)])
                        qpt = [qpt0, qpt1, qpt2]
                        for mc in range(3):
                            mw = 128 if mc < 2 else 10
                            for sp in range(NSPAN):
                                sl = slice(sp * SPAN, (sp + 1) * SPAN)
                                ps = psS0 if sp % 2 == 0 else psS1
                                items = [(pj2[hr, mc * 128:mc * 128 + mw], qb[hr, sl]),
                                         (allones[0:1, 0:mw], bqrow[0:1, sl])]
                                mm_group(ps[0:mw, 0:SPAN], items,
                                         reads=[pj2, qb, allones, bqrow], writes=[ps])
                                act(qpt[mc][0:mw, sl], ps[0:mw, 0:SPAN], ACT.Exp,
                                    reads=[ps], writes=[qpt[mc]])
                            vop("tensor_scalar", qpt[mc][0:mw, :], reads=[qpt[mc]],
                                writes=[qpt[mc]], in0=qpt[mc][0:mw, :], scalar1=RATIO,
                                scalar2=RATIO * EPS, op0=ALU.mult, op1=ALU.add)
                        items = [(va[nt][:, hh * 65:hh * 65 + 65], kp[:, nt * M:(nt + 1) * M])
                                 for nt in range(16)]
                        mm_group(psP0[0:65, 0:M], items,
                                 reads=[kp] + va, writes=[psP0])
                        scalar_copy(ctx[0:65, 0:M], psP0[0:65, 0:M], reads=[psP0], writes=[ctx])
                        for mc in range(3):
                            mw = 128 if mc < 2 else 10
                            transpose(psT[0:mw, 0:65], ctx[0:65, mc * 128:mc * 128 + mw],
                                      ident_bf[0:65, 0:65], reads=[ctx, ident_bf], writes=[psT])
                            scalar_copy(ctxa[mc][0:mw, :], psT[0:mw, 0:65],
                                        reads=[psT], writes=[ctxa[mc]])
                        dn_r = dnA if hh == 0 else dnB
                        for sp in range(NSPAN):
                            sl = slice(sp * SPAN, (sp + 1) * SPAN)
                            items = [(ctxa[mc][0:(128 if mc < 2 else 10), :],
                                      qpt[mc][0:(128 if mc < 2 else 10), sl])
                                     for mc in range(3)]
                            mm_group(psP0[0:65, 0:SPAN], items,
                                     reads=[qpt0, qpt1, qpt2] + ctxa, writes=[psP0])
                            scalar_copy(aof[hg][hr, sl], psP0[0:64, 0:SPAN],
                                        reads=[psP0], writes=[aof[hg]])
                            scalar_copy(dn_r[0:1, sl], psP0[64:65, 0:SPAN],
                                        reads=[psP0], writes=[denP2])
                    vop("reciprocal", denP2[0:33, :], reads=[denP2], writes=[denP2],
                        in_=denP2[0:33, :])
                    for sp in range(NSPAN):
                        sl = slice(sp * SPAN, (sp + 1) * SPAN)
                        def g0(eng, sl=sl):
                            return eng.matmul(out=psR[0:64, 0:SPAN], lhsT=allones[0:1, 0:64],
                                              rhs=dnA[0:1, sl], start=True, stop=True)
                        b.op("t", g0, reads=[allones, denP2], writes=[psR])
                        def g1(eng, sl=sl):
                            return eng.matmul(out=psR[64:128, 0:SPAN], lhsT=allones[32:33, 0:64],
                                              rhs=dnB[0:1, sl], start=True, stop=True)
                        b.op("t", g1, reads=[allones, denP2], writes=[psR])
                        vop("tensor_mul", aof[hg][:, sl], reads=[aof[hg], psR],
                            writes=[aof[hg]], in0=aof[hg][:, sl], in1=psR[0:128, 0:SPAN])
            def evo(ft, sp, ps):
                act(xln[ft][:, sp * SPAN:(sp + 1) * SPAN], ps[0:128, 0:SPAN],
                    ACT.Copy, reads=[ps, bias_sb], writes=[xln[ft]],
                    bias=bias_sb[:, 24 + ft:25 + ft])
            proj_fm(None, aof, H["l1a_wo"], D, evo)
            tws2 = b.sbuf_in(sc, f"{tag}_tws", [128, SPAN], BF16)
            def evf(ft, sp, ps):
                sl = slice(sp * SPAN, (sp + 1) * SPAN)
                act(tws2[:], ps[0:128, 0:SPAN], ACT.Relu, reads=[ps, bias_sb],
                    writes=[tws2], bias=bias_sb[:, 32 + ft:33 + ft])
                vop("tensor_add", xs[ft][:, sl], reads=[xs[ft], tws2], writes=[xs[ft]],
                    in0=xs[ft][:, sl], in1=tws2[:])
            proj_fm(None, xln, H["l1a_wfc"], D, evf)

    def l1_attn_b(tag="q1"):
        xsv = xspill.rearrange("(c p) t -> c p t", p=128)
        with b.scope() as sc:
            b.dma_group("gd", [(xsv[c], xs[c][:], [xs[c]], []) for c in range(KC)])
            emit_layernorm(xs, xs)
            trimask = b.sbuf_in(sc, f"{tag}_tm", [128, 4 * SPAN], BF16)
            b.dma("gd", trimask[:], H["c_trimask"], writes=[trimask])
            gam = b.sbuf_in(sc, f"{tag}_gam", [128, 48], F32)
            for i, g in enumerate(["g1", "b1", "g2", "b2", "g3", "b3"]):
                b.dma("y", gam[:, i * 8:(i + 1) * 8], H[f"l1b_{g}"][:, 0:8], writes=[gam])

            def gamma_of(i):
                return ((gam, lambda c, i=i: 2 * i * 8 + c),
                        (gam, lambda c, i=i: (2 * i + 1) * 8 + c))

            aof = [b.sbuf_in(sc, f"{tag}_ao{c}", [128, T], BF16) for c in range(KC)]
            tws3 = b.sbuf_in(sc, f"{tag}_tws", [128, SPAN], BF16)

            g1, b1_ = gamma_of(0)
            emit_layernorm(xs, xln, gamma=g1, beta=b1_)
            b.dma("y", bias_sb[:, 0:8], H["l1b_sfb"][:, 0:8], writes=[bias_sb])
            mstage = b.sbuf_in(sc, f"{tag}_mst", [128, 1024], F32)
            wvv = H["l1b_sfw"].rearrange("(kc p) f -> kc p f", p=128)
            mvv = H["l1b_mask"].rearrange("(kc p) f -> kc p f", p=128)
            for ft in range(KC):
                sg = ft % 2
                b.dma_group("y", [(wstage[sg][:, kc * 128:kc * 128 + 128],
                                   wvv[kc][:, ft * 128:(ft + 1) * 128], [], [wstage[sg]])
                                  for kc in range(KC)])
                b.dma_group("y", [(mstage[:, kc * 128:kc * 128 + 128],
                                   mvv[kc][:, ft * 128:(ft + 1) * 128], [], [mstage])
                                  for kc in range(KC)])
                vop("tensor_mul", wstage[sg][:], reads=[wstage[sg], mstage],
                    writes=[wstage[sg]], in0=wstage[sg][:], in1=mstage[:])
                cast_to("v", wtile[sg][:, 0:KC * 128], wstage[sg][:, 0:KC * 128],
                        reads=[wstage[sg]], writes=[wtile[sg]])
                for sp in range(NSPAN):
                    ps = psA if sp % 2 == 0 else psB
                    sl = slice(sp * SPAN, (sp + 1) * SPAN)
                    items = [(wtile[sg][:, kc * 128:kc * 128 + 128], xln[kc][:, sl])
                             for kc in range(KC)]
                    mm_group(ps[0:128, 0:SPAN], items, reads=[wtile[sg]] + xln, writes=[ps])
                    act(tws3[:], ps[0:128, 0:SPAN], ACT.Copy, reads=[ps, bias_sb],
                        writes=[tws3], bias=bias_sb[:, ft:ft + 1])
                    vop("tensor_add", xs[ft][:, sl], reads=[xs[ft], tws3], writes=[xs[ft]],
                        in0=xs[ft][:, sl], in1=tws3[:])

            g2, b2_ = gamma_of(1)
            emit_layernorm(xs, xln, gamma=g2, beta=b2_)
            for nm, col in [("bq", 8), ("bk", 16), ("bv", 24), ("bo", 32)]:
                b.dma("y", bias_sb[:, col:col + 8], H[f"l1b_{nm}"][:, 0:8], writes=[bias_sb])
            vop("tensor_scalar_mul", bias_sb[:, 8:16], reads=[bias_sb], writes=[bias_sb],
                in0=bias_sb[:, 8:16], scalar1=0.125)
            denP3 = b.sbuf_in(sc, f"{tag}_den", [64, T], F32)
            dn0 = denP3[0:1, :]
            dn1 = denP3[32:33, :]
            for hg in range(KC):
                hsl = slice(hg * 128, (hg + 1) * 128)
                with b.scope() as hs:
                    qh = b.sbuf_in(hs, f"{tag}_q{hg}", [128, T], BF16)
                    kh = b.sbuf_in(hs, f"{tag}_k{hg}", [128, T], BF16)
                    va = [b.sbuf_in(hs, f"{tag}_va{hg}_{t}", [128, 130], BF16)
                          for t in range(16)]
                    pt = b.sbuf_in(hs, f"{tag}_pt{hg}", [128, 16 * SPAN], BF16)
                    def evq(ft, sp, ps):
                        act(qh[:, sp * SPAN:(sp + 1) * SPAN], ps[0:128, 0:SPAN],
                            ACT.Copy, reads=[ps, bias_sb], writes=[qh], scale=0.125,
                            bias=bias_sb[:, 8 + hg:9 + hg])
                    def evk(ft, sp, ps):
                        act(kh[:, sp * SPAN:(sp + 1) * SPAN], ps[0:128, 0:SPAN],
                            ACT.Copy, reads=[ps, bias_sb], writes=[kh],
                            bias=bias_sb[:, 16 + hg:17 + hg])
                    proj_fm(None, xln, H["l1b_wq"][:, hsl], 128, evq)
                    proj_fm(None, xln, H["l1b_wk"][:, hsl], 128, evk)
                    sg = hg % 2
                    wvv2 = H["l1b_wv"].rearrange("(kc p) f -> kc p f", p=128)
                    b.dma_group("y", [(wstage[sg][:, kc * 128:kc * 128 + 128],
                                       wvv2[kc][:, hsl], [], [wstage[sg]])
                                      for kc in range(KC)])
                    cast_to("v", wtile[sg][:, 0:KC * 128], wstage[sg][:, 0:KC * 128],
                            reads=[wstage[sg]], writes=[wtile[sg]])
                    transpose(psA[0:1, 0:128], bias_sb[:, 24 + hg:25 + hg],
                              ident_f[:], reads=[bias_sb, ident_f], writes=[psA])
                    scalar_copy(dn0[0:1, 0:128], psA[0:1, 0:128], reads=[psA],
                                writes=[denP3])
                    for tt in range(16):
                        ps = psB if tt % 2 == 0 else psP0
                        items = [(xln[kc][:, tt * 128:tt * 128 + 128],
                                  wtile[sg][:, kc * 128:kc * 128 + 128]) for kc in range(KC)]
                        items.append((allones[0:1, 0:128], dn0[0:1, 0:128]))
                        mm_group(ps[0:128, 0:128], items,
                                 reads=[wtile[sg], denP3] + list(xln), writes=[ps])
                        scalar_copy(va[tt][:, 0:64], ps[0:128, 0:64], reads=[ps], writes=[va[tt]])
                        scalar_copy(va[tt][:, 65:129], ps[0:128, 64:128], reads=[ps], writes=[va[tt]])
                        vop("memset", va[tt][:, 64:65], reads=[], writes=[va[tt]], constant=1.0)
                        vop("memset", va[tt][:, 129:130], reads=[], writes=[va[tt]], constant=1.0)
                    for hh in range(2):
                        hr = slice(hh * 64, hh * 64 + 64)
                        dnr = dn0 if hh == 0 else dn1
                        for bb in range(NSPAN):
                            bsl = slice(bb * SPAN, (bb + 1) * SPAN)
                            nch = 4 * bb + 4
                            for cc0 in range(0, nch, 2):
                                ps = psS01 if (cc0 // 2) % 2 == 0 else psS23
                                def qkf(eng, cc0=cc0, bsl=bsl, ps=ps, hr=hr,
                                        kh=kh, qh=qh):
                                    i1 = eng.matmul(out=ps[0:128, 0:SPAN],
                                                    lhsT=kh[hr, cc0 * 128:(cc0 + 1) * 128],
                                                    rhs=qh[hr, bsl], start=True, stop=True)
                                    i1 = eng.matmul(out=ps[0:128, SPAN:2 * SPAN],
                                                    lhsT=kh[hr, (cc0 + 1) * 128:(cc0 + 2) * 128],
                                                    rhs=qh[hr, bsl], start=True, stop=True)
                                    return i1
                                b.op("t", qkf, reads=[kh, qh], writes=[ps])
                                for j in range(2):
                                    u = cc0 + j - 4 * bb
                                    if 0 <= u <= 3:
                                        vop("tensor_add", ps[0:128, j * SPAN:(j + 1) * SPAN],
                                            reads=[ps, trimask], writes=[ps],
                                            in0=ps[0:128, j * SPAN:(j + 1) * SPAN],
                                            in1=trimask[:, u * SPAN:(u + 1) * SPAN])
                                act(pt[:, cc0 * SPAN:(cc0 + 2) * SPAN], ps[0:128, 0:2 * SPAN],
                                    ACT.Relu, reads=[ps], writes=[pt])
                            items = [(va[ccn][:, hh * 65:hh * 65 + 65],
                                      pt[:, ccn * SPAN:(ccn + 1) * SPAN])
                                     for ccn in range(nch)]
                            mm_group(psP0[0:65, 0:SPAN], items,
                                     reads=[pt] + va, writes=[psP0])
                            scalar_copy(aof[hg][hr, bsl], psP0[0:64, 0:SPAN],
                                        reads=[psP0], writes=[aof[hg]])
                            scalar_copy(dnr[0:1, bsl], psP0[64:65, 0:SPAN],
                                        reads=[psP0], writes=[denP3])
                    vop("tensor_scalar_add", denP3[0:33, :], reads=[denP3], writes=[denP3],
                        in0=denP3[0:33, :], scalar1=1e-9)
                    vop("reciprocal", denP3[0:33, :], reads=[denP3], writes=[denP3],
                        in_=denP3[0:33, :])
                    for sp in range(NSPAN):
                        sl = slice(sp * SPAN, (sp + 1) * SPAN)
                        def g0(eng, sl=sl):
                            return eng.matmul(out=psR[0:64, 0:SPAN], lhsT=allones[0:1, 0:64],
                                              rhs=dn0[0:1, sl], start=True, stop=True)
                        b.op("t", g0, reads=[allones, denP3], writes=[psR])
                        def g1_(eng, sl=sl):
                            return eng.matmul(out=psR[64:128, 0:SPAN], lhsT=allones[32:33, 0:64],
                                              rhs=dn1[0:1, sl], start=True, stop=True)
                        b.op("t", g1_, reads=[allones, denP3], writes=[psR])
                        vop("tensor_mul", aof[hg][:, sl], reads=[aof[hg], psR],
                            writes=[aof[hg]], in0=aof[hg][:, sl], in1=psR[0:128, 0:SPAN])
            def evo2(ft, sp, ps):
                sl = slice(sp * SPAN, (sp + 1) * SPAN)
                act(tws3[:], ps[0:128, 0:SPAN], ACT.Copy, reads=[ps, bias_sb],
                    writes=[tws3], bias=bias_sb[:, 32 + ft:33 + ft])
                vop("tensor_add", xs[ft][:, sl], reads=[xs[ft], tws3], writes=[xs[ft]],
                    in0=xs[ft][:, sl], in1=tws3[:])
            proj_fm(None, aof, H["l1b_wo"], D, evo2)

            g3, b3_ = gamma_of(2)
            emit_layernorm(xs, xln, gamma=g3, beta=b3_)
            b.dma("y", bias_sb[:, 0:32], H["l1b_ff1b"][:, 0:32], writes=[bias_sb])
            b.dma("y", bias_sb[:, 32:40], H["l1b_ff2b"][:, 0:8], writes=[bias_sb])
            hsp = b.sbuf_in(sc, f"{tag}_hsp", [128, 32 * SPAN], BF16)
            w2v = H["l1b_ff2w"].rearrange("(kc p) f -> kc p f", p=128)
            for sp in range(NSPAN):
                sl = slice(sp * SPAN, (sp + 1) * SPAN)
                def evh(ft, sp2, ps, sp=sp):
                    act(hsp[:, ft * SPAN:(ft + 1) * SPAN], ps[0:128, 0:SPAN],
                        ACT.Relu, reads=[ps, bias_sb], writes=[hsp],
                        bias=bias_sb[:, ft:ft + 1])
                proj_fm(None, xln, H["l1b_ff1w"], 4 * D, evh, spans=[sp])
                for ft in range(KC):
                    for kq in range(4):
                        sg = kq % 2
                        b.dma_group("y", [(wstage[sg][:, kc * 128:kc * 128 + 128],
                                           w2v[kq * 8 + kc][:, ft * 128:(ft + 1) * 128],
                                           [], [wstage[sg]]) for kc in range(KC)])
                        cast_to("v", wtile[sg][:, 0:KC * 128], wstage[sg][:, 0:KC * 128],
                                reads=[wstage[sg]], writes=[wtile[sg]])
                        def fn(eng, sg=sg, kq=kq, ft=ft, sl=sl):
                            inst = None
                            for kc in range(KC):
                                inst = eng.matmul(
                                    out=psA[0:128, 0:SPAN],
                                    lhsT=wtile[sg][:, kc * 128:kc * 128 + 128],
                                    rhs=hsp[:, (kq * 8 + kc) * SPAN:(kq * 8 + kc + 1) * SPAN],
                                    start=(kq == 0 and kc == 0),
                                    stop=(kq == 3 and kc == KC - 1))
                            return inst
                        b.op("t", fn, reads=[wtile[sg], hsp], writes=[psA])
                    act(tws3[:], psA[0:128, 0:SPAN], ACT.Copy, reads=[psA, bias_sb],
                        writes=[tws3], bias=bias_sb[:, 32 + ft:33 + ft])
                    vop("tensor_add", xs[ft][:, sl], reads=[xs[ft], tws3], writes=[xs[ft]],
                        in0=xs[ft][:, sl], in1=tws3[:])

            for c in range(KC):
                for h in range(2):
                    ob = wstage[h]
                    hc = slice(h * 1024, (h + 1) * 1024)
                    b.dma("y", ob[:], xsv[c][:, hc], writes=[ob])
                    vop("tensor_add", xs[c][:, hc], reads=[xs[c], ob], writes=[xs[c]],
                        in0=xs[c][:, hc], in1=ob[:])

    def store_out(src_chunks):
        ov = out_ap.rearrange("(c p) t -> c p t", p=128)
        for c in range(KC):
            b.dma("gd", ov[c], src_chunks[c][:], reads=[src_chunks[c]])

    if trunc >= 1:
        l0_attn_a()
    if trunc >= 2:
        l0_ffn("f0a")
    if trunc >= 3:
        l0_attn_b()
    if trunc >= 4:
        l0_ffn("f0b")
    if trunc >= 5:
        l1_attn_a()
    if trunc >= 6:
        l1_ffn("f1a")
    if trunc >= 7:
        l1_attn_b()
    if trunc >= 8:
        l1_ffn("f1b")
    store_out(xs)
    b.finalize(["out"])
    return nc


def _consts():
    c = {}
    c["c_ident_bf"] = np.eye(128, dtype=np.float32)
    c["c_ident_f32"] = np.eye(128, dtype=np.float32)
    c["c_ones_col"] = np.ones((128, 1), dtype=np.float32)
    c["c_ones_colf"] = np.ones((128, 1), dtype=np.float32)
    c["c_ones_row"] = np.ones((1, 128), dtype=np.float32)
    sel2 = np.zeros((2, 128), dtype=np.float32)
    sel2[0, 0:64] = 1.0
    sel2[1, 64:128] = 1.0
    c["c_sel2"] = sel2
    tm = np.zeros((128, 4 * SPAN), dtype=np.float32)
    r = np.arange(128)[:, None]
    col = np.arange(SPAN)[None, :]
    for u in range(4):
        tm[:, u * SPAN:(u + 1) * SPAN] = np.where(col < r + 128 * u, NEG, 0.0)
    c["c_trimask"] = tm
    c["c_slotmask"] = np.where(col < 4 * (r + 1), NEG, 0.0).astype(np.float32)
    had = np.array([[1, 1, 1, 1], [1, -1, 1, -1], [1, 1, -1, -1], [1, -1, -1, 1]],
                   dtype=np.float32)
    cdma = np.tile(had, (1, 16))
    cs = np.zeros((128, 4), dtype=np.float32)
    for p in range(128):
        cs[p, :] = cdma[:, p % 64]
    c["c_csign"] = cs
    base = 10000.0
    dh = 64
    inv = 1.0 / base ** (np.arange(0, dh, 2, dtype=np.float32) / dh)
    pos = np.arange(T, dtype=np.float32)
    fr = pos[:, None] * inv[None, :]
    emb = np.concatenate([fr, fr], axis=-1)
    cosv = np.cos(emb).T
    sinv = np.sin(emb).T
    cos2 = np.tile(cosv, (2, 1)).astype(np.float32)
    sin2 = np.tile(sinv, (2, 1)).astype(np.float32)
    csT = np.zeros((128, T), dtype=np.float32)
    for p in range(128):
        csT[p, :] = cdma[np.arange(T) % 4, p % 64]
    c["c_ropec"] = cos2
    c["c_ropes"] = sin2
    rsign = np.where((np.arange(128) % 64) < 32, -1.0, 1.0).astype(np.float32)[:, None]
    c["c_ropecc"] = cos2 * csT
    c["c_ropesc"] = sin2 * csT * rsign
    return c


_BF16_INPUTS = {"c_ident_bf", "c_ones_col", "c_trimask", "c_slotmask"}


def _to_ml_bf16(a):
    import ml_dtypes
    return np.ascontiguousarray(a).astype(ml_dtypes.bfloat16)


def kernel(**inputs):
    trunc = TRUNC
    nc = build_program(trunc)
    consts = _consts()

    def TT(a):
        return np.ascontiguousarray(np.asarray(a).T.astype(np.float32))

    def BIAS(a, nft=None):
        a = np.asarray(a).astype(np.float32)
        n = a.shape[0] // 128
        return np.ascontiguousarray(a.reshape(n, 128).T) if a.shape[0] >= 128 else \
            np.ascontiguousarray(a.reshape(1, -1)).T.reshape(128, -1) if False else \
            np.pad(a.reshape(1, -1), ((0, 0), (0, 0))).T.reshape(-1, 1) * np.ones((1, 1), np.float32)

    def BIAS128(a):
        a = np.asarray(a).astype(np.float32)
        if a.size % 128 == 0 and a.size >= 128:
            return np.ascontiguousarray(a.reshape(-1, 128).T)
        out = np.zeros((128, 1), dtype=np.float32)
        out[:a.size, 0] = a
        return out

    base = {}
    base["l0a_wqkv"] = np.ascontiguousarray(inputs["L0_attn_a_W_qkv"].astype(np.float32))
    for i in range(1, 5):
        base[f"cn_w{i}"] = np.ascontiguousarray(inputs[f"L0_ffn_cn_w{i}"].astype(np.float32))
        base[f"cn_b{i}"] = BIAS128(inputs[f"L0_ffn_cn_b{i}"])
    for i in range(1, 4):
        base[f"gc_w{i}"] = np.ascontiguousarray(inputs[f"L0_ffn_gc_w{i}"].astype(np.float32))
        base[f"gc_b{i}"] = BIAS128(inputs[f"L0_ffn_gc_b{i}"])
    base["ar"] = inputs["L0_ffn_adaptive_range"].reshape(1, 1).astype(np.float32)
    base["l0b_wq"] = np.ascontiguousarray(inputs["L0_attn_b_Wq"].astype(np.float32))
    base["l0b_wk"] = np.ascontiguousarray(inputs["L0_attn_b_Wk"].astype(np.float32))
    base["l0b_wv"] = np.ascontiguousarray(inputs["L0_attn_b_Wv"].astype(np.float32))
    base["l0b_wo"] = np.ascontiguousarray(inputs["L0_attn_b_Wo"].astype(np.float32))
    base["l0b_aq"] = np.ascontiguousarray(inputs["L0_attn_b_Aq"].astype(np.float32))
    base["l0b_av"] = np.ascontiguousarray(inputs["L0_attn_b_Av"].astype(np.float32))
    base["l0b_ao"] = np.ascontiguousarray(inputs["L0_attn_b_Ao"].astype(np.float32))
    base["l0b_bq"] = np.ascontiguousarray(inputs["L0_attn_b_Bq"].astype(np.float32))
    base["l0b_bv"] = np.ascontiguousarray(inputs["L0_attn_b_Bv"].astype(np.float32))
    base["l0b_bo"] = np.ascontiguousarray(inputs["L0_attn_b_Bo"].astype(np.float32))
    for w in ["Wq", "Wk", "Wv", "Wo", "Wfc"]:
        base[f"l1a_{w.lower()}"] = TT(inputs[f"L1_attn_a_{w}"])
    for w in ["bq", "bk", "bv", "bo", "bfc"]:
        base[f"l1a_{w}"] = BIAS128(inputs[f"L1_attn_a_{w}"])
    base["l1a_projT"] = TT(inputs["L1_attn_a_proj"])
    base["l1b_sfw"] = TT(inputs["L1_attn_b_sf_w"])
    base["l1b_mask"] = TT(inputs["L1_attn_b_mask"])
    base["l1b_sfb"] = BIAS128(inputs["L1_attn_b_sf_b"])
    for w in ["wq", "wk", "wv", "wo"]:
        base[f"l1b_{w}"] = TT(inputs[f"L1_attn_b_{w}"])
        base[f"l1b_b{w[1]}"] = BIAS128(inputs[f"L1_attn_b_b{w[1]}"])
    base["l1b_ff1w"] = TT(inputs["L1_attn_b_ff1_w"])
    base["l1b_ff1b"] = BIAS128(inputs["L1_attn_b_ff1_b"])
    base["l1b_ff2w"] = TT(inputs["L1_attn_b_ff2_w"])
    base["l1b_ff2b"] = BIAS128(inputs["L1_attn_b_ff2_b"])
    for g in ["g1", "b1", "g2", "b2", "g3", "b3"]:
        base[f"l1b_{g}"] = BIAS128(inputs[f"L1_attn_b_{g}"])
    base["l1f_wr"] = TT(inputs["L1_ffn_W_r"])
    base["l1f_wk"] = TT(inputs["L1_ffn_W_k"])
    base["l1f_wv"] = TT(inputs["L1_ffn_W_v"])
    base["l1f_mur"] = BIAS128(inputs["L1_ffn_mu_r"])
    base["l1f_muk"] = BIAS128(inputs["L1_ffn_mu_k"])
    base.update(consts)
    for k in list(base.keys()):
        if k in _BF16_INPUTS:
            base[k] = _to_ml_bf16(base[k])

    x = np.asarray(inputs["L0_attn_a_x"]).astype(np.float32)
    in_maps = []
    for core in range(8):
        m = dict(base)
        m["x"] = np.ascontiguousarray(x[core].T)
        in_maps.append(m)

    res = run_bass_kernel_spmd(nc, in_maps, core_ids=list(range(8)))
    if MK_DEBUG:
        import __main__
        __main__.MK_DBG_RESULTS = res.results
    outs = [np.asarray(r["out"]).T for r in res.results]
    return np.stack(outs, axis=0).astype(np.float32)


if __name__ == "__main__":
    print("build test")
    nc = build_program(1)
    print("instructions:", nc.n_instructions() if hasattr(nc, "n_instructions") else "?")


# version 06
# speedup vs baseline: 1.0244x; 1.0016x over previous
import os
import numpy as np
from contextlib import ExitStack

import concourse.bass as bass
import concourse.mybir as mybir
from concourse.bass_utils import run_bass_kernel_spmd

F32 = mybir.dt.float32
BF16 = mybir.dt.bfloat16
AX = mybir.AxisListType.X
ALU = mybir.AluOpType
ACT = mybir.ActivationFunctionType

T = 2048
D = 1024
KC = 8
NSPAN = 4
SPAN = 512
NH = 16
DH = 64
NEG = -1.0e30

TRUNC = int(os.environ.get("MK_TRUNC", "99"))
MK_DEBUG = os.environ.get("MK_DEBUG", "")


class B:
    COMPUTE = ("t", "v", "s", "g")
    QUEUES = ("y", "gd")
    POOL = {"y": 20, "gd": 8}

    def __init__(self, nc):
        self.nc = nc
        self.st = ExitStack()
        self.sems = {e: self.st.enter_context(nc.semaphore(f"sem_{e}"))
                     for e in self.COMPUTE}
        self.qsems = {}
        self.qtot = {}
        self.qptr = {}
        for q in self.QUEUES:
            self.qsems[q] = [self.st.enter_context(nc.semaphore(f"qs_{q}{i}"))
                             for i in range(self.POOL[q])]
            self.qtot[q] = [0] * self.POOL[q]
            self.qptr[q] = 0
        self.seq = {e: 0 for e in self.COMPUTE + self.QUEUES}
        self.progs = {e: [] for e in self.COMPUTE + self.QUEUES}
        self.last_write = {}
        self.readers = {}
        self.watermark = {e: {} for e in self.COMPUTE + self.QUEUES}
        self._names = set()
        self._sbufs = {}
        self.pending_free = {}

    def sbuf(self, name, shape, dtype):
        assert name not in self._names, name
        self._names.add(name)
        h = self.st.enter_context(self.nc.sbuf_tensor(name, list(shape), dtype))
        return h

    def psum(self, name, shape, dtype):
        h = self.st.enter_context(self.nc.psum_tensor(name, list(shape), dtype))
        return h

    def scope(self):
        return ExitStack()

    def sbuf_in(self, stack, name, shape, dtype):
        assert name not in self._names, name
        self._names.add(name)
        h = stack.enter_context(self.nc.sbuf_tensor(name, list(shape), dtype))
        pend = getattr(self, "pending_free", None)
        if pend:
            self.readers[name] = [key + (v,) for key, v in pend.items()]
        def _cleanup(nm=name):
            self._names.discard(nm)
            pendd = self.pending_free
            lw = self.last_write.pop(nm, None)
            toks = list(self.readers.pop(nm, ()) or ())
            if lw is not None:
                toks.append(lw)
            for tok in toks:
                key, v = tok[:-1], tok[-1]
                if pendd.get(key, 0) < v:
                    pendd[key] = v
        stack.callback(_cleanup)
        return h

    @staticmethod
    def _key(x):
        t = getattr(x, "tensor", x)
        return t.name

    def _collect(self, reads, writes):
        waits = {}
        def need(tok):
            key = tok[:-1]
            v = tok[-1]
            if key not in waits or v > waits[key]:
                waits[key] = v
        for r in reads:
            lw = self.last_write.get(self._key(r))
            if lw:
                need(lw)
        for w in writes:
            k = self._key(w)
            lw = self.last_write.get(k)
            if lw:
                need(lw)
            for tok in self.readers.get(k, ()):
                need(tok)
        return waits

    def _emit_waits(self, e, waits):
        wl = []
        for key, v in waits.items():
            if key[0] == "E" and key[1] == e:
                pass
            if self.watermark[e].get(key, 0) >= v:
                continue
            self.watermark[e][key] = v
            wl.append((key, v))
        return wl

    def op(self, e, fn, reads=(), writes=()):
        assert e in self.COMPUTE
        waits = self._collect(reads, writes)
        idx = self.seq[e] + 1
        self.seq[e] = idx
        for r in reads:
            self.readers.setdefault(self._key(r), []).append(("E", e, idx))
        for w in writes:
            k = self._key(w)
            self.last_write[k] = ("E", e, idx)
            self.readers[k] = []
        wl = self._emit_waits(e, waits)
        self.progs[e].append((wl, fn, e))

    def dma_group(self, q, items):
        assert q in self.QUEUES
        reads = []
        writes = []
        for (_o, _i, rds, wts) in items:
            reads += list(rds)
            writes += list(wts)
        waits = self._collect(reads, writes)
        si = self.qptr[q]
        self.qptr[q] = (si + 1) % self.POOL[q]
        gate = self.qtot[q][si]
        self.qtot[q][si] = gate + 16 * len(items)
        tot = self.qtot[q][si]
        for r in reads:
            self.readers.setdefault(self._key(r), []).append(("Q", q, si, tot))
        for w in writes:
            k = self._key(w)
            self.last_write[k] = ("Q", q, si, tot)
            self.readers[k] = []
        wl = self._emit_waits(q, waits)
        aps = [(o, i) for (o, i, _r, _w) in items]
        self.progs[q].append((wl, (si, gate, aps), q))
        return ("Q", q, si, tot)

    def dma(self, q, out_ap, in_ap, reads=(), writes=()):
        self.dma_group(q, [(out_ap, in_ap, reads, writes)])

    def finalize(self, out_keys):
        nc = self.nc
        sems = self.sems
        qsems = self.qsems

        def wait_tok(eng, key, v):
            if key[0] == "E":
                eng.wait_ge(sems[key[1]], v)
            else:
                eng.wait_ge(qsems[key[1]][key[2]], v)

        def runner(e):
            def run(eng):
                for wl, fn, kind in self.progs[e]:
                    for (key, v) in wl:
                        wait_tok(eng, key, v)
                    if e in self.COMPUTE:
                        inst = fn(eng)
                        inst.then_inc(sems[e], 1)
                    else:
                        si, gate, aps = fn
                        if gate > 0:
                            eng.wait_ge(qsems[e][si], gate)
                        for (o, i) in aps:
                            eng.dma_start(out=o, in_=i).then_inc(
                                qsems[e][si], 16)
                if e == "gd":
                    for si2 in range(self.POOL["gd"]):
                        if self.qtot["gd"][si2] > 0:
                            eng.wait_ge(qsems["gd"][si2], self.qtot["gd"][si2])
            return run

        with nc.Block() as blk:
            blk.sync(lambda eng: runner("y")(eng))
            blk.vector(lambda eng: runner("v")(eng))
            blk.scalar(lambda eng: runner("s")(eng))
            blk.tensor(lambda eng: runner("t")(eng))

            def gps(eng):
                runner("g")(eng)
                runner("gd")(eng)
            blk.gpsimd(gps)
        self.st.close()


def build_program(trunc=99):
    nc = bass.Bass("TRN2", target_bir_lowering=False)
    b = B(nc)

    def din(name, shape, dtype=F32):
        return nc.dram_tensor(name, list(shape), dtype, kind="ExternalInput").ap()

    H = {}
    H["x"] = din("x", [D, T])
    H["l0a_wqkv"] = din("l0a_wqkv", [D, 3 * D])
    for i, (a, c) in enumerate([(D, D), (D, 512), (512, 256), (256, 1)]):
        H[f"cn_w{i+1}"] = din(f"cn_w{i+1}", [a, c])
        H[f"cn_b{i+1}"] = din(f"cn_b{i+1}", [128, max(1, c // 128) if c >= 128 else 1])
    for i, (a, c) in enumerate([(D, 512), (512, 256), (256, 1)]):
        H[f"gc_w{i+1}"] = din(f"gc_w{i+1}", [a, c])
        H[f"gc_b{i+1}"] = din(f"gc_b{i+1}", [128, max(1, c // 128) if c >= 128 else 1])
    H["ar"] = din("ar", [1, 1])
    for w in ["wq", "wk", "wv", "wo"]:
        H[f"l0b_{w}"] = din(f"l0b_{w}", [D, D])
    for w in ["aq", "av", "ao"]:
        H[f"l0b_{w}"] = din(f"l0b_{w}", [D, 64])
    for w in ["bq", "bv", "bo"]:
        H[f"l0b_{w}"] = din(f"l0b_{w}", [64, D])
    for w in ["wq", "wk", "wv", "wo", "wfc"]:
        H[f"l1a_{w}"] = din(f"l1a_{w}", [D, D])
    for w in ["bq", "bk", "bv", "bo", "bfc"]:
        H[f"l1a_{w}"] = din(f"l1a_{w}", [128, 8])
    H["l1a_projT"] = din("l1a_projT", [64, 266])
    H["l1b_sfw"] = din("l1b_sfw", [D, D])
    H["l1b_mask"] = din("l1b_mask", [D, D])
    H["l1b_sfb"] = din("l1b_sfb", [128, 8])
    for w in ["wq", "wk", "wv", "wo"]:
        H[f"l1b_{w}"] = din(f"l1b_{w}", [D, D])
    for w in ["bq", "bk", "bv", "bo", "bo2"][:4]:
        H[f"l1b_{w}"] = din(f"l1b_{w}", [128, 8])
    H["l1b_ff1w"] = din("l1b_ff1w", [D, 4 * D])
    H["l1b_ff1b"] = din("l1b_ff1b", [128, 32])
    H["l1b_ff2w"] = din("l1b_ff2w", [4 * D, D])
    H["l1b_ff2b"] = din("l1b_ff2b", [128, 8])
    for g in ["g1", "b1", "g2", "b2", "g3", "b3"]:
        H[f"l1b_{g}"] = din(f"l1b_{g}", [128, 8])
    for w in ["wr", "wk", "wv"]:
        H[f"l1f_{w}"] = din(f"l1f_{w}", [D, D])
    H["l1f_mur"] = din("l1f_mur", [128, 8])
    H["l1f_muk"] = din("l1f_muk", [128, 8])
    H["c_ident_bf"] = din("c_ident_bf", [128, 128], BF16)
    H["c_ident_f32"] = din("c_ident_f32", [128, 128])
    H["c_ones_col"] = din("c_ones_col", [128, 1], BF16)
    H["c_ones_colf"] = din("c_ones_colf", [128, 1])
    H["c_ones_row"] = din("c_ones_row", [1, 128])
    H["c_sel2"] = din("c_sel2", [2, 128])
    H["c_trimask"] = din("c_trimask", [128, 4 * SPAN], BF16)
    H["c_slotmask"] = din("c_slotmask", [128, SPAN], BF16)
    H["c_csign"] = din("c_csign", [128, 4])
    H["c_ropecc"] = din("c_ropecc", [128, T])
    H["c_ropesc"] = din("c_ropesc", [128, T])
    H["c_ropec"] = din("c_ropec", [128, T])
    H["c_ropes"] = din("c_ropes", [128, T])

    out_ap = nc.dram_tensor("out", [D, T], F32, kind="ExternalOutput").ap()
    xspill = nc.dram_tensor("xspill", [D, T], F32, kind="Internal").ap()

    xs = [b.sbuf(f"xs{c}", [128, T], F32) for c in range(KC)]
    xln = [b.sbuf(f"xln{c}", [128, T], BF16) for c in range(KC)]
    ident_bf = b.sbuf("ident_bf", [128, 128], BF16)
    ident_f = b.sbuf("ident_f", [128, 128], F32)
    ones_col = b.sbuf("ones_col", [128, 1], BF16)
    ones_colf = b.sbuf("ones_colf", [128, 1], F32)
    ones_row = b.sbuf("ones_row", [1, 128], F32)
    sel2 = b.sbuf("sel2", [2, 128], F32)
    allones = b.sbuf("allones", [128, 128], F32)
    slotmask = b.sbuf("slotmask", [128, SPAN], BF16)
    csign = b.sbuf("csign", [128, 4], F32)
    lnrows_t = b.sbuf("lnrows_t", [128, T], F32)
    lnrow = lnrows_t[0:1, :]
    lnrow2 = lnrows_t[32:33, :]
    lnrow3 = lnrows_t[64:65, :]

    psA = b.psum("psA", [128, 512], F32)
    psS01 = b.psum("psS01", [128, 1024], F32)
    psS23 = b.psum("psS23", [128, 1024], F32)
    psS0 = psS01[0:128, 0:512]
    psS1 = psS23[0:128, 0:512]
    psB = psS01[0:128, 512:1024]
    psR2 = psS23[0:128, 512:1024]
    psP0 = b.psum("psP0", [128, 512], F32)
    psR = b.psum("psR", [128, 512], F32)
    psT = b.psum("psT", [128, 1024], BF16)

    def _ao_ms(eng):
        return eng.memset(allones[:], 1.0)
    b.op("v", _ao_ms, reads=[], writes=[allones])
    for dst, src in [(ident_bf, H["c_ident_bf"]), (ident_f, H["c_ident_f32"]),
                     (ones_col, H["c_ones_col"]), (ones_colf, H["c_ones_colf"]),
                     (ones_row, H["c_ones_row"]), (sel2, H["c_sel2"]),
                     (slotmask, H["c_slotmask"]),
                     (csign, H["c_csign"])]:
        b.dma("gd", dst[:], src, writes=[dst])

    xv = H["x"].rearrange("(c p) t -> c p t", p=128)
    for c in range(KC):
        b.dma("gd", xs[c][:], xv[c], writes=[xs[c]])

    vop_early = []
    uid = [0]
    def U(pfx):
        uid[0] += 1
        return f"{pfx}{uid[0]}"

    dbg_outs = []
    def dbg(name, ap_list):
        if not MK_DEBUG or name not in MK_DEBUG.split(","):
            return
        for i, ap in enumerate(ap_list):
            p, f = ap.shape[0], ap.shape[1]
            dt = ap.dtype
            o = nc.dram_tensor(f"dbg_{name}_{i}", [p, f], dt, kind="ExternalOutput").ap()
            b.dma("gd", o, ap, reads=[ap])
            dbg_outs.append(f"dbg_{name}_{i}")

    def mm_group(ps_ap, items, reads, writes, extra=()):
        n = len(items)
        def fn(eng, items=tuple(items), ps=ps_ap, n=n):
            inst = None
            for i, (lt, rh) in enumerate(items):
                inst = eng.matmul(out=ps, lhsT=lt, rhs=rh,
                                  start=(i == 0), stop=(i == n - 1))
            return inst
        b.op("t", fn, reads=list(reads), writes=list(writes))

    def transpose(ps_ap, in_ap, ident_ap, reads, writes):
        def fn(eng, o=ps_ap, i=in_ap, idn=ident_ap):
            return eng.transpose(out=o, in_=i, identity=idn)
        b.op("t", fn, reads=list(reads), writes=list(writes))

    def act(out_ap, in_ap, func, reads, writes, bias=0.0, scale=1.0, accum=None):
        if func == ACT.Copy and not isinstance(bias, float):
            func = ACT.Identity
        def fn(eng, o=out_ap, i=in_ap, f=func, bi=bias, sc=scale, ac=accum):
            kw = {}
            if ac is not None:
                kw["accum_out"] = ac
            return eng.activation(out=o, in_=i, func=f, bias=bi, scale=sc, **kw)
        b.op("s", fn, reads=list(reads), writes=list(writes))

    def vop(name, out_ap, reads, writes, **kw):
        def fn(eng, nm=name, o=out_ap, kw=kw):
            if nm == "memset":
                return eng.memset(o, kw["constant"])
            return getattr(eng, nm)(out=o, **kw)
        b.op("v", fn, reads=list(reads), writes=list(writes))

    def scalar_copy(out_ap, in_ap, reads, writes):
        def fn(eng, o=out_ap, i=in_ap):
            return eng.copy(out=o, in_=i)
        b.op("s", fn, reads=list(reads), writes=list(writes))

    def cast_to(e, out_ap, in_ap, reads, writes):
        if e == "s":
            scalar_copy(out_ap, in_ap, reads, writes)
        else:
            vop("tensor_copy", out_ap, reads, writes, in_=in_ap)

    def bcast_row(ps_ap, row_ap, m, reads, base=0):
        def fn(eng, o=ps_ap, r=row_ap, m=m, base=base):
            return eng.matmul(out=o, lhsT=allones[base:base + 1, 0:m], rhs=r,
                              start=True, stop=True)
        b.op("t", fn, reads=[allones] + list(reads), writes=[ps_ap])

    wstage = [b.sbuf(f"wstage{i}", [128, 1024], F32) for i in range(2)]
    wtile = [b.sbuf(f"wtile{i}", [128, 1024], BF16) for i in range(2)]
    bias_sb = b.sbuf("bias_sb", [128, 64], F32)

    def load_bias(dram_ap, nft):
        b.dma("y", bias_sb[:, 0:nft], dram_ap[:, 0:nft], writes=[bias_sb])

    QROT = ["y", "gd"]

    def proj_fm(dst, src, w_dram, F_out, evac, src_rows=KC, w_row_off=0,
                extra_mm=None, spans=range(NSPAN), cast_e="v", src_get=None,
                src_reads=None):
        nft = F_out // 128
        wv = w_dram.rearrange("(kc p) f -> kc p f", p=128)
        if src_get is None:
            src_get = lambda kc, sp: src[kc][:, sp * SPAN:(sp + 1) * SPAN]
            src_reads = src
        spans = list(spans)
        banks = [psA, psB, psS0, psS1][:len(spans)]
        for ft in range(nft):
            sg = ft % 2
            b.dma_group(QROT[ft % 2],
                        [(wstage[sg][:, kc * 128:kc * 128 + 128],
                          wv[kc][:, ft * 128:(ft + 1) * 128], [], [wstage[sg]])
                         for kc in range(src_rows)])
            cast_to(cast_e, wtile[sg][:, 0:src_rows * 128], wstage[sg][:, 0:src_rows * 128],
                    reads=[wstage[sg]], writes=[wtile[sg]])
            ex_items = {j: [] for j in range(len(spans))}
            ex_reads = {j: [] for j in range(len(spans))}
            if extra_mm is not None:
                for j, sp in enumerate(spans):
                    for (lt, rh, rds) in extra_mm(ft, sp):
                        ex_items[j].append((lt, rh))
                        ex_reads[j] += rds
            def fn(eng, sg=sg, spans=tuple(spans), banks=tuple(banks),
                   ex=ex_items, ft=ft):
                inst = None
                for kc in range(src_rows):
                    for j, sp in enumerate(spans):
                        inst = eng.matmul(out=banks[j][0:128, 0:SPAN],
                                          lhsT=wtile[sg][:, kc * 128:kc * 128 + 128],
                                          rhs=src_get(kc, sp),
                                          start=(kc == 0),
                                          stop=(kc == src_rows - 1 and not ex[j]))
                for j in range(len(spans)):
                    for i2, (lt, rh) in enumerate(ex[j]):
                        inst = eng.matmul(out=banks[j][0:128, 0:SPAN], lhsT=lt, rhs=rh,
                                          start=False, stop=(i2 == len(ex[j]) - 1))
                return inst
            reads = [wtile[sg]] + list(src_reads)
            for j in range(len(spans)):
                reads += ex_reads[j]
            b.op("t", fn, reads=reads, writes=list(banks))
            for j, sp in enumerate(spans):
                evac(ft, sp, banks[j])

    sq_ps = [psS0, psS1, psP0, psR2]
    sqbuf = b.sbuf("sqbuf", [128, T], BF16)

    def emit_ln_stats(src_chunks, tag="ln"):
        tmp_bf = xln
        for c in range(KC):
            cast_to("v" if c % 2 == 0 else "s", tmp_bf[c][:], src_chunks[c][:],
                    reads=[src_chunks[c]], writes=[tmp_bf[c]])
        for c in range(KC):
            def sqf(eng, i=tmp_bf[c], o=sqbuf):
                return eng.activation(out=o[:], in_=i[:], func=ACT.Square)
            b.op("s", sqf, reads=[tmp_bf[c]], writes=[sqbuf])
            for sp in range(NSPAN):
                sl = slice(sp * SPAN, (sp + 1) * SPAN)
                ps = sq_ps[sp]
                def fn(eng, ps=ps, sl=sl, c=c):
                    return eng.matmul(out=ps[0:1, 0:SPAN], lhsT=ones_col[:],
                                      rhs=sqbuf[:, sl], start=(c == 0),
                                      stop=(c == KC - 1))
                b.op("t", fn, reads=[ones_col, sqbuf], writes=[ps])
        for sp in range(NSPAN):
            sl = slice(sp * SPAN, (sp + 1) * SPAN)
            items = [(ones_col[:], tmp_bf[c][:, sl]) for c in range(KC)]
            mm_group(psR[0:1, 0:SPAN], items,
                     reads=[ones_col] + list(tmp_bf), writes=[psR])
            act(lnrow3[0:1, sl], psR[0:1, 0:SPAN], ACT.Copy, reads=[psR],
                writes=[lnrow3], scale=1.0 / D)
            act(lnrow[0:1, sl], sq_ps[sp][0:1, 0:SPAN], ACT.Copy,
                reads=[sq_ps[sp]], writes=[lnrow], scale=1.0 / D)
        vop("tensor_mul", lnrow2[0:1, :], reads=[lnrow3], writes=[lnrow2],
            in0=lnrow3[0:1, :], in1=lnrow3[0:1, :])
        for sp in range(NSPAN):
            sl = slice(sp * SPAN, (sp + 1) * SPAN)
            scalar_copy(psR2[0:1, 0:SPAN], lnrow2[0:1, sl], reads=[lnrow2],
                        writes=[psR2])
            vop("tensor_sub", lnrow[0:1, sl], reads=[lnrow, psR2], writes=[lnrow],
                in0=lnrow[0:1, sl], in1=psR2[0:1, 0:SPAN])
        vop("tensor_scalar_add", lnrow[0:1, :], reads=[lnrow], writes=[lnrow],
            in0=lnrow[0:1, :], scalar1=1e-5)
        def rsq(eng):
            return eng.activation(out=lnrow[0:1, :], in_=lnrow[0:1, :],
                                  func=ACT.Sqrt)
        b.op("s", rsq, reads=[lnrow], writes=[lnrow])
        vop("reciprocal", lnrow[0:1, :], reads=[lnrow], writes=[lnrow],
            in_=lnrow[0:1, :])
        for sp in range(NSPAN):
            sl = slice(sp * SPAN, (sp + 1) * SPAN)
            scalar_copy(psR2[0:1, 0:SPAN], lnrow[0:1, sl], reads=[lnrow],
                        writes=[psR2])
            vop("tensor_mul", lnrow2[0:1, sl], reads=[lnrow3, psR2], writes=[lnrow2],
                in0=lnrow3[0:1, sl], in1=psR2[0:1, 0:SPAN])

    def emit_ln_apply(src_chunks, out_dst, spans=range(NSPAN), col0=0,
                      gamma=None, beta=None):
        for sp in spans:
            sl = slice(sp * SPAN, (sp + 1) * SPAN)
            osl = slice(sp * SPAN - col0, (sp + 1) * SPAN - col0)
            bcast_row(psR[0:128, 0:SPAN], lnrow[0:1, sl], 128, reads=[lnrow])
            bcast_row(psR2[0:128, 0:SPAN], lnrow2[0:1, sl], 128, reads=[lnrow2],
                      base=32)
            for c in range(KC):
                od = out_dst[c]
                vop("tensor_mul", od[:, osl], reads=[src_chunks[c], psR], writes=[od],
                    in0=src_chunks[c][:, sl], in1=psR[0:128, 0:SPAN])
                vop("tensor_sub", od[:, osl], reads=[od, psR2], writes=[od],
                    in0=od[:, osl], in1=psR2[0:128, 0:SPAN])
                if gamma is not None:
                    gsb, gcol = gamma
                    bsb, bcol = beta
                    vop("tensor_scalar", od[:, osl], reads=[od, gsb, bsb], writes=[od],
                        in0=od[:, osl], scalar1=gsb[:, gcol(c):gcol(c) + 1],
                        scalar2=bsb[:, bcol(c):bcol(c) + 1],
                        op0=ALU.mult, op1=ALU.add)

    def emit_layernorm(src_chunks, out_dst, gamma=None, beta=None, tag="ln"):
        emit_ln_stats(src_chunks, tag=tag)
        emit_ln_apply(src_chunks, out_dst, gamma=gamma, beta=beta)

    def l0_attn_a():
        with b.scope() as sc:
            trimask = b.sbuf_in(sc, "l0a_trimask", [128, 4 * SPAN], BF16)
            b.dma("gd", trimask[:], H["c_trimask"], writes=[trimask])
            emit_layernorm(xs, xln, tag="l0a_ln")
            dbg("xln", [xln[c][:] for c in range(KC)])
            qh2 = [b.sbuf_in(sc, f"l0a_q{i}", [128, T], BF16) for i in range(2)]
            kh2 = [b.sbuf_in(sc, f"l0a_k{i}", [128, T], BF16) for i in range(2)]
            va2 = [[b.sbuf_in(sc, f"l0a_va{i}_{t}", [128, 130], BF16) for t in range(16)]
                   for i in range(2)]
            pt2 = [b.sbuf_in(sc, f"l0a_pt{i}", [128, 16 * SPAN], BF16) for i in range(2)]
            denP2_ = [b.sbuf_in(sc, f"l0a_denP{i}", [64, T], F32) for i in range(2)]
            ao_s = b.sbuf_in(sc, "l0a_ao", [128, T], BF16)
            wq = H["l0a_wqkv"]
            for hg in range(8):
                qh, kh, va, pt = qh2[hg % 2], kh2[hg % 2], va2[hg % 2], pt2[hg % 2]
                denP = denP2_[hg % 2]
                den0 = denP[0:1, :]
                den1 = denP[32:33, :]
                ao = ao_s
                def evac_q(ft, sp, ps, dst=qh):
                    act(dst[:, sp * SPAN:(sp + 1) * SPAN], ps[0:128, 0:SPAN], ACT.Copy,
                        reads=[ps], writes=[dst], scale=0.125)
                def evac_k(ft, sp, ps, dst=kh):
                    act(dst[:, sp * SPAN:(sp + 1) * SPAN], ps[0:128, 0:SPAN], ACT.Copy,
                        reads=[ps], writes=[dst])
                proj_fm([qh], xln, wq[:, hg * 128:(hg + 1) * 128], 128, evac_q)
                proj_fm([kh], xln, wq[:, D + hg * 128:D + (hg + 1) * 128], 128, evac_k)
                if hg == 0:
                    dbg("qh", [qh[:]])
                    dbg("kh", [kh[:]])
                sg = hg % 2
                wvv = wq.rearrange("(kc p) f -> kc p f", p=128)
                b.dma_group("y", [(wstage[sg][:, kc * 128:kc * 128 + 128],
                                   wvv[kc][:, 2 * D + hg * 128:2 * D + hg * 128 + 128],
                                   [], [wstage[sg]]) for kc in range(KC)])
                cast_to("v", wtile[sg][:, 0:KC * 128], wstage[sg][:, 0:KC * 128],
                        reads=[wstage[sg]], writes=[wtile[sg]])
                for tt in range(16):
                    ps = psA if tt % 2 == 0 else psB
                    items = [(xln[kc][:, tt * 128:tt * 128 + 128],
                              wtile[sg][:, kc * 128:kc * 128 + 128]) for kc in range(KC)]
                    mm_group(ps[0:128, 0:128], items,
                             reads=[wtile[sg]] + list(xln), writes=[ps])
                    scalar_copy(va[tt][:, 0:64], ps[0:128, 0:64], reads=[ps], writes=[va[tt]])
                    scalar_copy(va[tt][:, 65:129], ps[0:128, 64:128], reads=[ps], writes=[va[tt]])
                    vop("memset", va[tt][:, 64:65], reads=[], writes=[va[tt]], constant=1.0)
                    vop("memset", va[tt][:, 129:130], reads=[], writes=[va[tt]], constant=1.0)
                if hg == 0:
                    dbg("va", [va[0][:], va[1][:]])
                for hh in range(2):
                    h = hg * 2 + hh
                    qsl = qh[hh * 64:(hh + 1) * 64, :]
                    ksl = kh[hh * 64:(hh + 1) * 64, :]
                    vcol = hh * 65
                    for bb in range(NSPAN):
                        qs = slice(bb * SPAN, (bb + 1) * SPAN)
                        nchunks = 4 * bb + 4
                        for cc0 in range(0, nchunks, 2):
                            ps = psS01 if (cc0 // 2) % 2 == 0 else psS23
                            def qkf(eng, cc0=cc0, qs=qs, ps=ps, ksl=ksl, qsl=qsl):
                                i1 = eng.matmul(out=ps[0:128, 0:SPAN],
                                                lhsT=ksl[:, cc0 * 128:cc0 * 128 + 128],
                                                rhs=qsl[:, qs], start=True, stop=True)
                                i1 = eng.matmul(out=ps[0:128, SPAN:2 * SPAN],
                                                lhsT=ksl[:, (cc0 + 1) * 128:(cc0 + 2) * 128],
                                                rhs=qsl[:, qs], start=True, stop=True)
                                return i1
                            b.op("t", qkf, reads=[kh, qh], writes=[ps])
                            for j in range(2):
                                u = cc0 + j - 4 * bb
                                if 0 <= u <= 3:
                                    vop("tensor_add", ps[0:128, j * SPAN:(j + 1) * SPAN],
                                        reads=[ps, trimask], writes=[ps],
                                        in0=ps[0:128, j * SPAN:(j + 1) * SPAN],
                                        in1=trimask[:, u * SPAN:(u + 1) * SPAN])
                            act(pt[:, cc0 * SPAN:(cc0 + 2) * SPAN], ps[0:128, 0:2 * SPAN],
                                ACT.Exp, reads=[ps], writes=[pt])
                        items = [(va[cc][:, vcol:vcol + 65],
                                  pt[:, cc * SPAN:(cc + 1) * SPAN]) for cc in range(nchunks)]
                        mm_group(psP0[0:65, 0:SPAN], items,
                                 reads=[pt] + [va[cc] for cc in range(nchunks)],
                                 writes=[psP0])
                        if hg == 0 and hh == 0 and bb == 0:
                            dbg("pt", [pt[:, 0:4 * SPAN]])
                        scalar_copy(ao[hh * 64:hh * 64 + 64, qs], psP0[0:64, 0:SPAN],
                                    reads=[psP0], writes=[ao])
                        dh_ = den0 if hh == 0 else den1
                        scalar_copy(dh_[0:1, qs], psP0[64:65, 0:SPAN],
                                    reads=[psP0], writes=[dh_])
                if hg == 0:
                    dbg("ao", [ao[:]])
                    dbg("den", [den0[:], den1[:]])
                vop("reciprocal", den0[0:1, :], reads=[den0], writes=[den0], in_=den0[0:1, :])
                vop("reciprocal", den1[0:1, :], reads=[den1], writes=[den1], in_=den1[0:1, :])
                for sp in range(NSPAN):
                    sl = slice(sp * SPAN, (sp + 1) * SPAN)
                    def fn0(eng, sl=sl, den0=den0):
                        return eng.matmul(out=psR[0:64, 0:SPAN], lhsT=ones_row[:, 0:64],
                                          rhs=den0[0:1, sl], start=True, stop=True)
                    b.op("t", fn0, reads=[ones_row, den0], writes=[psR])
                    def fn1(eng, sl=sl, den1=den1):
                        return eng.matmul(out=psR[64:128, 0:SPAN],
                                          lhsT=allones[32:33, 0:64],
                                          rhs=den1[0:1, sl], start=True, stop=True)
                    b.op("t", fn1, reads=[allones, den1], writes=[psR])
                    vop("tensor_mul", ao[:, sl], reads=[ao, psR], writes=[ao],
                        in0=ao[:, sl], in1=psR[0:128, 0:SPAN])
                    vop("tensor_add", xs[hg][:, sl], reads=[xs[hg], ao], writes=[xs[hg]],
                        in0=xs[hg][:, sl], in1=ao[:, sl])
                if hg == 0:
                    dbg("aon", [ao[:]])
                    dbg("rden", [den0[:]])

    def l0_ffn(tag):
        TH = T // 2
        with b.scope() as sc:
            emit_ln_stats(xs, tag=tag)
            of = [b.sbuf_in(sc, f"{tag}_of{c}", [128, TH], F32) for c in range(KC)]
            ob = [xln[c] for c in range(KC)]
            h1 = [xln[c] for c in range(KC)]
            h2 = [b.sbuf_in(sc, f"{tag}_h2{c}", [128, TH], BF16) for c in range(4)]
            h3 = [b.sbuf_in(sc, f"{tag}_h3{c}", [128, TH], BF16) for c in range(2)]
            gsb = b.sbuf_in(sc, f"{tag}_gsb", [128, 8], F32)
            g1 = b.sbuf_in(sc, f"{tag}_g1", [128, 4], F32)
            g2 = b.sbuf_in(sc, f"{tag}_g2", [128, 2], F32)
            gfv = b.sbuf_in(sc, f"{tag}_gfv", [1, 2], F32)
            arow = b.sbuf_in(sc, f"{tag}_arow", [1, 4], F32)
            smsb = b.sbuf_in(sc, f"{tag}_smsb", [128, 1], F32)
            tw = [b.sbuf_in(sc, f"{tag}_tw{i}", [128, SPAN], F32) for i in range(2)]
            gsc = b.sbuf_in(sc, f"{tag}_gsc", [128, T], F32)

            for c in range(KC):
                for sp in range(NSPAN):
                    sl = slice(sp * SPAN, (sp + 1) * SPAN)
                    bcast_row(psR[0:128, 0:SPAN], lnrow[0:1, sl], 128, reads=[lnrow])
                    vop("tensor_mul", gsc[:, sl], reads=[xs[c], psR], writes=[gsc],
                        in0=xs[c][:, sl], in1=psR[0:128, 0:SPAN])
                vop("reduce_sum", gsb[:, c:c + 1], reads=[gsc], writes=[gsb],
                    in_=gsc[:], axis=AX)
            vop("reduce_sum", smsb[0:1, 0:1], reads=[lnrow2], writes=[smsb],
                in_=lnrow2[0:1, :], axis=AX)
            def pbc(eng):
                return eng.matmul(out=psB[0:128, 0:1], lhsT=ones_row[:],
                                  rhs=smsb[0:1, 0:1], start=True, stop=True)
            b.op("t", pbc, reads=[smsb, ones_row], writes=[psB])
            vop("scalar_tensor_tensor", gsb[:], reads=[gsb, psB], writes=[gsb],
                in0=gsb[:], scalar=1.0, in1=psB[0:128, 0:1].broadcast_to([128, 8]),
                op0=ALU.mult, op1=ALU.subtract)
            vop("tensor_scalar", gsb[:], reads=[gsb], writes=[gsb],
                in0=gsb[:], scalar1=1.0 / T, scalar2=None, op0=ALU.mult)

            for nm, col in [("gc_b1", 0), ("gc_b2", 8), ("gc_b3", 12)]:
                w = 4 if nm == "gc_b1" else (2 if nm == "gc_b2" else 1)
                b.dma("y", bias_sb[:, col:col + w], H[nm][:, 0:w], writes=[bias_sb])
            wv1 = H["gc_w1"].rearrange("(kc p) f -> kc p f", p=128)
            for ft in range(4):
                sg = ft % 2
                b.dma_group("y", [(wstage[sg][:, kc * 128:kc * 128 + 128],
                                   wv1[kc][:, ft * 128:(ft + 1) * 128], [], [wstage[sg]])
                                  for kc in range(KC)])
                cast_to("v", wtile[sg][:, 0:KC * 128], wstage[sg][:, 0:KC * 128],
                        reads=[wstage[sg]], writes=[wtile[sg]])
                gb = b.sbuf
                def gcast(eng, sg=sg):
                    return eng.tensor_copy(out=gsc[:, 0:8], in_=gsb[:, 0:8])
                vop("tensor_copy", sqbuf[:, 0:8], reads=[gsb], writes=[sqbuf],
                    in_=gsb[:, 0:8])
                items = [(wtile[sg][:, kc * 128:kc * 128 + 128], sqbuf[:, kc:kc + 1])
                         for kc in range(KC)]
                mm_group(psA[0:128, 0:1], items, reads=[wtile[sg], sqbuf], writes=[psA])
                act(g1[:, ft:ft + 1], psA[0:128, 0:1], ACT.Gelu, reads=[psA, bias_sb],
                    writes=[g1], bias=bias_sb[:, ft:ft + 1])
            vop("tensor_copy", sqbuf[:, 8:12], reads=[g1], writes=[sqbuf],
                in_=g1[:, 0:4])
            wv2 = H["gc_w2"].rearrange("(kc p) f -> kc p f", p=128)
            for ft in range(2):
                sg = ft % 2
                b.dma_group("y", [(wstage[sg][:, kc * 128:kc * 128 + 128],
                                   wv2[kc][:, ft * 128:(ft + 1) * 128], [], [wstage[sg]])
                                  for kc in range(4)])
                cast_to("v", wtile[sg][:, 0:4 * 128], wstage[sg][:, 0:4 * 128],
                        reads=[wstage[sg]], writes=[wtile[sg]])
                items = [(wtile[sg][:, kc * 128:kc * 128 + 128], sqbuf[:, 8 + kc:9 + kc])
                         for kc in range(4)]
                mm_group(psA[0:128, 0:1], items, reads=[wtile[sg], sqbuf], writes=[psA])
                act(g2[:, ft:ft + 1], psA[0:128, 0:1], ACT.Gelu, reads=[psA, bias_sb],
                    writes=[g2], bias=bias_sb[:, 8 + ft:9 + ft])
            vop("tensor_copy", sqbuf[:, 12:14], reads=[g2], writes=[sqbuf],
                in_=g2[:, 0:2])
            wv3 = H["gc_w3"].rearrange("(kc p) f -> kc p f", p=128)
            sg = 0
            b.dma_group("y", [(wstage[sg][:, kc * 128:kc * 128 + 1],
                               wv3[kc][:, 0:1], [], [wstage[sg]])
                              for kc in range(2)])
            cast_to("v", wtile[sg][:, 0:2 * 128], wstage[sg][:, 0:2 * 128],
                    reads=[wstage[sg]], writes=[wtile[sg]])
            items = [(wtile[sg][:, kc * 128:kc * 128 + 1], sqbuf[:, 12 + kc:13 + kc])
                     for kc in range(2)]
            mm_group(psA[0:1, 0:1], items, reads=[wtile[sg], sqbuf], writes=[psA])
            def sig(eng):
                return eng.activation(out=gfv[0:1, 0:1], in_=psA[0:1, 0:1],
                                      func=ACT.Sigmoid,
                                      bias=bias_sb[0:1, 12:13])
            b.op("s", sig, reads=[psA, bias_sb], writes=[gfv])

            b.dma("y", arow[0:1, 0:1], H["ar"], writes=[arow])
            vop("tensor_scalar_mul", arow[0:1, 1:2], reads=[arow], writes=[arow],
                in0=arow[0:1, 0:1], scalar1=0.6)
            vop("tensor_mul", arow[0:1, 2:3], reads=[arow, gfv], writes=[arow],
                in0=arow[0:1, 0:1], in1=gfv[0:1, 0:1])
            vop("tensor_scalar", arow[0:1, 2:3], reads=[arow], writes=[arow],
                in0=arow[0:1, 2:3], scalar1=1.4, scalar2=0.1, op0=ALU.mult,
                op1=ALU.add)
            vop("tensor_sub", arow[0:1, 2:3], reads=[arow], writes=[arow],
                in0=arow[0:1, 2:3], in1=arow[0:1, 0:1])

            for nm, col, w in [("cn_b1", 16, 8), ("cn_b2", 24, 4), ("cn_b3", 28, 2)]:
                b.dma("y", bias_sb[:, col:col + w], H[nm][:, 0:w], writes=[bias_sb])
            b.dma("y", arow[0:1, 3:4], H["cn_b4"][0:1, 0:1], writes=[arow])

            pairs = [(0, 4, 1.0, -1.0), (1, 5, 1.0, -1.0),
                     (2, 6, -1.0, 1.0), (3, 7, -1.0, 1.0)]
            for half in range(2):
                spans = [2 * half, 2 * half + 1]
                col0 = half * TH
                emit_ln_apply(xs, of, spans=spans, col0=col0)
                for cyc in range(3):
                    for c in range(KC):
                        cast_to("v" if c % 2 == 0 else "s", ob[c][:, 0:TH], of[c][:],
                                reads=[of[c]], writes=[ob[c]])
                    def ev1(ft, sp, ps, half=half):
                        act(h1[ft][:, TH + (sp - 2 * half) * SPAN:TH + (sp - 2 * half + 1) * SPAN],
                            ps[0:128, 0:SPAN], ACT.Gelu, reads=[ps, bias_sb],
                            writes=[h1[ft]], bias=bias_sb[:, 16 + ft:17 + ft])
                    proj_fm(None, None, H["cn_w1"], 1024, ev1, spans=spans,
                            src_get=lambda kc, sp, half=half: ob[kc][:, (sp - 2 * half) * SPAN:(sp - 2 * half + 1) * SPAN],
                            src_reads=ob)
                    def ev2(ft, sp, ps, half=half):
                        act(h2[ft][:, (sp - 2 * half) * SPAN:(sp - 2 * half + 1) * SPAN],
                            ps[0:128, 0:SPAN], ACT.Gelu, reads=[ps, bias_sb],
                            writes=[h2[ft]], bias=bias_sb[:, 24 + ft:25 + ft])
                    proj_fm(None, None, H["cn_w2"], 512, ev2, spans=spans,
                            src_get=lambda kc, sp, half=half: h1[kc][:, TH + (sp - 2 * half) * SPAN:TH + (sp - 2 * half + 1) * SPAN],
                            src_reads=h1)
                    def ev3(ft, sp, ps, half=half):
                        act(h3[ft][:, (sp - 2 * half) * SPAN:(sp - 2 * half + 1) * SPAN],
                            ps[0:128, 0:SPAN], ACT.Gelu, reads=[ps, bias_sb],
                            writes=[h3[ft]], bias=bias_sb[:, 28 + ft:29 + ft])
                    proj_fm(None, None, H["cn_w3"], 256, ev3, spans=spans,
                            src_rows=4,
                            src_get=lambda kc, sp, half=half: h2[kc][:, (sp - 2 * half) * SPAN:(sp - 2 * half + 1) * SPAN],
                            src_reads=h2)
                    wv4 = H["cn_w4"].rearrange("(kc p) f -> kc p f", p=128)
                    sg4 = 0
                    b.dma_group("y", [(wstage[sg4][:, 1024 - 2 + kc:1024 - 1 + kc],
                                       wv4[kc][:, 0:1], [], [wstage[sg4]])
                                      for kc in range(2)])
                    cast_to("v", wtile[sg4][:, 1022:1024], wstage[sg4][:, 1022:1024],
                            reads=[wstage[sg4]], writes=[wtile[sg4]])
                    for sp in spans:
                        lsl = slice((sp - 2 * half) * SPAN, (sp - 2 * half + 1) * SPAN)
                        items = [(wtile[sg4][:, 1022 + kc:1023 + kc], h3[kc][:, lsl])
                                 for kc in range(2)]
                        mm_group(psP0[0:1, 0:SPAN], items,
                                 reads=[wtile[sg4]] + h3, writes=[psP0])
                        def sig4(eng, sp=sp):
                            return eng.activation(
                                out=lnrow3[0:1, sp * SPAN:(sp + 1) * SPAN],
                                in_=psP0[0:1, 0:SPAN], func=ACT.Sigmoid,
                                bias=arow[0:1, 3:4])
                        b.op("s", sig4, reads=[psP0, arow], writes=[lnrow3])
                        vop("tensor_scalar", lnrow3[0:1, sp * SPAN:(sp + 1) * SPAN],
                            reads=[lnrow3, arow], writes=[lnrow3],
                            in0=lnrow3[0:1, sp * SPAN:(sp + 1) * SPAN],
                            scalar1=arow[0:1, 1:2], scalar2=arow[0:1, 2:3],
                            op0=ALU.mult, op1=ALU.add)
                        bcast_row(psR2[0:128, 0:SPAN],
                                  lnrow3[0:1, sp * SPAN:(sp + 1) * SPAN], 128,
                                  reads=[lnrow3], base=64)
                        for (p, qq, s1, s2) in pairs:
                            vop("scalar_tensor_tensor", tw[0][:], reads=[of[qq], psR2],
                                writes=[tw[0]], in0=of[qq][:, lsl], scalar=s1,
                                in1=psR2[0:128, 0:SPAN], op0=ALU.mult, op1=ALU.mult)
                            vop("scalar_tensor_tensor", tw[1][:], reads=[of[p], psR2],
                                writes=[tw[1]], in0=of[p][:, lsl], scalar=s2,
                                in1=psR2[0:128, 0:SPAN], op0=ALU.mult, op1=ALU.mult)
                            vop("tensor_add", of[p][:, lsl], reads=[of[p], tw[0]],
                                writes=[of[p]], in0=of[p][:, lsl], in1=tw[0][:])
                            vop("tensor_add", of[qq][:, lsl], reads=[of[qq], tw[1]],
                                writes=[of[qq]], in0=of[qq][:, lsl], in1=tw[1][:])
                for c in range(KC):
                    gsl = slice(col0, col0 + TH)
                    vop("tensor_add", xs[c][:, gsl], reads=[xs[c], of[c]],
                        writes=[xs[c]], in0=xs[c][:, gsl], in1=of[c][:])

    def l0_attn_b(tag="b0"):
        csap = csign[:].unsqueeze(1).broadcast_to([128, SPAN // 4, 4])
        with b.scope() as sc:
            emit_layernorm(xs, xln)
            cc = b.sbuf_in(sc, f"{tag}_cc", [128, T], F32)
            scs = b.sbuf_in(sc, f"{tag}_scs", [128, T], F32)
            b.dma("gd", cc[:], H["c_ropecc"], writes=[cc])
            b.dma("gd", scs[:], H["c_ropesc"], writes=[scs])
            aof = [b.sbuf_in(sc, f"{tag}_ao{c}", [128, T], BF16) for c in range(KC)]
            t1 = b.sbuf_in(sc, f"{tag}_t1", [128, T], BF16)
            bsm = b.sbuf_in(sc, f"{tag}_bsm", [128, 2 * D], BF16)
            rows4 = b.sbuf_in(sc, f"{tag}_rows4", [128, T], F32)
            expA = rows4[0:1, :]
            expB = rows4[32:33, :]
            denA = rows4[64:65, :]
            denB = rows4[0:1, :]

            for nm, r0, c0 in [("bq", 0, 0), ("bo", 0, D), ("bv", 64, 0)]:
                b.dma("y", wstage[0][0:64, 0:D], H[f"l0b_{nm}"], writes=[wstage[0]])
                cast_to("v", bsm[r0:r0 + 64, c0:c0 + D], wstage[0][0:64, 0:D],
                        reads=[wstage[0]], writes=[bsm])

            for i, nm in enumerate(["aq", "av"]):
                wvv = H[f"l0b_{nm}"].rearrange("(kc p) f -> kc p f", p=128)
                b.dma_group("y", [(wstage[1][:, kc * 64:kc * 64 + 64], wvv[kc],
                                   [], [wstage[1]]) for kc in range(KC)])
                cast_to("v", wtile[1][:, 0:KC * 64], wstage[1][:, 0:KC * 64],
                        reads=[wstage[1]], writes=[wtile[1]])
                for sp in range(NSPAN):
                    sl = slice(sp * SPAN, (sp + 1) * SPAN)
                    items = [(wtile[1][:, kc * 64:kc * 64 + 64], xln[kc][:, sl])
                             for kc in range(KC)]
                    mm_group(psA[0:64, 0:SPAN], items, reads=[wtile[1]] + xln,
                             writes=[psA])
                    scalar_copy(t1[i * 64:(i + 1) * 64, sl], psA[0:64, 0:SPAN],
                                reads=[psA], writes=[t1])

            for hg in range(KC):
                hsl = slice(hg * 128, (hg + 1) * 128)
                with b.scope() as hs:
                    fullK = b.sbuf_in(hs, f"{tag}_fk{hg}", [128, SPAN], BF16)
                    fullV = b.sbuf_in(hs, f"{tag}_fv{hg}", [128, SPAN], BF16)
                    cpv = b.sbuf_in(hs, f"{tag}_cpv{hg}", [128, T], BF16)
                    fvt = b.sbuf_in(hs, f"{tag}_fvt{hg}", [128, 2 * 260], BF16)
                    pts = b.sbuf_in(hs, f"{tag}_pts{hg}", [128, 4 * SPAN], BF16)
                    cptmp = b.sbuf_in(hs, f"{tag}_cptmp{hg}", [64, SPAN], F32)
                    sh = sqbuf

                    def do_proj(wnm, dst, lora_i, rope):
                        def evq(ft, sp, ps, dst=dst):
                            act(dst[:, sp * SPAN:(sp + 1) * SPAN], ps[0:128, 0:SPAN],
                                ACT.Copy, reads=[ps], writes=[dst])
                        ex = None
                        if lora_i is not None:
                            def ex(ft, sp, li=lora_i):
                                r0 = 0 if li == 0 else 64
                                return [(bsm[r0:r0 + 64, hg * 128:(hg + 1) * 128],
                                         t1[r0:r0 + 64, sp * SPAN:(sp + 1) * SPAN],
                                         [bsm, t1])]
                        proj_fm(None, xln, H[f"l0b_{wnm}"][:, hsl], 128, evq,
                                extra_mm=ex)
                        if rope:
                            b.dma_group("y", [
                                (sh[0:32, :], dst[32:64, :], [dst], [sh]),
                                (sh[32:64, :], dst[0:32, :], [dst], [sh]),
                                (sh[64:96, :], dst[96:128, :], [dst], [sh]),
                                (sh[96:128, :], dst[64:96, :], [dst], [sh])])
                            vop("tensor_mul", dst[:], reads=[dst, cc], writes=[dst],
                                in0=dst[:], in1=cc[:])
                            vop("scalar_tensor_tensor", sh[:], reads=[sh, scs],
                                writes=[sh], in0=sh[:], scalar=1.0, in1=scs[:],
                                op0=ALU.mult, op1=ALU.mult)
                            vop("tensor_add", dst[:], reads=[dst, sh], writes=[dst],
                                in0=dst[:], in1=sh[:])
                        else:
                            def csm(eng, d=dst):
                                return eng.tensor_mul(
                                    out=d[:].rearrange("p (s f) -> p s f", f=4),
                                    in0=d[:].rearrange("p (s f) -> p s f", f=4),
                                    in1=csign[:].unsqueeze(1).broadcast_to([128, T // 4, 4]))
                            b.op("v", csm, reads=[dst, csign], writes=[dst])

                    def slot_sum(srcb, dstb):
                        v3 = srcb[:].rearrange("p (s f) -> p s f", f=4)
                        vop("tensor_add", dstb[:].rearrange("p (s o) -> p s o", o=1),
                            reads=[srcb], writes=[dstb], in0=v3[:, :, 0:1], in1=v3[:, :, 1:2])
                        vop("tensor_add", dstb[:].rearrange("p (s o) -> p s o", o=1),
                            reads=[srcb, dstb], writes=[dstb],
                            in0=dstb[:].rearrange("p (s o) -> p s o", o=1), in1=v3[:, :, 2:3])
                        vop("tensor_add", dstb[:].rearrange("p (s o) -> p s o", o=1),
                            reads=[srcb, dstb], writes=[dstb],
                            in0=dstb[:].rearrange("p (s o) -> p s o", o=1), in1=v3[:, :, 3:4])

                    def slot_cumsum(srcb, cpb):
                        vop("tensor_copy", cpb[:], reads=[srcb], writes=[cpb], in_=srcb[:])
                        c3 = cpb[:].rearrange("p (s f) -> p s f", f=4)
                        s3 = srcb[:].rearrange("p (s f) -> p s f", f=4)
                        for j in range(1, 4):
                            vop("tensor_add", c3[:, :, j:j + 1], reads=[srcb, cpb],
                                writes=[cpb], in0=c3[:, :, j - 1:j], in1=s3[:, :, j:j + 1])

                    with b.scope() as vsc:
                        vs = b.sbuf_in(vsc, f"{tag}_vs{hg}", [128, T], BF16)
                        do_proj("wv", vs, 1, False)
                        slot_sum(vs, fullV)
                        slot_cumsum(vs, cpv)
                        for hh in range(2):
                            hr = slice(hh * 64, hh * 64 + 64)
                            for cs_ in range(4):
                                transpose(psT[0:128, 0:64], fullV[hr, cs_ * 128:(cs_ + 1) * 128],
                                          ident_bf[hr, hh * 64:hh * 64 + 64],
                                          reads=[fullV, ident_bf], writes=[psT])
                                col = hh * 260 + cs_ * 65
                                scalar_copy(fvt[:, col:col + 64], psT[0:128, 0:64],
                                            reads=[psT], writes=[fvt])
                                vop("memset", fvt[:, col + 64:col + 65], reads=[],
                                    writes=[fvt], constant=1.0)
                    aq = b.sbuf_in(hs, f"{tag}_aq{hg}", [128, T], BF16)
                    ks = aof[hg]
                    m2 = pts
                    do_proj("wq", aq, 0, True)
                    do_proj("wk", ks, None, True)
                    slot_sum(ks, fullK)
                    vop("tensor_copy", m2[:], reads=[ks], writes=[m2], in_=ks[:])
                    c3 = m2[:].rearrange("p (s f) -> p s f", f=4)
                    s3 = ks[:].rearrange("p (s f) -> p s f", f=4)
                    for j in range(1, 4):
                        vop("tensor_add", c3[:, :, j:j + 1], reads=[ks, m2],
                            writes=[m2], in0=c3[:, :, j - 1:j], in1=s3[:, :, j:j + 1])
                    vop("tensor_mul", sh[:], reads=[aq, m2], writes=[sh],
                        in0=aq[:], in1=m2[:])
                    for hh, exr in [(0, expA), (1, expB)]:
                        hr = slice(hh * 64, hh * 64 + 64)
                        for sp in range(NSPAN):
                            sl = slice(sp * SPAN, (sp + 1) * SPAN)
                            mm_group(psR2[0:1, 0:SPAN],
                                     [(ones_col[hr, :], sh[hr, sl])],
                                     reads=[ones_col, sh], writes=[psR2])
                            act(exr[0:1, sl], psR2[0:1, 0:SPAN], ACT.Exp,
                                reads=[psR2], writes=[exr], scale=1.0 / 16.0)
                    for hh, exr, dnr in [(0, expA, denA), (1, expB, denB)]:
                        hr = slice(hh * 64, hh * 64 + 64)
                        for bb in range(NSPAN):
                            bsl = slice(bb * SPAN, (bb + 1) * SPAN)
                            for cc0 in range(0, bb + 1, 2):
                                nj = min(2, bb + 1 - cc0)
                                ps = psS01 if (cc0 // 2) % 2 == 0 else psS23
                                def qkf(eng, cc0=cc0, bsl=bsl, ps=ps, hr=hr, nj=nj,
                                        fullK=fullK, aq=aq):
                                    i1 = None
                                    for j in range(nj):
                                        i1 = eng.matmul(
                                            out=ps[0:128, j * SPAN:(j + 1) * SPAN],
                                            lhsT=fullK[hr, (cc0 + j) * 128:(cc0 + j + 1) * 128],
                                            rhs=aq[hr, bsl], start=True, stop=True)
                                    return i1
                                b.op("t", qkf, reads=[fullK, aq], writes=[ps])
                                for j in range(nj):
                                    if cc0 + j == bb:
                                        vop("tensor_add", ps[0:128, j * SPAN:(j + 1) * SPAN],
                                            reads=[ps, slotmask], writes=[ps],
                                            in0=ps[0:128, j * SPAN:(j + 1) * SPAN],
                                            in1=slotmask[:])
                                act(pts[:, cc0 * SPAN:(cc0 + nj) * SPAN],
                                    ps[0:128, 0:nj * SPAN],
                                    ACT.Exp, reads=[ps], writes=[pts], scale=1.0 / 16.0)
                            items = [(fvt[:, hh * 260 + ccn * 65:hh * 260 + ccn * 65 + 65],
                                      pts[:, ccn * SPAN:(ccn + 1) * SPAN])
                                     for ccn in range(bb + 1)]
                            mm_group(psP0[0:65, 0:SPAN], items, reads=[fvt, pts],
                                     writes=[psP0])
                            bcast_row(psR[0:64, 0:SPAN], exr[0:1, bsl], 64,
                                      reads=[exr], base=(0 if hh == 0 else 32))
                            vop("tensor_mul", cptmp[:], reads=[cpv, psR], writes=[cptmp],
                                in0=cpv[hr, bsl], in1=psR[0:64, 0:SPAN])
                            vop("tensor_add", psP0[0:64, 0:SPAN], reads=[psP0, cptmp],
                                writes=[psP0], in0=psP0[0:64, 0:SPAN], in1=cptmp[:])
                            vop("tensor_add", psP0[64:65, 0:SPAN], reads=[psP0, exr],
                                writes=[psP0], in0=psP0[64:65, 0:SPAN], in1=exr[0:1, bsl])
                            scalar_copy(aof[hg][hr, bsl], psP0[0:64, 0:SPAN],
                                        reads=[psP0], writes=[aof[hg]])
                            scalar_copy(dnr[0:1, bsl], psP0[64:65, 0:SPAN],
                                        reads=[psP0], writes=[dnr])
                    vop("reciprocal", rows4[0:65, :], reads=[rows4], writes=[rows4],
                        in_=rows4[0:65, :])
                    for sp in range(NSPAN):
                        sl = slice(sp * SPAN, (sp + 1) * SPAN)
                        def f0(eng, sl=sl):
                            return eng.matmul(out=psR[0:64, 0:SPAN],
                                              lhsT=allones[64:65, 0:64],
                                              rhs=denA[0:1, sl], start=True, stop=True)
                        b.op("t", f0, reads=[allones, denA], writes=[psR])
                        def f1(eng, sl=sl):
                            return eng.matmul(out=psR[64:128, 0:SPAN],
                                              lhsT=allones[0:1, 0:64],
                                              rhs=denB[0:1, sl], start=True, stop=True)
                        b.op("t", f1, reads=[allones, denB], writes=[psR])
                        vop("tensor_mul", aof[hg][:, sl], reads=[aof[hg], psR],
                            writes=[aof[hg]], in0=aof[hg][:, sl], in1=psR[0:128, 0:SPAN])
                        def csm2(eng, sl=sl, hg=hg):
                            return eng.tensor_mul(
                                out=aof[hg][:, sl].rearrange("p (s f) -> p s f", f=4),
                                in0=aof[hg][:, sl].rearrange("p (s f) -> p s f", f=4),
                                in1=csap)
                        b.op("v", csm2, reads=[aof[hg], csign], writes=[aof[hg]])
            wvv = H["l0b_ao"].rearrange("(kc p) f -> kc p f", p=128)
            b.dma_group("y", [(wstage[1][:, kc * 64:kc * 64 + 64], wvv[kc],
                               [], [wstage[1]]) for kc in range(KC)])
            cast_to("v", wtile[1][:, 0:KC * 64], wstage[1][:, 0:KC * 64],
                    reads=[wstage[1]], writes=[wtile[1]])
            for sp in range(NSPAN):
                sl = slice(sp * SPAN, (sp + 1) * SPAN)
                items = [(wtile[1][:, kc * 64:kc * 64 + 64], aof[kc][:, sl])
                         for kc in range(KC)]
                mm_group(psA[0:64, 0:SPAN], items, reads=[wtile[1]] + aof, writes=[psA])
                scalar_copy(t1[0:64, sl], psA[0:64, 0:SPAN], reads=[psA], writes=[t1])
            def evr(ft, sp, ps):
                sl = slice(sp * SPAN, (sp + 1) * SPAN)
                vop("tensor_add", xs[ft][:, sl], reads=[xs[ft], ps], writes=[xs[ft]],
                    in0=xs[ft][:, sl], in1=ps[0:128, 0:SPAN])
            def exo(ft, sp):
                return [(bsm[0:64, D + ft * 128:D + (ft + 1) * 128],
                         t1[0:64, sp * SPAN:(sp + 1) * SPAN], [bsm, t1])]
            proj_fm(None, aof, H["l0b_wo"], D, evr, extra_mm=exo)

    def l1_ffn(tag):
        with b.scope() as sc:
            emit_layernorm(xs, xln)
            musb = b.sbuf_in(sc, f"{tag}_mu", [128, 32], F32)
            b.dma("y", musb[:, 0:8], H["l1f_mur"][:, 0:8], writes=[musb])
            b.dma("y", musb[:, 8:16], H["l1f_muk"][:, 0:8], writes=[musb])
            vop("tensor_scalar", musb[:, 16:32], reads=[musb], writes=[musb],
                in0=musb[:, 0:16], scalar1=-1.0, scalar2=1.0, op0=ALU.mult, op1=ALU.add)
            r = [b.sbuf_in(sc, f"{tag}_r{c}", [128, T], BF16) for c in range(KC)]

            def mix(dst, mucol):
                for c in range(KC):
                    vop("tensor_scalar_mul", dst[c][:], reads=[xln[c], musb],
                        writes=[dst[c]], in0=xln[c][:], scalar1=musb[:, mucol + c:mucol + c + 1])
                    vop("scalar_tensor_tensor", dst[c][:, 1:T], reads=[xln[c], musb, dst[c]],
                        writes=[dst[c]], in0=xln[c][:, 0:T - 1],
                        scalar=musb[:, 16 + mucol + c:17 + mucol + c],
                        in1=dst[c][:, 1:T], op0=ALU.mult, op1=ALU.add)

            with b.scope() as sa:
                xr = [b.sbuf_in(sa, f"{tag}_xr{c}", [128, T], BF16) for c in range(KC)]
                mix(xr, 0)
                def evr(ft, sp, ps):
                    act(r[ft][:, sp * SPAN:(sp + 1) * SPAN], ps[0:128, 0:SPAN],
                        ACT.Sigmoid, reads=[ps], writes=[r[ft]])
                proj_fm(None, xr, H["l1f_wr"], D, evr)
            with b.scope() as sb_:
                xk = [b.sbuf_in(sb_, f"{tag}_xk{c}", [128, T], BF16) for c in range(KC)]
                mix(xk, 8)
                ka = xln
                def evk(ft, sp, ps):
                    act(ka[ft][:, sp * SPAN:(sp + 1) * SPAN], ps[0:128, 0:SPAN],
                        ACT.Relu, reads=[ps], writes=[ka[ft]])
                proj_fm(None, xk, H["l1f_wk"], D, evk)
            for c in range(KC):
                def sqk(eng, c=c):
                    return eng.activation(out=xln[c][:], in_=xln[c][:], func=ACT.Square)
                b.op("s", sqk, reads=[xln[c]], writes=[xln[c]])
            tws = b.sbuf_in(sc, f"{tag}_tws", [128, SPAN], F32)
            def evv(ft, sp, ps):
                sl = slice(sp * SPAN, (sp + 1) * SPAN)
                vop("tensor_mul", tws[:], reads=[r[ft], ps], writes=[tws],
                    in0=r[ft][:, sl], in1=ps[0:128, 0:SPAN])
                vop("tensor_add", xs[ft][:, sl], reads=[xs[ft], tws], writes=[xs[ft]],
                    in0=xs[ft][:, sl], in1=tws[:])
            proj_fm(None, xln, H["l1f_wv"], D, evv)

    def l1_attn_a(tag="p1"):
        M = 266
        DN = float(64.0 ** -0.25)
        RATIO = float(266.0 ** -0.5)
        EPS = 1e-4
        with b.scope() as sc:
            emit_layernorm(xs, xln)
            aof = [b.sbuf_in(sc, f"{tag}_ao{c}", [128, T], BF16) for c in range(KC)]
            pj2 = b.sbuf_in(sc, f"{tag}_pj2", [128, M], BF16)
            b.dma("y", wstage[0][0:64, 0:M], H["l1a_projT"], writes=[wstage[0]])
            cast_to("v", pj2[0:64, 0:M], wstage[0][0:64, 0:M], reads=[wstage[0]], writes=[pj2])
            cast_to("v", pj2[64:128, 0:M], wstage[0][0:64, 0:M], reads=[wstage[0]], writes=[pj2])
            for nm, col in [("bq", 0), ("bk", 8), ("bv", 16), ("bo", 24), ("bfc", 32)]:
                b.dma("y", bias_sb[:, col:col + 8], H[f"l1a_{nm}"][:, 0:8], writes=[bias_sb])
            vop("tensor_scalar_mul", bias_sb[:, 0:16], reads=[bias_sb], writes=[bias_sb],
                in0=bias_sb[:, 0:16], scalar1=DN)
            denP2 = b.sbuf_in(sc, f"{tag}_den", [64, T], F32)
            dnA = denP2[0:1, :]
            dnB = denP2[32:33, :]
            bqrow = b.sbuf_in(sc, f"{tag}_bqrow", [1, T], F32)
            for hg in range(KC):
                hsl = slice(hg * 128, (hg + 1) * 128)
                with b.scope() as hs:
                    qb = b.sbuf_in(hs, f"{tag}_qb{hg}", [128, T], BF16)
                    kb = b.sbuf_in(hs, f"{tag}_kb{hg}", [128, T], BF16)
                    va = [b.sbuf_in(hs, f"{tag}_va{hg}_{t}", [128, 130], BF16)
                          for t in range(16)]
                    kp = b.sbuf_in(hs, f"{tag}_kp{hg}", [128, 16 * M], BF16)
                    qpt0 = b.sbuf_in(hs, f"{tag}_qpt0{hg}", [128, T], BF16)
                    qpt1 = b.sbuf_in(hs, f"{tag}_qpt1{hg}", [128, T], BF16)
                    qpt2 = b.sbuf_in(hs, f"{tag}_qpt2{hg}", [10, T], BF16)
                    stat = b.sbuf_in(hs, f"{tag}_st{hg}", [128, 64], F32)
                    trow = b.sbuf_in(hs, f"{tag}_tr{hg}", [16, 128], F32)
                    ctx = b.sbuf_in(hs, f"{tag}_ctx{hg}", [65, M], BF16)
                    ctxa = [b.sbuf_in(hs, f"{tag}_cx{hg}_{i}", [128, 65], BF16)
                            for i in range(3)]
                    def evq(ft, sp, ps, dst=qb, bc=0):
                        act(dst[:, sp * SPAN:(sp + 1) * SPAN], ps[0:128, 0:SPAN],
                            ACT.Copy, reads=[ps, bias_sb], writes=[dst], scale=DN,
                            bias=bias_sb[:, bc + hg:bc + hg + 1])
                    def evk_(ft, sp, ps, dst=kb, bc=8):
                        act(dst[:, sp * SPAN:(sp + 1) * SPAN], ps[0:128, 0:SPAN],
                            ACT.Copy, reads=[ps, bias_sb], writes=[dst], scale=DN,
                            bias=bias_sb[:, bc + hg:bc + hg + 1])
                    proj_fm(None, xln, H["l1a_wq"][:, hsl], 128, evq)
                    proj_fm(None, xln, H["l1a_wk"][:, hsl], 128, evk_)
                    sg = hg % 2
                    wvv = H["l1a_wv"].rearrange("(kc p) f -> kc p f", p=128)
                    b.dma_group("y", [(wstage[sg][:, kc * 128:kc * 128 + 128],
                                       wvv[kc][:, hsl], [], [wstage[sg]])
                                      for kc in range(KC)])
                    cast_to("v", wtile[sg][:, 0:KC * 128], wstage[sg][:, 0:KC * 128],
                            reads=[wstage[sg]], writes=[wtile[sg]])
                    transpose(psA[0:1, 0:128], bias_sb[:, 16 + hg:17 + hg],
                              ident_f[:], reads=[bias_sb, ident_f], writes=[psA])
                    scalar_copy(bqrow[0:1, 0:128], psA[0:1, 0:128], reads=[psA],
                                writes=[bqrow])
                    for tt in range(16):
                        ps = psB if tt % 2 == 0 else psP0
                        items = [(xln[kc][:, tt * 128:tt * 128 + 128],
                                  wtile[sg][:, kc * 128:kc * 128 + 128]) for kc in range(KC)]
                        items.append((allones[0:1, 0:128], bqrow[0:1, 0:128]))
                        mm_group(ps[0:128, 0:128], items,
                                 reads=[wtile[sg], bqrow] + list(xln), writes=[ps])
                        scalar_copy(va[tt][:, 0:64], ps[0:128, 0:64], reads=[ps], writes=[va[tt]])
                        scalar_copy(va[tt][:, 65:129], ps[0:128, 64:128], reads=[ps], writes=[va[tt]])
                        vop("memset", va[tt][:, 64:65], reads=[], writes=[va[tt]], constant=1.0)
                        vop("memset", va[tt][:, 129:130], reads=[], writes=[va[tt]], constant=1.0)
                    for hh in range(2):
                        h0 = hh * 64
                        hr = slice(h0, h0 + 64)
                        def sqq(eng, s=qb):
                            return eng.activation(out=sqbuf[:], in_=s[:], func=ACT.Square)
                        b.op("s", sqq, reads=[qb], writes=[sqbuf])
                        for nt in range(16):
                            mm_group(psS0[0:128, 0:1],
                                     [(sqbuf[hr, nt * 128:(nt + 1) * 128], ones_col[hr, :])],
                                     reads=[sqbuf, ones_col], writes=[psS0])
                            scalar_copy(stat[:, nt:nt + 1], psS0[0:128, 0:1],
                                        reads=[psS0], writes=[stat])
                            mm_group(psS1[0:128, 0:M],
                                     [(qb[hr, nt * 128:(nt + 1) * 128], pj2[hr, :])],
                                     reads=[qb, pj2], writes=[psS1])
                            vop("reduce_max", stat[:, 16 + nt:17 + nt], reads=[psS1],
                                writes=[stat], in_=psS1[0:128, 0:M], axis=AX)
                        def sqk2(eng, s=kb):
                            return eng.activation(out=sqbuf[:], in_=s[:], func=ACT.Square)
                        b.op("s", sqk2, reads=[kb], writes=[sqbuf])
                        for nt in range(16):
                            mm_group(psS0[0:128, 0:1],
                                     [(sqbuf[hr, nt * 128:(nt + 1) * 128], ones_col[hr, :])],
                                     reads=[sqbuf, ones_col], writes=[psS0])
                            scalar_copy(stat[:, 32 + nt:33 + nt], psS0[0:128, 0:1],
                                        reads=[psS0], writes=[stat])
                            mm_group(psS1[0:128, 0:M],
                                     [(kb[hr, nt * 128:(nt + 1) * 128], pj2[hr, :])],
                                     reads=[kb, pj2], writes=[psS1])
                            vop("reduce_max", stat[:, 48 + nt:49 + nt], reads=[psS1],
                                writes=[stat], in_=psS1[0:128, 0:M], axis=AX)
                        vop("reduce_max", stat[:, 48:49], reads=[stat], writes=[stat],
                            in_=stat[:, 48:64], axis=AX)
                        transpose(psA[0:1, 0:128], stat[:, 48:49], ident_f[:],
                                  reads=[stat, ident_f], writes=[psA])
                        vop("reduce_max", psA[0:1, 0:1], reads=[psA], writes=[psA],
                            in_=psA[0:1, 0:128], axis=AX)
                        scalar_copy(bqrow[0:1, 256:257], psA[0:1, 0:1], reads=[psA],
                                    writes=[bqrow])
                        def mxb(eng):
                            return eng.matmul(out=psR2[0:128, 0:1], lhsT=ones_row[:],
                                              rhs=bqrow[0:1, 256:257], start=True, stop=True)
                        b.op("t", mxb, reads=[ones_row, bqrow], writes=[psR2])
                        vop("scalar_tensor_tensor", stat[:, 32:48], reads=[stat, psR2],
                            writes=[stat], in0=stat[:, 32:48], scalar=-0.5,
                            in1=psR2[0:128, 0:1].broadcast_to([128, 16]),
                            op0=ALU.mult, op1=ALU.subtract)
                        vop("tensor_scalar_mul", stat[:, 0:16], reads=[stat], writes=[stat],
                            in0=stat[:, 0:16], scalar1=0.5)
                        vop("tensor_add", stat[:, 0:16], reads=[stat], writes=[stat],
                            in0=stat[:, 0:16], in1=stat[:, 16:32])
                        vop("tensor_scalar_mul", stat[:, 0:16], reads=[stat], writes=[stat],
                            in0=stat[:, 0:16], scalar1=-1.0)
                        for nt in range(16):
                            ps = psS0 if nt % 2 == 0 else psS1
                            mm_group(ps[0:128, 0:M],
                                     [(kb[hr, nt * 128:(nt + 1) * 128], pj2[hr, :])],
                                     reads=[kb, pj2], writes=[ps])
                            act(kp[:, nt * M:(nt + 1) * M], ps[0:128, 0:M], ACT.Exp,
                                reads=[ps, stat], writes=[kp],
                                bias=stat[:, 32 + nt:33 + nt])
                        vop("tensor_scalar", kp[:], reads=[kp], writes=[kp],
                            in0=kp[:], scalar1=RATIO, scalar2=RATIO * EPS,
                            op0=ALU.mult, op1=ALU.add)
                        transpose(psA[0:16, 0:128], stat[:, 0:16], ident_f[:],
                                  reads=[stat, ident_f], writes=[psA])
                        scalar_copy(trow[0:16, :], psA[0:16, 0:128], reads=[psA],
                                    writes=[trow])
                        b.dma_group("y", [(bqrow[0:1, nt * 128:(nt + 1) * 128],
                                           trow[nt:nt + 1, 0:128], [trow], [bqrow])
                                          for nt in range(16)])
                        qpt = [qpt0, qpt1, qpt2]
                        for mc in range(3):
                            mw = 128 if mc < 2 else 10
                            for sp in range(NSPAN):
                                sl = slice(sp * SPAN, (sp + 1) * SPAN)
                                ps = psS0 if sp % 2 == 0 else psS1
                                items = [(pj2[hr, mc * 128:mc * 128 + mw], qb[hr, sl]),
                                         (allones[0:1, 0:mw], bqrow[0:1, sl])]
                                mm_group(ps[0:mw, 0:SPAN], items,
                                         reads=[pj2, qb, allones, bqrow], writes=[ps])
                                act(qpt[mc][0:mw, sl], ps[0:mw, 0:SPAN], ACT.Exp,
                                    reads=[ps], writes=[qpt[mc]])
                            vop("tensor_scalar", qpt[mc][0:mw, :], reads=[qpt[mc]],
                                writes=[qpt[mc]], in0=qpt[mc][0:mw, :], scalar1=RATIO,
                                scalar2=RATIO * EPS, op0=ALU.mult, op1=ALU.add)
                        items = [(va[nt][:, hh * 65:hh * 65 + 65], kp[:, nt * M:(nt + 1) * M])
                                 for nt in range(16)]
                        mm_group(psP0[0:65, 0:M], items,
                                 reads=[kp] + va, writes=[psP0])
                        scalar_copy(ctx[0:65, 0:M], psP0[0:65, 0:M], reads=[psP0], writes=[ctx])
                        for mc in range(3):
                            mw = 128 if mc < 2 else 10
                            transpose(psT[0:mw, 0:65], ctx[0:65, mc * 128:mc * 128 + mw],
                                      ident_bf[0:65, 0:65], reads=[ctx, ident_bf], writes=[psT])
                            scalar_copy(ctxa[mc][0:mw, :], psT[0:mw, 0:65],
                                        reads=[psT], writes=[ctxa[mc]])
                        dn_r = dnA if hh == 0 else dnB
                        for sp in range(NSPAN):
                            sl = slice(sp * SPAN, (sp + 1) * SPAN)
                            items = [(ctxa[mc][0:(128 if mc < 2 else 10), :],
                                      qpt[mc][0:(128 if mc < 2 else 10), sl])
                                     for mc in range(3)]
                            mm_group(psP0[0:65, 0:SPAN], items,
                                     reads=[qpt0, qpt1, qpt2] + ctxa, writes=[psP0])
                            scalar_copy(aof[hg][hr, sl], psP0[0:64, 0:SPAN],
                                        reads=[psP0], writes=[aof[hg]])
                            scalar_copy(dn_r[0:1, sl], psP0[64:65, 0:SPAN],
                                        reads=[psP0], writes=[denP2])
                    vop("reciprocal", denP2[0:33, :], reads=[denP2], writes=[denP2],
                        in_=denP2[0:33, :])
                    for sp in range(NSPAN):
                        sl = slice(sp * SPAN, (sp + 1) * SPAN)
                        def g0(eng, sl=sl):
                            return eng.matmul(out=psR[0:64, 0:SPAN], lhsT=allones[0:1, 0:64],
                                              rhs=dnA[0:1, sl], start=True, stop=True)
                        b.op("t", g0, reads=[allones, denP2], writes=[psR])
                        def g1(eng, sl=sl):
                            return eng.matmul(out=psR[64:128, 0:SPAN], lhsT=allones[32:33, 0:64],
                                              rhs=dnB[0:1, sl], start=True, stop=True)
                        b.op("t", g1, reads=[allones, denP2], writes=[psR])
                        vop("tensor_mul", aof[hg][:, sl], reads=[aof[hg], psR],
                            writes=[aof[hg]], in0=aof[hg][:, sl], in1=psR[0:128, 0:SPAN])
            def evo(ft, sp, ps):
                act(xln[ft][:, sp * SPAN:(sp + 1) * SPAN], ps[0:128, 0:SPAN],
                    ACT.Copy, reads=[ps, bias_sb], writes=[xln[ft]],
                    bias=bias_sb[:, 24 + ft:25 + ft])
            proj_fm(None, aof, H["l1a_wo"], D, evo)
            tws2 = b.sbuf_in(sc, f"{tag}_tws", [128, SPAN], BF16)
            def evf(ft, sp, ps):
                sl = slice(sp * SPAN, (sp + 1) * SPAN)
                act(tws2[:], ps[0:128, 0:SPAN], ACT.Relu, reads=[ps, bias_sb],
                    writes=[tws2], bias=bias_sb[:, 32 + ft:33 + ft])
                vop("tensor_add", xs[ft][:, sl], reads=[xs[ft], tws2], writes=[xs[ft]],
                    in0=xs[ft][:, sl], in1=tws2[:])
            proj_fm(None, xln, H["l1a_wfc"], D, evf)

    def l1_attn_b(tag="q1"):
        xsv = xspill.rearrange("(c p) t -> c p t", p=128)
        with b.scope() as sc:
            b.dma_group("gd", [(xsv[c], xs[c][:], [xs[c]], []) for c in range(KC)])
            emit_layernorm(xs, xs)
            trimask = b.sbuf_in(sc, f"{tag}_tm", [128, 4 * SPAN], BF16)
            b.dma("gd", trimask[:], H["c_trimask"], writes=[trimask])
            gam = b.sbuf_in(sc, f"{tag}_gam", [128, 48], F32)
            for i, g in enumerate(["g1", "b1", "g2", "b2", "g3", "b3"]):
                b.dma("y", gam[:, i * 8:(i + 1) * 8], H[f"l1b_{g}"][:, 0:8], writes=[gam])

            def gamma_of(i):
                return ((gam, lambda c, i=i: 2 * i * 8 + c),
                        (gam, lambda c, i=i: (2 * i + 1) * 8 + c))

            aof = [b.sbuf_in(sc, f"{tag}_ao{c}", [128, T], BF16) for c in range(KC)]
            tws3 = b.sbuf_in(sc, f"{tag}_tws", [128, SPAN], BF16)

            g1, b1_ = gamma_of(0)
            emit_layernorm(xs, xln, gamma=g1, beta=b1_)
            b.dma("y", bias_sb[:, 0:8], H["l1b_sfb"][:, 0:8], writes=[bias_sb])
            sfsc = b.scope()
            sfsc.__enter__()
            mstage = b.sbuf_in(sfsc, f"{tag}_mst", [128, 1024], F32)
            wvv = H["l1b_sfw"].rearrange("(kc p) f -> kc p f", p=128)
            mvv = H["l1b_mask"].rearrange("(kc p) f -> kc p f", p=128)
            for ft in range(KC):
                sg = ft % 2
                b.dma_group("y", [(wstage[sg][:, kc * 128:kc * 128 + 128],
                                   wvv[kc][:, ft * 128:(ft + 1) * 128], [], [wstage[sg]])
                                  for kc in range(KC)])
                b.dma_group("y", [(mstage[:, kc * 128:kc * 128 + 128],
                                   mvv[kc][:, ft * 128:(ft + 1) * 128], [], [mstage])
                                  for kc in range(KC)])
                vop("tensor_mul", wstage[sg][:], reads=[wstage[sg], mstage],
                    writes=[wstage[sg]], in0=wstage[sg][:], in1=mstage[:])
                cast_to("v", wtile[sg][:, 0:KC * 128], wstage[sg][:, 0:KC * 128],
                        reads=[wstage[sg]], writes=[wtile[sg]])
                for sp in range(NSPAN):
                    ps = psA if sp % 2 == 0 else psB
                    sl = slice(sp * SPAN, (sp + 1) * SPAN)
                    items = [(wtile[sg][:, kc * 128:kc * 128 + 128], xln[kc][:, sl])
                             for kc in range(KC)]
                    mm_group(ps[0:128, 0:SPAN], items, reads=[wtile[sg]] + xln, writes=[ps])
                    act(tws3[:], ps[0:128, 0:SPAN], ACT.Copy, reads=[ps, bias_sb],
                        writes=[tws3], bias=bias_sb[:, ft:ft + 1])
                    vop("tensor_add", xs[ft][:, sl], reads=[xs[ft], tws3], writes=[xs[ft]],
                        in0=xs[ft][:, sl], in1=tws3[:])

            sfsc.__exit__(None, None, None)
            g2, b2_ = gamma_of(1)
            emit_layernorm(xs, xln, gamma=g2, beta=b2_)
            for nm, col in [("bq", 8), ("bk", 16), ("bv", 24), ("bo", 32)]:
                b.dma("y", bias_sb[:, col:col + 8], H[f"l1b_{nm}"][:, 0:8], writes=[bias_sb])
            vop("tensor_scalar_mul", bias_sb[:, 8:16], reads=[bias_sb], writes=[bias_sb],
                in0=bias_sb[:, 8:16], scalar1=0.125)
            atsc = b.scope()
            atsc.__enter__()
            qh2 = [b.sbuf_in(atsc, f"{tag}_q{i}", [128, T], BF16) for i in range(2)]
            kh2 = [b.sbuf_in(atsc, f"{tag}_k{i}", [128, T], BF16) for i in range(2)]
            va_s = [b.sbuf_in(atsc, f"{tag}_va_{t}", [128, 130], BF16) for t in range(16)]
            pt_s = b.sbuf_in(atsc, f"{tag}_pt", [128, 16 * SPAN], BF16)
            denQ_s = b.sbuf_in(atsc, f"{tag}_den", [64, T], F32)
            for hg in range(KC):
                hsl = slice(hg * 128, (hg + 1) * 128)
                if True:
                    qh, kh, va, pt = qh2[hg % 2], kh2[hg % 2], va_s, pt_s
                    denP3 = denQ_s
                    dn0 = denP3[0:1, :]
                    dn1 = denP3[32:33, :]
                    def evq(ft, sp, ps):
                        act(qh[:, sp * SPAN:(sp + 1) * SPAN], ps[0:128, 0:SPAN],
                            ACT.Copy, reads=[ps, bias_sb], writes=[qh], scale=0.125,
                            bias=bias_sb[:, 8 + hg:9 + hg])
                    def evk(ft, sp, ps):
                        act(kh[:, sp * SPAN:(sp + 1) * SPAN], ps[0:128, 0:SPAN],
                            ACT.Copy, reads=[ps, bias_sb], writes=[kh],
                            bias=bias_sb[:, 16 + hg:17 + hg])
                    proj_fm(None, xln, H["l1b_wq"][:, hsl], 128, evq)
                    proj_fm(None, xln, H["l1b_wk"][:, hsl], 128, evk)
                    sg = hg % 2
                    wvv2 = H["l1b_wv"].rearrange("(kc p) f -> kc p f", p=128)
                    b.dma_group("y", [(wstage[sg][:, kc * 128:kc * 128 + 128],
                                       wvv2[kc][:, hsl], [], [wstage[sg]])
                                      for kc in range(KC)])
                    cast_to("v", wtile[sg][:, 0:KC * 128], wstage[sg][:, 0:KC * 128],
                            reads=[wstage[sg]], writes=[wtile[sg]])
                    transpose(psA[0:1, 0:128], bias_sb[:, 24 + hg:25 + hg],
                              ident_f[:], reads=[bias_sb, ident_f], writes=[psA])
                    scalar_copy(dn0[0:1, 0:128], psA[0:1, 0:128], reads=[psA],
                                writes=[denP3])
                    for tt in range(16):
                        ps = psB if tt % 2 == 0 else psP0
                        items = [(xln[kc][:, tt * 128:tt * 128 + 128],
                                  wtile[sg][:, kc * 128:kc * 128 + 128]) for kc in range(KC)]
                        items.append((allones[0:1, 0:128], dn0[0:1, 0:128]))
                        mm_group(ps[0:128, 0:128], items,
                                 reads=[wtile[sg], denP3] + list(xln), writes=[ps])
                        scalar_copy(va[tt][:, 0:64], ps[0:128, 0:64], reads=[ps], writes=[va[tt]])
                        scalar_copy(va[tt][:, 65:129], ps[0:128, 64:128], reads=[ps], writes=[va[tt]])
                        vop("memset", va[tt][:, 64:65], reads=[], writes=[va[tt]], constant=1.0)
                        vop("memset", va[tt][:, 129:130], reads=[], writes=[va[tt]], constant=1.0)
                    for hh in range(2):
                        hr = slice(hh * 64, hh * 64 + 64)
                        dnr = dn0 if hh == 0 else dn1
                        for bb in range(NSPAN):
                            bsl = slice(bb * SPAN, (bb + 1) * SPAN)
                            nch = 4 * bb + 4
                            for cc0 in range(0, nch, 2):
                                ps = psS01 if (cc0 // 2) % 2 == 0 else psS23
                                def qkf(eng, cc0=cc0, bsl=bsl, ps=ps, hr=hr,
                                        kh=kh, qh=qh):
                                    i1 = eng.matmul(out=ps[0:128, 0:SPAN],
                                                    lhsT=kh[hr, cc0 * 128:(cc0 + 1) * 128],
                                                    rhs=qh[hr, bsl], start=True, stop=True)
                                    i1 = eng.matmul(out=ps[0:128, SPAN:2 * SPAN],
                                                    lhsT=kh[hr, (cc0 + 1) * 128:(cc0 + 2) * 128],
                                                    rhs=qh[hr, bsl], start=True, stop=True)
                                    return i1
                                b.op("t", qkf, reads=[kh, qh], writes=[ps])
                                for j in range(2):
                                    u = cc0 + j - 4 * bb
                                    if 0 <= u <= 3:
                                        vop("tensor_add", ps[0:128, j * SPAN:(j + 1) * SPAN],
                                            reads=[ps, trimask], writes=[ps],
                                            in0=ps[0:128, j * SPAN:(j + 1) * SPAN],
                                            in1=trimask[:, u * SPAN:(u + 1) * SPAN])
                                act(pt[:, cc0 * SPAN:(cc0 + 2) * SPAN], ps[0:128, 0:2 * SPAN],
                                    ACT.Relu, reads=[ps], writes=[pt])
                            items = [(va[ccn][:, hh * 65:hh * 65 + 65],
                                      pt[:, ccn * SPAN:(ccn + 1) * SPAN])
                                     for ccn in range(nch)]
                            mm_group(psP0[0:65, 0:SPAN], items,
                                     reads=[pt] + va, writes=[psP0])
                            scalar_copy(aof[hg][hr, bsl], psP0[0:64, 0:SPAN],
                                        reads=[psP0], writes=[aof[hg]])
                            scalar_copy(dnr[0:1, bsl], psP0[64:65, 0:SPAN],
                                        reads=[psP0], writes=[denP3])
                    vop("tensor_scalar_add", denP3[0:33, :], reads=[denP3], writes=[denP3],
                        in0=denP3[0:33, :], scalar1=1e-9)
                    vop("reciprocal", denP3[0:33, :], reads=[denP3], writes=[denP3],
                        in_=denP3[0:33, :])
                    for sp in range(NSPAN):
                        sl = slice(sp * SPAN, (sp + 1) * SPAN)
                        def g0(eng, sl=sl, dn0=dn0):
                            return eng.matmul(out=psR[0:64, 0:SPAN], lhsT=allones[0:1, 0:64],
                                              rhs=dn0[0:1, sl], start=True, stop=True)
                        b.op("t", g0, reads=[allones, denP3], writes=[psR])
                        def g1_(eng, sl=sl, dn1=dn1):
                            return eng.matmul(out=psR[64:128, 0:SPAN], lhsT=allones[32:33, 0:64],
                                              rhs=dn1[0:1, sl], start=True, stop=True)
                        b.op("t", g1_, reads=[allones, denP3], writes=[psR])
                        vop("tensor_mul", aof[hg][:, sl], reads=[aof[hg], psR],
                            writes=[aof[hg]], in0=aof[hg][:, sl], in1=psR[0:128, 0:SPAN])
            def evo2(ft, sp, ps):
                sl = slice(sp * SPAN, (sp + 1) * SPAN)
                act(tws3[:], ps[0:128, 0:SPAN], ACT.Copy, reads=[ps, bias_sb],
                    writes=[tws3], bias=bias_sb[:, 32 + ft:33 + ft])
                vop("tensor_add", xs[ft][:, sl], reads=[xs[ft], tws3], writes=[xs[ft]],
                    in0=xs[ft][:, sl], in1=tws3[:])
            atsc.__exit__(None, None, None)
            proj_fm(None, aof, H["l1b_wo"], D, evo2)

            g3, b3_ = gamma_of(2)
            emit_layernorm(xs, xln, gamma=g3, beta=b3_)
            b.dma("y", bias_sb[:, 0:32], H["l1b_ff1b"][:, 0:32], writes=[bias_sb])
            b.dma("y", bias_sb[:, 32:40], H["l1b_ff2b"][:, 0:8], writes=[bias_sb])
            hsp = b.sbuf_in(sc, f"{tag}_hsp", [128, 32 * SPAN], BF16)
            w2v = H["l1b_ff2w"].rearrange("(kc p) f -> kc p f", p=128)
            for sp in range(NSPAN):
                sl = slice(sp * SPAN, (sp + 1) * SPAN)
                def evh(ft, sp2, ps, sp=sp):
                    act(hsp[:, ft * SPAN:(ft + 1) * SPAN], ps[0:128, 0:SPAN],
                        ACT.Relu, reads=[ps, bias_sb], writes=[hsp],
                        bias=bias_sb[:, ft:ft + 1])
                proj_fm(None, xln, H["l1b_ff1w"], 4 * D, evh, spans=[sp])
                for ft in range(KC):
                    for kq in range(4):
                        sg = kq % 2
                        b.dma_group("y", [(wstage[sg][:, kc * 128:kc * 128 + 128],
                                           w2v[kq * 8 + kc][:, ft * 128:(ft + 1) * 128],
                                           [], [wstage[sg]]) for kc in range(KC)])
                        cast_to("v", wtile[sg][:, 0:KC * 128], wstage[sg][:, 0:KC * 128],
                                reads=[wstage[sg]], writes=[wtile[sg]])
                        def fn(eng, sg=sg, kq=kq, ft=ft, sl=sl):
                            inst = None
                            for kc in range(KC):
                                inst = eng.matmul(
                                    out=psA[0:128, 0:SPAN],
                                    lhsT=wtile[sg][:, kc * 128:kc * 128 + 128],
                                    rhs=hsp[:, (kq * 8 + kc) * SPAN:(kq * 8 + kc + 1) * SPAN],
                                    start=(kq == 0 and kc == 0),
                                    stop=(kq == 3 and kc == KC - 1))
                            return inst
                        b.op("t", fn, reads=[wtile[sg], hsp], writes=[psA])
                    act(tws3[:], psA[0:128, 0:SPAN], ACT.Copy, reads=[psA, bias_sb],
                        writes=[tws3], bias=bias_sb[:, 32 + ft:33 + ft])
                    vop("tensor_add", xs[ft][:, sl], reads=[xs[ft], tws3], writes=[xs[ft]],
                        in0=xs[ft][:, sl], in1=tws3[:])

            for c in range(KC):
                for h in range(2):
                    ob = wstage[h]
                    hc = slice(h * 1024, (h + 1) * 1024)
                    b.dma("y", ob[:], xsv[c][:, hc], writes=[ob])
                    vop("tensor_add", xs[c][:, hc], reads=[xs[c], ob], writes=[xs[c]],
                        in0=xs[c][:, hc], in1=ob[:])

    def store_out(src_chunks):
        ov = out_ap.rearrange("(c p) t -> c p t", p=128)
        for c in range(KC):
            b.dma("gd", ov[c], src_chunks[c][:], reads=[src_chunks[c]])

    if trunc >= 1:
        l0_attn_a()
    if trunc >= 2:
        l0_ffn("f0a")
    if trunc >= 3:
        l0_attn_b()
    if trunc >= 4:
        l0_ffn("f0b")
    if trunc >= 5:
        l1_attn_a()
    if trunc >= 6:
        l1_ffn("f1a")
    if trunc >= 7:
        l1_attn_b()
    if trunc >= 8:
        l1_ffn("f1b")
    store_out(xs)
    b.finalize(["out"])
    return nc


def _consts():
    c = {}
    c["c_ident_bf"] = np.eye(128, dtype=np.float32)
    c["c_ident_f32"] = np.eye(128, dtype=np.float32)
    c["c_ones_col"] = np.ones((128, 1), dtype=np.float32)
    c["c_ones_colf"] = np.ones((128, 1), dtype=np.float32)
    c["c_ones_row"] = np.ones((1, 128), dtype=np.float32)
    sel2 = np.zeros((2, 128), dtype=np.float32)
    sel2[0, 0:64] = 1.0
    sel2[1, 64:128] = 1.0
    c["c_sel2"] = sel2
    tm = np.zeros((128, 4 * SPAN), dtype=np.float32)
    r = np.arange(128)[:, None]
    col = np.arange(SPAN)[None, :]
    for u in range(4):
        tm[:, u * SPAN:(u + 1) * SPAN] = np.where(col < r + 128 * u, NEG, 0.0)
    c["c_trimask"] = tm
    c["c_slotmask"] = np.where(col < 4 * (r + 1), NEG, 0.0).astype(np.float32)
    had = np.array([[1, 1, 1, 1], [1, -1, 1, -1], [1, 1, -1, -1], [1, -1, -1, 1]],
                   dtype=np.float32)
    cdma = np.tile(had, (1, 16))
    cs = np.zeros((128, 4), dtype=np.float32)
    for p in range(128):
        cs[p, :] = cdma[:, p % 64]
    c["c_csign"] = cs
    base = 10000.0
    dh = 64
    inv = 1.0 / base ** (np.arange(0, dh, 2, dtype=np.float32) / dh)
    pos = np.arange(T, dtype=np.float32)
    fr = pos[:, None] * inv[None, :]
    emb = np.concatenate([fr, fr], axis=-1)
    cosv = np.cos(emb).T
    sinv = np.sin(emb).T
    cos2 = np.tile(cosv, (2, 1)).astype(np.float32)
    sin2 = np.tile(sinv, (2, 1)).astype(np.float32)
    csT = np.zeros((128, T), dtype=np.float32)
    for p in range(128):
        csT[p, :] = cdma[np.arange(T) % 4, p % 64]
    c["c_ropec"] = cos2
    c["c_ropes"] = sin2
    rsign = np.where((np.arange(128) % 64) < 32, -1.0, 1.0).astype(np.float32)[:, None]
    c["c_ropecc"] = cos2 * csT
    c["c_ropesc"] = sin2 * csT * rsign
    return c


_BF16_INPUTS = {"c_ident_bf", "c_ones_col", "c_trimask", "c_slotmask"}


def _to_ml_bf16(a):
    import ml_dtypes
    return np.ascontiguousarray(a).astype(ml_dtypes.bfloat16)


def kernel(**inputs):
    trunc = TRUNC
    nc = build_program(trunc)
    consts = _consts()

    def TT(a):
        return np.ascontiguousarray(np.asarray(a).T.astype(np.float32))

    def BIAS(a, nft=None):
        a = np.asarray(a).astype(np.float32)
        n = a.shape[0] // 128
        return np.ascontiguousarray(a.reshape(n, 128).T) if a.shape[0] >= 128 else \
            np.ascontiguousarray(a.reshape(1, -1)).T.reshape(128, -1) if False else \
            np.pad(a.reshape(1, -1), ((0, 0), (0, 0))).T.reshape(-1, 1) * np.ones((1, 1), np.float32)

    def BIAS128(a):
        a = np.asarray(a).astype(np.float32)
        if a.size % 128 == 0 and a.size >= 128:
            return np.ascontiguousarray(a.reshape(-1, 128).T)
        out = np.zeros((128, 1), dtype=np.float32)
        out[:a.size, 0] = a
        return out

    base = {}
    base["l0a_wqkv"] = np.ascontiguousarray(inputs["L0_attn_a_W_qkv"].astype(np.float32))
    for i in range(1, 5):
        base[f"cn_w{i}"] = np.ascontiguousarray(inputs[f"L0_ffn_cn_w{i}"].astype(np.float32))
        base[f"cn_b{i}"] = BIAS128(inputs[f"L0_ffn_cn_b{i}"])
    for i in range(1, 4):
        base[f"gc_w{i}"] = np.ascontiguousarray(inputs[f"L0_ffn_gc_w{i}"].astype(np.float32))
        base[f"gc_b{i}"] = BIAS128(inputs[f"L0_ffn_gc_b{i}"])
    base["ar"] = inputs["L0_ffn_adaptive_range"].reshape(1, 1).astype(np.float32)
    base["l0b_wq"] = np.ascontiguousarray(inputs["L0_attn_b_Wq"].astype(np.float32))
    base["l0b_wk"] = np.ascontiguousarray(inputs["L0_attn_b_Wk"].astype(np.float32))
    base["l0b_wv"] = np.ascontiguousarray(inputs["L0_attn_b_Wv"].astype(np.float32))
    base["l0b_wo"] = np.ascontiguousarray(inputs["L0_attn_b_Wo"].astype(np.float32))
    base["l0b_aq"] = np.ascontiguousarray(inputs["L0_attn_b_Aq"].astype(np.float32))
    base["l0b_av"] = np.ascontiguousarray(inputs["L0_attn_b_Av"].astype(np.float32))
    base["l0b_ao"] = np.ascontiguousarray(inputs["L0_attn_b_Ao"].astype(np.float32))
    base["l0b_bq"] = np.ascontiguousarray(inputs["L0_attn_b_Bq"].astype(np.float32))
    base["l0b_bv"] = np.ascontiguousarray(inputs["L0_attn_b_Bv"].astype(np.float32))
    base["l0b_bo"] = np.ascontiguousarray(inputs["L0_attn_b_Bo"].astype(np.float32))
    for w in ["Wq", "Wk", "Wv", "Wo", "Wfc"]:
        base[f"l1a_{w.lower()}"] = TT(inputs[f"L1_attn_a_{w}"])
    for w in ["bq", "bk", "bv", "bo", "bfc"]:
        base[f"l1a_{w}"] = BIAS128(inputs[f"L1_attn_a_{w}"])
    base["l1a_projT"] = TT(inputs["L1_attn_a_proj"])
    base["l1b_sfw"] = TT(inputs["L1_attn_b_sf_w"])
    base["l1b_mask"] = TT(inputs["L1_attn_b_mask"])
    base["l1b_sfb"] = BIAS128(inputs["L1_attn_b_sf_b"])
    for w in ["wq", "wk", "wv", "wo"]:
        base[f"l1b_{w}"] = TT(inputs[f"L1_attn_b_{w}"])
        base[f"l1b_b{w[1]}"] = BIAS128(inputs[f"L1_attn_b_b{w[1]}"])
    base["l1b_ff1w"] = TT(inputs["L1_attn_b_ff1_w"])
    base["l1b_ff1b"] = BIAS128(inputs["L1_attn_b_ff1_b"])
    base["l1b_ff2w"] = TT(inputs["L1_attn_b_ff2_w"])
    base["l1b_ff2b"] = BIAS128(inputs["L1_attn_b_ff2_b"])
    for g in ["g1", "b1", "g2", "b2", "g3", "b3"]:
        base[f"l1b_{g}"] = BIAS128(inputs[f"L1_attn_b_{g}"])
    base["l1f_wr"] = TT(inputs["L1_ffn_W_r"])
    base["l1f_wk"] = TT(inputs["L1_ffn_W_k"])
    base["l1f_wv"] = TT(inputs["L1_ffn_W_v"])
    base["l1f_mur"] = BIAS128(inputs["L1_ffn_mu_r"])
    base["l1f_muk"] = BIAS128(inputs["L1_ffn_mu_k"])
    base.update(consts)
    for k in list(base.keys()):
        if k in _BF16_INPUTS:
            base[k] = _to_ml_bf16(base[k])

    x = np.asarray(inputs["L0_attn_a_x"]).astype(np.float32)
    in_maps = []
    for core in range(8):
        m = dict(base)
        m["x"] = np.ascontiguousarray(x[core].T)
        in_maps.append(m)

    res = run_bass_kernel_spmd(nc, in_maps, core_ids=list(range(8)))
    if MK_DEBUG:
        import __main__
        __main__.MK_DBG_RESULTS = res.results
    outs = [np.asarray(r["out"]).T for r in res.results]
    return np.stack(outs, axis=0).astype(np.float32)


if __name__ == "__main__":
    print("build test")
    nc = build_program(1)
    print("instructions:", nc.n_instructions() if hasattr(nc, "n_instructions") else "?")


# version 07 (final)
# speedup vs baseline: 1.0577x; 1.0325x over previous
import os
import numpy as np
from contextlib import ExitStack

import concourse.bass as bass
import concourse.mybir as mybir
from concourse.bass_utils import run_bass_kernel_spmd

F32 = mybir.dt.float32
BF16 = mybir.dt.bfloat16
AX = mybir.AxisListType.X
ALU = mybir.AluOpType
ACT = mybir.ActivationFunctionType

T = 2048
D = 1024
KC = 8
NSPAN = 4
SPAN = 512
NH = 16
DH = 64
NEG = -1.0e30

TRUNC = int(os.environ.get("MK_TRUNC", "99"))
MK_DEBUG = os.environ.get("MK_DEBUG", "")


class B:
    COMPUTE = ("t", "v", "s", "g")
    QUEUES = ("y", "gd")
    POOL = {"y": 20, "gd": 8}

    def __init__(self, nc):
        self.nc = nc
        self.st = ExitStack()
        self.sems = {e: self.st.enter_context(nc.semaphore(f"sem_{e}"))
                     for e in self.COMPUTE}
        self.qsems = {}
        self.qtot = {}
        self.qptr = {}
        for q in self.QUEUES:
            self.qsems[q] = [self.st.enter_context(nc.semaphore(f"qs_{q}{i}"))
                             for i in range(self.POOL[q])]
            self.qtot[q] = [0] * self.POOL[q]
            self.qptr[q] = 0
        self.seq = {e: 0 for e in self.COMPUTE + self.QUEUES}
        self.progs = {e: [] for e in self.COMPUTE + self.QUEUES}
        self.last_write = {}
        self.readers = {}
        self.watermark = {e: {} for e in self.COMPUTE + self.QUEUES}
        self._names = set()
        self._sbufs = {}
        self.pending_free = {}

    def sbuf(self, name, shape, dtype):
        assert name not in self._names, name
        self._names.add(name)
        h = self.st.enter_context(self.nc.sbuf_tensor(name, list(shape), dtype))
        return h

    def psum(self, name, shape, dtype):
        h = self.st.enter_context(self.nc.psum_tensor(name, list(shape), dtype))
        return h

    def scope(self):
        return ExitStack()

    def sbuf_in(self, stack, name, shape, dtype):
        assert name not in self._names, name
        self._names.add(name)
        h = stack.enter_context(self.nc.sbuf_tensor(name, list(shape), dtype))
        pend = getattr(self, "pending_free", None)
        if pend:
            self.readers[name] = [key + (v,) for key, v in pend.items()]
        def _cleanup(nm=name):
            self._names.discard(nm)
            pendd = self.pending_free
            lw = self.last_write.pop(nm, None)
            toks = list(self.readers.pop(nm, ()) or ())
            if lw is not None:
                toks.append(lw)
            for tok in toks:
                key, v = tok[:-1], tok[-1]
                if pendd.get(key, 0) < v:
                    pendd[key] = v
        stack.callback(_cleanup)
        return h

    @staticmethod
    def _key(x):
        t = getattr(x, "tensor", x)
        return t.name

    def _collect(self, reads, writes):
        waits = {}
        def need(tok):
            key = tok[:-1]
            v = tok[-1]
            if key not in waits or v > waits[key]:
                waits[key] = v
        for r in reads:
            lw = self.last_write.get(self._key(r))
            if lw:
                need(lw)
        for w in writes:
            k = self._key(w)
            lw = self.last_write.get(k)
            if lw:
                need(lw)
            for tok in self.readers.get(k, ()):
                need(tok)
        return waits

    def _emit_waits(self, e, waits):
        wl = []
        for key, v in waits.items():
            if key[0] == "E" and key[1] == e:
                pass
            if self.watermark[e].get(key, 0) >= v:
                continue
            self.watermark[e][key] = v
            wl.append((key, v))
        return wl

    def op(self, e, fn, reads=(), writes=()):
        assert e in self.COMPUTE
        waits = self._collect(reads, writes)
        idx = self.seq[e] + 1
        self.seq[e] = idx
        for r in reads:
            self.readers.setdefault(self._key(r), []).append(("E", e, idx))
        for w in writes:
            k = self._key(w)
            self.last_write[k] = ("E", e, idx)
            self.readers[k] = []
        wl = self._emit_waits(e, waits)
        self.progs[e].append((wl, fn, e))

    def dma_group(self, q, items):
        assert q in self.QUEUES
        reads = []
        writes = []
        for (_o, _i, rds, wts) in items:
            reads += list(rds)
            writes += list(wts)
        waits = self._collect(reads, writes)
        si = self.qptr[q]
        self.qptr[q] = (si + 1) % self.POOL[q]
        gate = self.qtot[q][si]
        self.qtot[q][si] = gate + 16 * len(items)
        tot = self.qtot[q][si]
        for r in reads:
            self.readers.setdefault(self._key(r), []).append(("Q", q, si, tot))
        for w in writes:
            k = self._key(w)
            self.last_write[k] = ("Q", q, si, tot)
            self.readers[k] = []
        wl = self._emit_waits(q, waits)
        aps = [(o, i) for (o, i, _r, _w) in items]
        self.progs[q].append((wl, (si, gate, aps), q))
        return ("Q", q, si, tot)

    def dma(self, q, out_ap, in_ap, reads=(), writes=()):
        self.dma_group(q, [(out_ap, in_ap, reads, writes)])

    def finalize(self, out_keys):
        nc = self.nc
        sems = self.sems
        qsems = self.qsems

        def wait_tok(eng, key, v):
            if key[0] == "E":
                eng.wait_ge(sems[key[1]], v)
            else:
                eng.wait_ge(qsems[key[1]][key[2]], v)

        def runner(e):
            def run(eng):
                for wl, fn, kind in self.progs[e]:
                    for (key, v) in wl:
                        wait_tok(eng, key, v)
                    if e in self.COMPUTE:
                        inst = fn(eng)
                        inst.then_inc(sems[e], 1)
                    else:
                        si, gate, aps = fn
                        if gate > 0:
                            eng.wait_ge(qsems[e][si], gate)
                        for (o, i) in aps:
                            eng.dma_start(out=o, in_=i).then_inc(
                                qsems[e][si], 16)
                if e == "gd":
                    for si2 in range(self.POOL["gd"]):
                        if self.qtot["gd"][si2] > 0:
                            eng.wait_ge(qsems["gd"][si2], self.qtot["gd"][si2])
            return run

        with nc.Block() as blk:
            blk.sync(lambda eng: runner("y")(eng))
            blk.vector(lambda eng: runner("v")(eng))
            blk.scalar(lambda eng: runner("s")(eng))
            blk.tensor(lambda eng: runner("t")(eng))

            def gps(eng):
                runner("g")(eng)
                runner("gd")(eng)
            blk.gpsimd(gps)
        self.st.close()


def build_program(trunc=99):
    nc = bass.Bass("TRN2", target_bir_lowering=False)
    b = B(nc)

    def din(name, shape, dtype=F32):
        return nc.dram_tensor(name, list(shape), dtype, kind="ExternalInput").ap()

    H = {}
    H["x"] = din("x", [D, T])
    H["l0a_wqkv"] = din("l0a_wqkv", [D, 3 * D])
    for i, (a, c) in enumerate([(D, D), (D, 512), (512, 256), (256, 1)]):
        H[f"cn_w{i+1}"] = din(f"cn_w{i+1}", [a, c])
        H[f"cn_b{i+1}"] = din(f"cn_b{i+1}", [128, max(1, c // 128) if c >= 128 else 1])
    for i, (a, c) in enumerate([(D, 512), (512, 256), (256, 1)]):
        H[f"gc_w{i+1}"] = din(f"gc_w{i+1}", [a, c])
        H[f"gc_b{i+1}"] = din(f"gc_b{i+1}", [128, max(1, c // 128) if c >= 128 else 1])
    H["ar"] = din("ar", [1, 1])
    for w in ["wq", "wk", "wv", "wo"]:
        H[f"l0b_{w}"] = din(f"l0b_{w}", [D, D])
    for w in ["aq", "av", "ao"]:
        H[f"l0b_{w}"] = din(f"l0b_{w}", [D, 64])
    for w in ["bq", "bv", "bo"]:
        H[f"l0b_{w}"] = din(f"l0b_{w}", [64, D])
    for w in ["wq", "wk", "wv", "wo", "wfc"]:
        H[f"l1a_{w}"] = din(f"l1a_{w}", [D, D])
    for w in ["bq", "bk", "bv", "bo", "bfc"]:
        H[f"l1a_{w}"] = din(f"l1a_{w}", [128, 8])
    H["l1a_projT"] = din("l1a_projT", [64, 266])
    H["l1b_sfw"] = din("l1b_sfw", [D, D])
    H["l1b_mask"] = din("l1b_mask", [D, D])
    H["l1b_sfb"] = din("l1b_sfb", [128, 8])
    for w in ["wq", "wk", "wv", "wo"]:
        H[f"l1b_{w}"] = din(f"l1b_{w}", [D, D])
    for w in ["bq", "bk", "bv", "bo", "bo2"][:4]:
        H[f"l1b_{w}"] = din(f"l1b_{w}", [128, 8])
    H["l1b_ff1w"] = din("l1b_ff1w", [D, 4 * D])
    H["l1b_ff1b"] = din("l1b_ff1b", [128, 32])
    H["l1b_ff2w"] = din("l1b_ff2w", [4 * D, D])
    H["l1b_ff2b"] = din("l1b_ff2b", [128, 8])
    for g in ["g1", "b1", "g2", "b2", "g3", "b3"]:
        H[f"l1b_{g}"] = din(f"l1b_{g}", [128, 8])
    for w in ["wr", "wk", "wv"]:
        H[f"l1f_{w}"] = din(f"l1f_{w}", [D, D])
    H["l1f_mur"] = din("l1f_mur", [128, 8])
    H["l1f_muk"] = din("l1f_muk", [128, 8])
    H["c_ident_bf"] = din("c_ident_bf", [128, 128], BF16)
    H["c_ident_f32"] = din("c_ident_f32", [128, 128])
    H["c_ones_col"] = din("c_ones_col", [128, 1], BF16)
    H["c_ones_colf"] = din("c_ones_colf", [128, 1])
    H["c_ones_row"] = din("c_ones_row", [1, 128])
    H["c_sel2"] = din("c_sel2", [2, 128])
    H["c_trimask"] = din("c_trimask", [128, 4 * SPAN], BF16)
    H["c_slotmask"] = din("c_slotmask", [128, SPAN], BF16)
    H["c_csign"] = din("c_csign", [128, 4])
    H["c_ropecc"] = din("c_ropecc", [128, T])
    H["c_ropesc"] = din("c_ropesc", [128, T])
    H["c_ropec"] = din("c_ropec", [128, T])
    H["c_ropes"] = din("c_ropes", [128, T])

    out_ap = nc.dram_tensor("out", [D, T], F32, kind="ExternalOutput").ap()
    xspill = nc.dram_tensor("xspill", [D, T], F32, kind="Internal").ap()

    xs = [b.sbuf(f"xs{c}", [128, T], F32) for c in range(KC)]
    xln = [b.sbuf(f"xln{c}", [128, T], BF16) for c in range(KC)]
    ident_bf = b.sbuf("ident_bf", [128, 128], BF16)
    ident_f = b.sbuf("ident_f", [128, 128], F32)
    ones_col = b.sbuf("ones_col", [128, 1], BF16)
    ones_colf = b.sbuf("ones_colf", [128, 1], F32)
    ones_row = b.sbuf("ones_row", [1, 128], F32)
    sel2 = b.sbuf("sel2", [2, 128], F32)
    allones = b.sbuf("allones", [128, 128], F32)
    allones_bf = b.sbuf("allones_bf", [128, 128], BF16)
    slotmask = b.sbuf("slotmask", [128, SPAN], BF16)
    csign = b.sbuf("csign", [128, 4], F32)
    lnrows_t = b.sbuf("lnrows_t", [128, T], F32)
    lnrow = lnrows_t[0:1, :]
    lnrow2 = lnrows_t[32:33, :]
    lnrow3 = lnrows_t[64:65, :]

    psA = b.psum("psA", [128, 512], F32)
    psS01 = b.psum("psS01", [128, 1024], F32)
    psS23 = b.psum("psS23", [128, 1024], F32)
    psS0 = psS01[0:128, 0:512]
    psS1 = psS23[0:128, 0:512]
    psB = psS01[0:128, 512:1024]
    psR2 = psS23[0:128, 512:1024]
    psP0 = b.psum("psP0", [128, 512], F32)
    psR = b.psum("psR", [128, 512], F32)
    psT = b.psum("psT", [128, 1024], BF16)

    def _ao_ms(eng):
        return eng.memset(allones[:], 1.0)
    b.op("v", _ao_ms, reads=[], writes=[allones])
    def _ao_ms2(eng):
        return eng.memset(allones_bf[:], 1.0)
    b.op("v", _ao_ms2, reads=[], writes=[allones_bf])
    for dst, src in [(ident_bf, H["c_ident_bf"]), (ident_f, H["c_ident_f32"]),
                     (ones_col, H["c_ones_col"]), (ones_colf, H["c_ones_colf"]),
                     (ones_row, H["c_ones_row"]), (sel2, H["c_sel2"]),
                     (slotmask, H["c_slotmask"]),
                     (csign, H["c_csign"])]:
        b.dma("gd", dst[:], src, writes=[dst])

    xv = H["x"].rearrange("(c p) t -> c p t", p=128)
    for c in range(KC):
        b.dma("gd", xs[c][:], xv[c], writes=[xs[c]])

    vop_early = []
    uid = [0]
    def U(pfx):
        uid[0] += 1
        return f"{pfx}{uid[0]}"

    dbg_outs = []
    def dbg(name, ap_list):
        if not MK_DEBUG or name not in MK_DEBUG.split(","):
            return
        for i, ap in enumerate(ap_list):
            p, f = ap.shape[0], ap.shape[1]
            dt = ap.dtype
            o = nc.dram_tensor(f"dbg_{name}_{i}", [p, f], dt, kind="ExternalOutput").ap()
            b.dma("gd", o, ap, reads=[ap])
            dbg_outs.append(f"dbg_{name}_{i}")

    def mm_group(ps_ap, items, reads, writes, extra=()):
        n = len(items)
        def fn(eng, items=tuple(items), ps=ps_ap, n=n):
            inst = None
            for i, (lt, rh) in enumerate(items):
                inst = eng.matmul(out=ps, lhsT=lt, rhs=rh,
                                  start=(i == 0), stop=(i == n - 1))
            return inst
        b.op("t", fn, reads=list(reads), writes=list(writes))

    def transpose(ps_ap, in_ap, ident_ap, reads, writes):
        def fn(eng, o=ps_ap, i=in_ap, idn=ident_ap):
            return eng.transpose(out=o, in_=i, identity=idn)
        b.op("t", fn, reads=list(reads), writes=list(writes))

    def act(out_ap, in_ap, func, reads, writes, bias=0.0, scale=1.0, accum=None):
        if func == ACT.Copy and not isinstance(bias, float):
            func = ACT.Identity
        def fn(eng, o=out_ap, i=in_ap, f=func, bi=bias, sc=scale, ac=accum):
            kw = {}
            if ac is not None:
                kw["accum_out"] = ac
            return eng.activation(out=o, in_=i, func=f, bias=bi, scale=sc, **kw)
        b.op("s", fn, reads=list(reads), writes=list(writes))

    def vop(name, out_ap, reads, writes, **kw):
        def fn(eng, nm=name, o=out_ap, kw=kw):
            if nm == "memset":
                return eng.memset(o, kw["constant"])
            return getattr(eng, nm)(out=o, **kw)
        b.op("v", fn, reads=list(reads), writes=list(writes))

    def scalar_copy(out_ap, in_ap, reads, writes):
        def fn(eng, o=out_ap, i=in_ap):
            return eng.copy(out=o, in_=i)
        b.op("s", fn, reads=list(reads), writes=list(writes))

    def cast_to(e, out_ap, in_ap, reads, writes):
        if e == "s":
            scalar_copy(out_ap, in_ap, reads, writes)
        else:
            vop("tensor_copy", out_ap, reads, writes, in_=in_ap)

    def bcast_row(ps_ap, row_ap, m, reads, base=0):
        def fn(eng, o=ps_ap, r=row_ap, m=m, base=base):
            return eng.matmul(out=o, lhsT=allones[base:base + 1, 0:m], rhs=r,
                              start=True, stop=True)
        b.op("t", fn, reads=[allones] + list(reads), writes=[ps_ap])

    wstage = [b.sbuf(f"wstage{i}", [128, 1024], F32) for i in range(2)]
    wtile = [b.sbuf(f"wtile{i}", [128, 1024], BF16) for i in range(2)]
    bias_sb = b.sbuf("bias_sb", [128, 64], F32)

    def load_bias(dram_ap, nft):
        b.dma("y", bias_sb[:, 0:nft], dram_ap[:, 0:nft], writes=[bias_sb])

    QROT = ["y", "gd"]

    def proj_fm(dst, src, w_dram, F_out, evac, src_rows=KC, w_row_off=0,
                extra_mm=None, spans=range(NSPAN), cast_e="v", src_get=None,
                src_reads=None):
        nft = F_out // 128
        wv = w_dram.rearrange("(kc p) f -> kc p f", p=128)
        if src_get is None:
            src_get = lambda kc, sp: src[kc][:, sp * SPAN:(sp + 1) * SPAN]
            src_reads = src
        spans = list(spans)
        banks = [psA, psB, psS0, psS1][:len(spans)]
        for ft in range(nft):
            sg = ft % 2
            b.dma_group(QROT[ft % 2],
                        [(wstage[sg][:, kc * 128:kc * 128 + 128],
                          wv[kc][:, ft * 128:(ft + 1) * 128], [], [wstage[sg]])
                         for kc in range(src_rows)])
            cast_to(cast_e, wtile[sg][:, 0:src_rows * 128], wstage[sg][:, 0:src_rows * 128],
                    reads=[wstage[sg]], writes=[wtile[sg]])
            ex_items = {j: [] for j in range(len(spans))}
            ex_reads = {j: [] for j in range(len(spans))}
            if extra_mm is not None:
                for j, sp in enumerate(spans):
                    for (lt, rh, rds) in extra_mm(ft, sp):
                        ex_items[j].append((lt, rh))
                        ex_reads[j] += rds
            def fn(eng, sg=sg, spans=tuple(spans), banks=tuple(banks),
                   ex=ex_items, ft=ft):
                inst = None
                for kc in range(src_rows):
                    for j, sp in enumerate(spans):
                        inst = eng.matmul(out=banks[j][0:128, 0:SPAN],
                                          lhsT=wtile[sg][:, kc * 128:kc * 128 + 128],
                                          rhs=src_get(kc, sp),
                                          start=(kc == 0),
                                          stop=(kc == src_rows - 1 and not ex[j]))
                for j in range(len(spans)):
                    for i2, (lt, rh) in enumerate(ex[j]):
                        inst = eng.matmul(out=banks[j][0:128, 0:SPAN], lhsT=lt, rhs=rh,
                                          start=False, stop=(i2 == len(ex[j]) - 1))
                return inst
            reads = [wtile[sg]] + list(src_reads)
            for j in range(len(spans)):
                reads += ex_reads[j]
            b.op("t", fn, reads=reads, writes=list(banks))
            for j, sp in enumerate(spans):
                evac(ft, sp, banks[j])

    sq_ps = [psS0, psS1, psP0, psR2]
    sqbuf = b.sbuf("sqbuf", [128, T], BF16)

    def emit_ln_stats(src_chunks, tag="ln"):
        tmp_bf = xln
        for c in range(KC):
            cast_to("v" if c % 2 == 0 else "s", tmp_bf[c][:], src_chunks[c][:],
                    reads=[src_chunks[c]], writes=[tmp_bf[c]])
        for c in range(KC):
            def sqf(eng, i=tmp_bf[c], o=sqbuf):
                return eng.activation(out=o[:], in_=i[:], func=ACT.Square)
            b.op("s", sqf, reads=[tmp_bf[c]], writes=[sqbuf])
            for sp in range(NSPAN):
                sl = slice(sp * SPAN, (sp + 1) * SPAN)
                ps = sq_ps[sp]
                def fn(eng, ps=ps, sl=sl, c=c):
                    return eng.matmul(out=ps[0:1, 0:SPAN], lhsT=ones_col[:],
                                      rhs=sqbuf[:, sl], start=(c == 0),
                                      stop=(c == KC - 1))
                b.op("t", fn, reads=[ones_col, sqbuf], writes=[ps])
        for sp in range(NSPAN):
            sl = slice(sp * SPAN, (sp + 1) * SPAN)
            items = [(ones_col[:], tmp_bf[c][:, sl]) for c in range(KC)]
            mm_group(psR[0:1, 0:SPAN], items,
                     reads=[ones_col] + list(tmp_bf), writes=[psR])
            act(lnrow3[0:1, sl], psR[0:1, 0:SPAN], ACT.Copy, reads=[psR],
                writes=[lnrow3], scale=1.0 / D)
            act(lnrow[0:1, sl], sq_ps[sp][0:1, 0:SPAN], ACT.Copy,
                reads=[sq_ps[sp]], writes=[lnrow], scale=1.0 / D)
        vop("tensor_mul", lnrow2[0:1, :], reads=[lnrow3], writes=[lnrow2],
            in0=lnrow3[0:1, :], in1=lnrow3[0:1, :])
        for sp in range(NSPAN):
            sl = slice(sp * SPAN, (sp + 1) * SPAN)
            scalar_copy(psR2[0:1, 0:SPAN], lnrow2[0:1, sl], reads=[lnrow2],
                        writes=[psR2])
            vop("tensor_sub", lnrow[0:1, sl], reads=[lnrow, psR2], writes=[lnrow],
                in0=lnrow[0:1, sl], in1=psR2[0:1, 0:SPAN])
        vop("tensor_scalar_add", lnrow[0:1, :], reads=[lnrow], writes=[lnrow],
            in0=lnrow[0:1, :], scalar1=1e-5)
        def rsq(eng):
            return eng.activation(out=lnrow[0:1, :], in_=lnrow[0:1, :],
                                  func=ACT.Sqrt)
        b.op("s", rsq, reads=[lnrow], writes=[lnrow])
        vop("reciprocal", lnrow[0:1, :], reads=[lnrow], writes=[lnrow],
            in_=lnrow[0:1, :])
        for sp in range(NSPAN):
            sl = slice(sp * SPAN, (sp + 1) * SPAN)
            scalar_copy(psR2[0:1, 0:SPAN], lnrow[0:1, sl], reads=[lnrow],
                        writes=[psR2])
            vop("tensor_mul", lnrow2[0:1, sl], reads=[lnrow3, psR2], writes=[lnrow2],
                in0=lnrow3[0:1, sl], in1=psR2[0:1, 0:SPAN])

    def emit_ln_apply(src_chunks, out_dst, spans=range(NSPAN), col0=0,
                      gamma=None, beta=None):
        for sp in spans:
            sl = slice(sp * SPAN, (sp + 1) * SPAN)
            osl = slice(sp * SPAN - col0, (sp + 1) * SPAN - col0)
            bcast_row(psR[0:128, 0:SPAN], lnrow[0:1, sl], 128, reads=[lnrow])
            bcast_row(psR2[0:128, 0:SPAN], lnrow2[0:1, sl], 128, reads=[lnrow2],
                      base=32)
            for c in range(KC):
                od = out_dst[c]
                vop("tensor_mul", od[:, osl], reads=[src_chunks[c], psR], writes=[od],
                    in0=src_chunks[c][:, sl], in1=psR[0:128, 0:SPAN])
                vop("tensor_sub", od[:, osl], reads=[od, psR2], writes=[od],
                    in0=od[:, osl], in1=psR2[0:128, 0:SPAN])
                if gamma is not None:
                    gsb, gcol = gamma
                    bsb, bcol = beta
                    vop("tensor_scalar", od[:, osl], reads=[od, gsb, bsb], writes=[od],
                        in0=od[:, osl], scalar1=gsb[:, gcol(c):gcol(c) + 1],
                        scalar2=bsb[:, bcol(c):bcol(c) + 1],
                        op0=ALU.mult, op1=ALU.add)

    def emit_layernorm(src_chunks, out_dst, gamma=None, beta=None, tag="ln"):
        emit_ln_stats(src_chunks, tag=tag)
        emit_ln_apply(src_chunks, out_dst, gamma=gamma, beta=beta)

    def l0_attn_a():
        with b.scope() as sc:
            trimask = b.sbuf_in(sc, "l0a_trimask", [128, 4 * SPAN], BF16)
            b.dma("gd", trimask[:], H["c_trimask"], writes=[trimask])
            emit_layernorm(xs, xln, tag="l0a_ln")
            dbg("xln", [xln[c][:] for c in range(KC)])
            qh2 = [b.sbuf_in(sc, f"l0a_q{i}", [128, T], BF16) for i in range(2)]
            kh2 = [b.sbuf_in(sc, f"l0a_k{i}", [128, T], BF16) for i in range(2)]
            va2 = [[b.sbuf_in(sc, f"l0a_va{i}_{t}", [128, 130], BF16) for t in range(16)]
                   for i in range(2)]
            pt2 = [b.sbuf_in(sc, f"l0a_pt{i}", [128, 16 * SPAN], BF16) for i in range(2)]
            denP2_ = [b.sbuf_in(sc, f"l0a_denP{i}", [64, T], F32) for i in range(2)]
            ao_s = b.sbuf_in(sc, "l0a_ao", [128, T], BF16)
            wq = H["l0a_wqkv"]
            for hg in range(8):
                qh, kh, va, pt = qh2[hg % 2], kh2[hg % 2], va2[hg % 2], pt2[hg % 2]
                denP = denP2_[hg % 2]
                den0 = denP[0:1, :]
                den1 = denP[32:33, :]
                ao = ao_s
                def evac_q(ft, sp, ps, dst=qh):
                    act(dst[:, sp * SPAN:(sp + 1) * SPAN], ps[0:128, 0:SPAN], ACT.Copy,
                        reads=[ps], writes=[dst], scale=0.125)
                def evac_k(ft, sp, ps, dst=kh):
                    act(dst[:, sp * SPAN:(sp + 1) * SPAN], ps[0:128, 0:SPAN], ACT.Copy,
                        reads=[ps], writes=[dst])
                proj_fm([qh], xln, wq[:, hg * 128:(hg + 1) * 128], 128, evac_q)
                proj_fm([kh], xln, wq[:, D + hg * 128:D + (hg + 1) * 128], 128, evac_k)
                if hg == 0:
                    dbg("qh", [qh[:]])
                    dbg("kh", [kh[:]])
                sg = hg % 2
                wvv = wq.rearrange("(kc p) f -> kc p f", p=128)
                b.dma_group("y", [(wstage[sg][:, kc * 128:kc * 128 + 128],
                                   wvv[kc][:, 2 * D + hg * 128:2 * D + hg * 128 + 128],
                                   [], [wstage[sg]]) for kc in range(KC)])
                cast_to("v", wtile[sg][:, 0:KC * 128], wstage[sg][:, 0:KC * 128],
                        reads=[wstage[sg]], writes=[wtile[sg]])
                for tt in range(16):
                    ps = psA if tt % 2 == 0 else psB
                    items = [(xln[kc][:, tt * 128:tt * 128 + 128],
                              wtile[sg][:, kc * 128:kc * 128 + 128]) for kc in range(KC)]
                    mm_group(ps[0:128, 0:128], items,
                             reads=[wtile[sg]] + list(xln), writes=[ps])
                    scalar_copy(va[tt][:, 0:64], ps[0:128, 0:64], reads=[ps], writes=[va[tt]])
                    scalar_copy(va[tt][:, 65:129], ps[0:128, 64:128], reads=[ps], writes=[va[tt]])
                    vop("memset", va[tt][:, 64:65], reads=[], writes=[va[tt]], constant=1.0)
                    vop("memset", va[tt][:, 129:130], reads=[], writes=[va[tt]], constant=1.0)
                if hg == 0:
                    dbg("va", [va[0][:], va[1][:]])
                for hh in range(2):
                    h = hg * 2 + hh
                    qsl = qh[hh * 64:(hh + 1) * 64, :]
                    ksl = kh[hh * 64:(hh + 1) * 64, :]
                    vcol = hh * 65
                    for bb in range(NSPAN):
                        qs = slice(bb * SPAN, (bb + 1) * SPAN)
                        nchunks = 4 * bb + 4
                        for cc0 in range(0, nchunks, 2):
                            ps = psS01 if (cc0 // 2) % 2 == 0 else psS23
                            def qkf(eng, cc0=cc0, qs=qs, ps=ps, ksl=ksl, qsl=qsl):
                                i1 = eng.matmul(out=ps[0:128, 0:SPAN],
                                                lhsT=ksl[:, cc0 * 128:cc0 * 128 + 128],
                                                rhs=qsl[:, qs], start=True, stop=True)
                                i1 = eng.matmul(out=ps[0:128, SPAN:2 * SPAN],
                                                lhsT=ksl[:, (cc0 + 1) * 128:(cc0 + 2) * 128],
                                                rhs=qsl[:, qs], start=True, stop=True)
                                return i1
                            b.op("t", qkf, reads=[kh, qh], writes=[ps])
                            for j in range(2):
                                u = cc0 + j - 4 * bb
                                if 0 <= u <= 3:
                                    vop("tensor_add", ps[0:128, j * SPAN:(j + 1) * SPAN],
                                        reads=[ps, trimask], writes=[ps],
                                        in0=ps[0:128, j * SPAN:(j + 1) * SPAN],
                                        in1=trimask[:, u * SPAN:(u + 1) * SPAN])
                            act(pt[:, cc0 * SPAN:(cc0 + 2) * SPAN], ps[0:128, 0:2 * SPAN],
                                ACT.Exp, reads=[ps], writes=[pt])
                        items = [(va[cc][:, vcol:vcol + 65],
                                  pt[:, cc * SPAN:(cc + 1) * SPAN]) for cc in range(nchunks)]
                        mm_group(psP0[0:65, 0:SPAN], items,
                                 reads=[pt] + [va[cc] for cc in range(nchunks)],
                                 writes=[psP0])
                        if hg == 0 and hh == 0 and bb == 0:
                            dbg("pt", [pt[:, 0:4 * SPAN]])
                        scalar_copy(ao[hh * 64:hh * 64 + 64, qs], psP0[0:64, 0:SPAN],
                                    reads=[psP0], writes=[ao])
                        dh_ = den0 if hh == 0 else den1
                        scalar_copy(dh_[0:1, qs], psP0[64:65, 0:SPAN],
                                    reads=[psP0], writes=[dh_])
                if hg == 0:
                    dbg("ao", [ao[:]])
                    dbg("den", [den0[:], den1[:]])
                vop("reciprocal", den0[0:1, :], reads=[den0], writes=[den0], in_=den0[0:1, :])
                vop("reciprocal", den1[0:1, :], reads=[den1], writes=[den1], in_=den1[0:1, :])
                for sp in range(NSPAN):
                    sl = slice(sp * SPAN, (sp + 1) * SPAN)
                    def fn0(eng, sl=sl, den0=den0):
                        return eng.matmul(out=psR[0:64, 0:SPAN], lhsT=ones_row[:, 0:64],
                                          rhs=den0[0:1, sl], start=True, stop=True)
                    b.op("t", fn0, reads=[ones_row, den0], writes=[psR])
                    def fn1(eng, sl=sl, den1=den1):
                        return eng.matmul(out=psR[64:128, 0:SPAN],
                                          lhsT=allones[32:33, 0:64],
                                          rhs=den1[0:1, sl], start=True, stop=True)
                    b.op("t", fn1, reads=[allones, den1], writes=[psR])
                    vop("tensor_mul", ao[:, sl], reads=[ao, psR], writes=[ao],
                        in0=ao[:, sl], in1=psR[0:128, 0:SPAN])
                    vop("tensor_add", xs[hg][:, sl], reads=[xs[hg], ao], writes=[xs[hg]],
                        in0=xs[hg][:, sl], in1=ao[:, sl])
                if hg == 0:
                    dbg("aon", [ao[:]])
                    dbg("rden", [den0[:]])

    def l0_ffn(tag):
        TH = T // 2
        with b.scope() as sc:
            emit_ln_stats(xs, tag=tag)
            of = [b.sbuf_in(sc, f"{tag}_of{c}", [128, TH], F32) for c in range(KC)]
            ob = [xln[c] for c in range(KC)]
            h1 = [xln[c] for c in range(KC)]
            h2 = [b.sbuf_in(sc, f"{tag}_h2{c}", [128, TH], BF16) for c in range(4)]
            h3 = [b.sbuf_in(sc, f"{tag}_h3{c}", [128, TH], BF16) for c in range(2)]
            gsb = b.sbuf_in(sc, f"{tag}_gsb", [128, 8], F32)
            g1 = b.sbuf_in(sc, f"{tag}_g1", [128, 4], F32)
            g2 = b.sbuf_in(sc, f"{tag}_g2", [128, 2], F32)
            gfv = b.sbuf_in(sc, f"{tag}_gfv", [1, 2], F32)
            arow = b.sbuf_in(sc, f"{tag}_arow", [1, 4], F32)
            smsb = b.sbuf_in(sc, f"{tag}_smsb", [128, 1], F32)
            tw = [b.sbuf_in(sc, f"{tag}_tw{i}", [128, SPAN], F32) for i in range(2)]
            gsc = b.sbuf_in(sc, f"{tag}_gsc", [128, T], F32)

            for c in range(KC):
                for sp in range(NSPAN):
                    sl = slice(sp * SPAN, (sp + 1) * SPAN)
                    bcast_row(psR[0:128, 0:SPAN], lnrow[0:1, sl], 128, reads=[lnrow])
                    vop("tensor_mul", gsc[:, sl], reads=[xs[c], psR], writes=[gsc],
                        in0=xs[c][:, sl], in1=psR[0:128, 0:SPAN])
                vop("reduce_sum", gsb[:, c:c + 1], reads=[gsc], writes=[gsb],
                    in_=gsc[:], axis=AX)
            vop("reduce_sum", smsb[0:1, 0:1], reads=[lnrow2], writes=[smsb],
                in_=lnrow2[0:1, :], axis=AX)
            def pbc(eng):
                return eng.matmul(out=psB[0:128, 0:1], lhsT=ones_row[:],
                                  rhs=smsb[0:1, 0:1], start=True, stop=True)
            b.op("t", pbc, reads=[smsb, ones_row], writes=[psB])
            vop("scalar_tensor_tensor", gsb[:], reads=[gsb, psB], writes=[gsb],
                in0=gsb[:], scalar=1.0, in1=psB[0:128, 0:1].broadcast_to([128, 8]),
                op0=ALU.mult, op1=ALU.subtract)
            vop("tensor_scalar", gsb[:], reads=[gsb], writes=[gsb],
                in0=gsb[:], scalar1=1.0 / T, scalar2=None, op0=ALU.mult)

            for nm, col in [("gc_b1", 0), ("gc_b2", 8), ("gc_b3", 12)]:
                w = 4 if nm == "gc_b1" else (2 if nm == "gc_b2" else 1)
                b.dma("y", bias_sb[:, col:col + w], H[nm][:, 0:w], writes=[bias_sb])
            wv1 = H["gc_w1"].rearrange("(kc p) f -> kc p f", p=128)
            for ft in range(4):
                sg = ft % 2
                b.dma_group("y", [(wstage[sg][:, kc * 128:kc * 128 + 128],
                                   wv1[kc][:, ft * 128:(ft + 1) * 128], [], [wstage[sg]])
                                  for kc in range(KC)])
                cast_to("v", wtile[sg][:, 0:KC * 128], wstage[sg][:, 0:KC * 128],
                        reads=[wstage[sg]], writes=[wtile[sg]])
                gb = b.sbuf
                def gcast(eng, sg=sg):
                    return eng.tensor_copy(out=gsc[:, 0:8], in_=gsb[:, 0:8])
                vop("tensor_copy", sqbuf[:, 0:8], reads=[gsb], writes=[sqbuf],
                    in_=gsb[:, 0:8])
                items = [(wtile[sg][:, kc * 128:kc * 128 + 128], sqbuf[:, kc:kc + 1])
                         for kc in range(KC)]
                mm_group(psA[0:128, 0:1], items, reads=[wtile[sg], sqbuf], writes=[psA])
                act(g1[:, ft:ft + 1], psA[0:128, 0:1], ACT.Gelu, reads=[psA, bias_sb],
                    writes=[g1], bias=bias_sb[:, ft:ft + 1])
            vop("tensor_copy", sqbuf[:, 8:12], reads=[g1], writes=[sqbuf],
                in_=g1[:, 0:4])
            wv2 = H["gc_w2"].rearrange("(kc p) f -> kc p f", p=128)
            for ft in range(2):
                sg = ft % 2
                b.dma_group("y", [(wstage[sg][:, kc * 128:kc * 128 + 128],
                                   wv2[kc][:, ft * 128:(ft + 1) * 128], [], [wstage[sg]])
                                  for kc in range(4)])
                cast_to("v", wtile[sg][:, 0:4 * 128], wstage[sg][:, 0:4 * 128],
                        reads=[wstage[sg]], writes=[wtile[sg]])
                items = [(wtile[sg][:, kc * 128:kc * 128 + 128], sqbuf[:, 8 + kc:9 + kc])
                         for kc in range(4)]
                mm_group(psA[0:128, 0:1], items, reads=[wtile[sg], sqbuf], writes=[psA])
                act(g2[:, ft:ft + 1], psA[0:128, 0:1], ACT.Gelu, reads=[psA, bias_sb],
                    writes=[g2], bias=bias_sb[:, 8 + ft:9 + ft])
            vop("tensor_copy", sqbuf[:, 12:14], reads=[g2], writes=[sqbuf],
                in_=g2[:, 0:2])
            wv3 = H["gc_w3"].rearrange("(kc p) f -> kc p f", p=128)
            sg = 0
            b.dma_group("y", [(wstage[sg][:, kc * 128:kc * 128 + 1],
                               wv3[kc][:, 0:1], [], [wstage[sg]])
                              for kc in range(2)])
            cast_to("v", wtile[sg][:, 0:2 * 128], wstage[sg][:, 0:2 * 128],
                    reads=[wstage[sg]], writes=[wtile[sg]])
            items = [(wtile[sg][:, kc * 128:kc * 128 + 1], sqbuf[:, 12 + kc:13 + kc])
                     for kc in range(2)]
            mm_group(psA[0:1, 0:1], items, reads=[wtile[sg], sqbuf], writes=[psA])
            def sig(eng):
                return eng.activation(out=gfv[0:1, 0:1], in_=psA[0:1, 0:1],
                                      func=ACT.Sigmoid,
                                      bias=bias_sb[0:1, 12:13])
            b.op("s", sig, reads=[psA, bias_sb], writes=[gfv])

            b.dma("y", arow[0:1, 0:1], H["ar"], writes=[arow])
            vop("tensor_scalar_mul", arow[0:1, 1:2], reads=[arow], writes=[arow],
                in0=arow[0:1, 0:1], scalar1=0.6)
            vop("tensor_mul", arow[0:1, 2:3], reads=[arow, gfv], writes=[arow],
                in0=arow[0:1, 0:1], in1=gfv[0:1, 0:1])
            vop("tensor_scalar", arow[0:1, 2:3], reads=[arow], writes=[arow],
                in0=arow[0:1, 2:3], scalar1=1.4, scalar2=0.1, op0=ALU.mult,
                op1=ALU.add)
            vop("tensor_sub", arow[0:1, 2:3], reads=[arow], writes=[arow],
                in0=arow[0:1, 2:3], in1=arow[0:1, 0:1])

            for nm, col, w in [("cn_b1", 16, 8), ("cn_b2", 24, 4), ("cn_b3", 28, 2)]:
                b.dma("y", bias_sb[:, col:col + w], H[nm][:, 0:w], writes=[bias_sb])
            b.dma("y", arow[0:1, 3:4], H["cn_b4"][0:1, 0:1], writes=[arow])

            pairs = [(0, 4, 1.0, -1.0), (1, 5, 1.0, -1.0),
                     (2, 6, -1.0, 1.0), (3, 7, -1.0, 1.0)]
            for half in range(2):
                spans = [2 * half, 2 * half + 1]
                col0 = half * TH
                emit_ln_apply(xs, of, spans=spans, col0=col0)
                for cyc in range(3):
                    for c in range(KC):
                        cast_to("v" if c % 2 == 0 else "s", ob[c][:, 0:TH], of[c][:],
                                reads=[of[c]], writes=[ob[c]])
                    def ev1(ft, sp, ps, half=half):
                        act(h1[ft][:, TH + (sp - 2 * half) * SPAN:TH + (sp - 2 * half + 1) * SPAN],
                            ps[0:128, 0:SPAN], ACT.Gelu, reads=[ps, bias_sb],
                            writes=[h1[ft]], bias=bias_sb[:, 16 + ft:17 + ft])
                    proj_fm(None, None, H["cn_w1"], 1024, ev1, spans=spans,
                            src_get=lambda kc, sp, half=half: ob[kc][:, (sp - 2 * half) * SPAN:(sp - 2 * half + 1) * SPAN],
                            src_reads=ob)
                    def ev2(ft, sp, ps, half=half):
                        act(h2[ft][:, (sp - 2 * half) * SPAN:(sp - 2 * half + 1) * SPAN],
                            ps[0:128, 0:SPAN], ACT.Gelu, reads=[ps, bias_sb],
                            writes=[h2[ft]], bias=bias_sb[:, 24 + ft:25 + ft])
                    proj_fm(None, None, H["cn_w2"], 512, ev2, spans=spans,
                            src_get=lambda kc, sp, half=half: h1[kc][:, TH + (sp - 2 * half) * SPAN:TH + (sp - 2 * half + 1) * SPAN],
                            src_reads=h1)
                    def ev3(ft, sp, ps, half=half):
                        act(h3[ft][:, (sp - 2 * half) * SPAN:(sp - 2 * half + 1) * SPAN],
                            ps[0:128, 0:SPAN], ACT.Gelu, reads=[ps, bias_sb],
                            writes=[h3[ft]], bias=bias_sb[:, 28 + ft:29 + ft])
                    proj_fm(None, None, H["cn_w3"], 256, ev3, spans=spans,
                            src_rows=4,
                            src_get=lambda kc, sp, half=half: h2[kc][:, (sp - 2 * half) * SPAN:(sp - 2 * half + 1) * SPAN],
                            src_reads=h2)
                    wv4 = H["cn_w4"].rearrange("(kc p) f -> kc p f", p=128)
                    sg4 = 0
                    b.dma_group("y", [(wstage[sg4][:, 1024 - 2 + kc:1024 - 1 + kc],
                                       wv4[kc][:, 0:1], [], [wstage[sg4]])
                                      for kc in range(2)])
                    cast_to("v", wtile[sg4][:, 1022:1024], wstage[sg4][:, 1022:1024],
                            reads=[wstage[sg4]], writes=[wtile[sg4]])
                    for sp in spans:
                        lsl = slice((sp - 2 * half) * SPAN, (sp - 2 * half + 1) * SPAN)
                        items = [(wtile[sg4][:, 1022 + kc:1023 + kc], h3[kc][:, lsl])
                                 for kc in range(2)]
                        mm_group(psP0[0:1, 0:SPAN], items,
                                 reads=[wtile[sg4]] + h3, writes=[psP0])
                        def sig4(eng, sp=sp):
                            return eng.activation(
                                out=lnrow3[0:1, sp * SPAN:(sp + 1) * SPAN],
                                in_=psP0[0:1, 0:SPAN], func=ACT.Sigmoid,
                                bias=arow[0:1, 3:4])
                        b.op("s", sig4, reads=[psP0, arow], writes=[lnrow3])
                        vop("tensor_scalar", lnrow3[0:1, sp * SPAN:(sp + 1) * SPAN],
                            reads=[lnrow3, arow], writes=[lnrow3],
                            in0=lnrow3[0:1, sp * SPAN:(sp + 1) * SPAN],
                            scalar1=arow[0:1, 1:2], scalar2=arow[0:1, 2:3],
                            op0=ALU.mult, op1=ALU.add)
                        bcast_row(psR2[0:128, 0:SPAN],
                                  lnrow3[0:1, sp * SPAN:(sp + 1) * SPAN], 128,
                                  reads=[lnrow3], base=64)
                        for (p, qq, s1, s2) in pairs:
                            vop("scalar_tensor_tensor", tw[0][:], reads=[of[qq], psR2],
                                writes=[tw[0]], in0=of[qq][:, lsl], scalar=s1,
                                in1=psR2[0:128, 0:SPAN], op0=ALU.mult, op1=ALU.mult)
                            vop("scalar_tensor_tensor", tw[1][:], reads=[of[p], psR2],
                                writes=[tw[1]], in0=of[p][:, lsl], scalar=s2,
                                in1=psR2[0:128, 0:SPAN], op0=ALU.mult, op1=ALU.mult)
                            vop("tensor_add", of[p][:, lsl], reads=[of[p], tw[0]],
                                writes=[of[p]], in0=of[p][:, lsl], in1=tw[0][:])
                            vop("tensor_add", of[qq][:, lsl], reads=[of[qq], tw[1]],
                                writes=[of[qq]], in0=of[qq][:, lsl], in1=tw[1][:])
                for c in range(KC):
                    gsl = slice(col0, col0 + TH)
                    vop("tensor_add", xs[c][:, gsl], reads=[xs[c], of[c]],
                        writes=[xs[c]], in0=xs[c][:, gsl], in1=of[c][:])

    def l0_attn_b(tag="b0"):
        csap = csign[:].unsqueeze(1).broadcast_to([128, SPAN // 4, 4])
        with b.scope() as sc:
            emit_layernorm(xs, xln)
            cc = b.sbuf_in(sc, f"{tag}_cc", [128, T], F32)
            scs = b.sbuf_in(sc, f"{tag}_scs", [128, T], F32)
            b.dma("gd", cc[:], H["c_ropecc"], writes=[cc])
            b.dma("gd", scs[:], H["c_ropesc"], writes=[scs])
            aof = [b.sbuf_in(sc, f"{tag}_ao{c}", [128, T], BF16) for c in range(KC)]
            t1 = b.sbuf_in(sc, f"{tag}_t1", [128, T], BF16)
            bsm = b.sbuf_in(sc, f"{tag}_bsm", [128, 2 * D], BF16)
            rows4 = b.sbuf_in(sc, f"{tag}_rows4", [128, T], F32)
            expA = rows4[0:1, :]
            expB = rows4[32:33, :]
            denA = rows4[64:65, :]
            denB = rows4[0:1, :]

            for nm, r0, c0 in [("bq", 0, 0), ("bo", 0, D), ("bv", 64, 0)]:
                b.dma("y", wstage[0][0:64, 0:D], H[f"l0b_{nm}"], writes=[wstage[0]])
                cast_to("v", bsm[r0:r0 + 64, c0:c0 + D], wstage[0][0:64, 0:D],
                        reads=[wstage[0]], writes=[bsm])

            for i, nm in enumerate(["aq", "av"]):
                wvv = H[f"l0b_{nm}"].rearrange("(kc p) f -> kc p f", p=128)
                b.dma_group("y", [(wstage[1][:, kc * 64:kc * 64 + 64], wvv[kc],
                                   [], [wstage[1]]) for kc in range(KC)])
                cast_to("v", wtile[1][:, 0:KC * 64], wstage[1][:, 0:KC * 64],
                        reads=[wstage[1]], writes=[wtile[1]])
                for sp in range(NSPAN):
                    sl = slice(sp * SPAN, (sp + 1) * SPAN)
                    items = [(wtile[1][:, kc * 64:kc * 64 + 64], xln[kc][:, sl])
                             for kc in range(KC)]
                    mm_group(psA[0:64, 0:SPAN], items, reads=[wtile[1]] + xln,
                             writes=[psA])
                    scalar_copy(t1[i * 64:(i + 1) * 64, sl], psA[0:64, 0:SPAN],
                                reads=[psA], writes=[t1])

            for hg in range(KC):
                hsl = slice(hg * 128, (hg + 1) * 128)
                with b.scope() as hs:
                    fullK = b.sbuf_in(hs, f"{tag}_fk{hg}", [128, SPAN], BF16)
                    fullV = b.sbuf_in(hs, f"{tag}_fv{hg}", [128, SPAN], BF16)
                    cpv = b.sbuf_in(hs, f"{tag}_cpv{hg}", [128, T], BF16)
                    fvt = b.sbuf_in(hs, f"{tag}_fvt{hg}", [128, 2 * 260], BF16)
                    pts = b.sbuf_in(hs, f"{tag}_pts{hg}", [128, 4 * SPAN], BF16)
                    cptmp = b.sbuf_in(hs, f"{tag}_cptmp{hg}", [64, SPAN], F32)
                    sh = sqbuf

                    def do_proj(wnm, dst, lora_i, rope):
                        def evq(ft, sp, ps, dst=dst):
                            act(dst[:, sp * SPAN:(sp + 1) * SPAN], ps[0:128, 0:SPAN],
                                ACT.Copy, reads=[ps], writes=[dst])
                        ex = None
                        if lora_i is not None:
                            def ex(ft, sp, li=lora_i):
                                r0 = 0 if li == 0 else 64
                                return [(bsm[r0:r0 + 64, hg * 128:(hg + 1) * 128],
                                         t1[r0:r0 + 64, sp * SPAN:(sp + 1) * SPAN],
                                         [bsm, t1])]
                        proj_fm(None, xln, H[f"l0b_{wnm}"][:, hsl], 128, evq,
                                extra_mm=ex)
                        if rope:
                            b.dma_group("y", [
                                (sh[0:32, :], dst[32:64, :], [dst], [sh]),
                                (sh[32:64, :], dst[0:32, :], [dst], [sh]),
                                (sh[64:96, :], dst[96:128, :], [dst], [sh]),
                                (sh[96:128, :], dst[64:96, :], [dst], [sh])])
                            vop("tensor_mul", dst[:], reads=[dst, cc], writes=[dst],
                                in0=dst[:], in1=cc[:])
                            vop("scalar_tensor_tensor", sh[:], reads=[sh, scs],
                                writes=[sh], in0=sh[:], scalar=1.0, in1=scs[:],
                                op0=ALU.mult, op1=ALU.mult)
                            vop("tensor_add", dst[:], reads=[dst, sh], writes=[dst],
                                in0=dst[:], in1=sh[:])
                        else:
                            def csm(eng, d=dst):
                                return eng.tensor_mul(
                                    out=d[:].rearrange("p (s f) -> p s f", f=4),
                                    in0=d[:].rearrange("p (s f) -> p s f", f=4),
                                    in1=csign[:].unsqueeze(1).broadcast_to([128, T // 4, 4]))
                            b.op("v", csm, reads=[dst, csign], writes=[dst])

                    def slot_sum(srcb, dstb):
                        v3 = srcb[:].rearrange("p (s f) -> p s f", f=4)
                        vop("tensor_add", dstb[:].rearrange("p (s o) -> p s o", o=1),
                            reads=[srcb], writes=[dstb], in0=v3[:, :, 0:1], in1=v3[:, :, 1:2])
                        vop("tensor_add", dstb[:].rearrange("p (s o) -> p s o", o=1),
                            reads=[srcb, dstb], writes=[dstb],
                            in0=dstb[:].rearrange("p (s o) -> p s o", o=1), in1=v3[:, :, 2:3])
                        vop("tensor_add", dstb[:].rearrange("p (s o) -> p s o", o=1),
                            reads=[srcb, dstb], writes=[dstb],
                            in0=dstb[:].rearrange("p (s o) -> p s o", o=1), in1=v3[:, :, 3:4])

                    def slot_cumsum(srcb, cpb):
                        vop("tensor_copy", cpb[:], reads=[srcb], writes=[cpb], in_=srcb[:])
                        c3 = cpb[:].rearrange("p (s f) -> p s f", f=4)
                        s3 = srcb[:].rearrange("p (s f) -> p s f", f=4)
                        for j in range(1, 4):
                            vop("tensor_add", c3[:, :, j:j + 1], reads=[srcb, cpb],
                                writes=[cpb], in0=c3[:, :, j - 1:j], in1=s3[:, :, j:j + 1])

                    with b.scope() as vsc:
                        vs = b.sbuf_in(vsc, f"{tag}_vs{hg}", [128, T], BF16)
                        do_proj("wv", vs, 1, False)
                        slot_sum(vs, fullV)
                        slot_cumsum(vs, cpv)
                        for hh in range(2):
                            hr = slice(hh * 64, hh * 64 + 64)
                            for cs_ in range(4):
                                transpose(psT[0:128, 0:64], fullV[hr, cs_ * 128:(cs_ + 1) * 128],
                                          ident_bf[hr, hh * 64:hh * 64 + 64],
                                          reads=[fullV, ident_bf], writes=[psT])
                                col = hh * 260 + cs_ * 65
                                scalar_copy(fvt[:, col:col + 64], psT[0:128, 0:64],
                                            reads=[psT], writes=[fvt])
                                vop("memset", fvt[:, col + 64:col + 65], reads=[],
                                    writes=[fvt], constant=1.0)
                    aq = b.sbuf_in(hs, f"{tag}_aq{hg}", [128, T], BF16)
                    ks = aof[hg]
                    m2 = pts
                    do_proj("wq", aq, 0, True)
                    do_proj("wk", ks, None, True)
                    slot_sum(ks, fullK)
                    vop("tensor_copy", m2[:], reads=[ks], writes=[m2], in_=ks[:])
                    c3 = m2[:].rearrange("p (s f) -> p s f", f=4)
                    s3 = ks[:].rearrange("p (s f) -> p s f", f=4)
                    for j in range(1, 4):
                        vop("tensor_add", c3[:, :, j:j + 1], reads=[ks, m2],
                            writes=[m2], in0=c3[:, :, j - 1:j], in1=s3[:, :, j:j + 1])
                    vop("tensor_mul", sh[:], reads=[aq, m2], writes=[sh],
                        in0=aq[:], in1=m2[:])
                    for hh, exr in [(0, expA), (1, expB)]:
                        hr = slice(hh * 64, hh * 64 + 64)
                        for sp in range(NSPAN):
                            sl = slice(sp * SPAN, (sp + 1) * SPAN)
                            mm_group(psR2[0:1, 0:SPAN],
                                     [(ones_col[hr, :], sh[hr, sl])],
                                     reads=[ones_col, sh], writes=[psR2])
                            act(exr[0:1, sl], psR2[0:1, 0:SPAN], ACT.Exp,
                                reads=[psR2], writes=[exr], scale=1.0 / 16.0)
                    for hh, exr, dnr in [(0, expA, denA), (1, expB, denB)]:
                        hr = slice(hh * 64, hh * 64 + 64)
                        for bb in range(NSPAN):
                            bsl = slice(bb * SPAN, (bb + 1) * SPAN)
                            for cc0 in range(0, bb + 1, 2):
                                nj = min(2, bb + 1 - cc0)
                                ps = psS01 if (cc0 // 2) % 2 == 0 else psS23
                                def qkf(eng, cc0=cc0, bsl=bsl, ps=ps, hr=hr, nj=nj,
                                        fullK=fullK, aq=aq):
                                    i1 = None
                                    for j in range(nj):
                                        i1 = eng.matmul(
                                            out=ps[0:128, j * SPAN:(j + 1) * SPAN],
                                            lhsT=fullK[hr, (cc0 + j) * 128:(cc0 + j + 1) * 128],
                                            rhs=aq[hr, bsl], start=True, stop=True)
                                    return i1
                                b.op("t", qkf, reads=[fullK, aq], writes=[ps])
                                for j in range(nj):
                                    if cc0 + j == bb:
                                        vop("tensor_add", ps[0:128, j * SPAN:(j + 1) * SPAN],
                                            reads=[ps, slotmask], writes=[ps],
                                            in0=ps[0:128, j * SPAN:(j + 1) * SPAN],
                                            in1=slotmask[:])
                                act(pts[:, cc0 * SPAN:(cc0 + nj) * SPAN],
                                    ps[0:128, 0:nj * SPAN],
                                    ACT.Exp, reads=[ps], writes=[pts], scale=1.0 / 16.0)
                            items = [(fvt[:, hh * 260 + ccn * 65:hh * 260 + ccn * 65 + 65],
                                      pts[:, ccn * SPAN:(ccn + 1) * SPAN])
                                     for ccn in range(bb + 1)]
                            mm_group(psP0[0:65, 0:SPAN], items, reads=[fvt, pts],
                                     writes=[psP0])
                            bcast_row(psR[0:64, 0:SPAN], exr[0:1, bsl], 64,
                                      reads=[exr], base=(0 if hh == 0 else 32))
                            vop("tensor_mul", cptmp[:], reads=[cpv, psR], writes=[cptmp],
                                in0=cpv[hr, bsl], in1=psR[0:64, 0:SPAN])
                            vop("tensor_add", psP0[0:64, 0:SPAN], reads=[psP0, cptmp],
                                writes=[psP0], in0=psP0[0:64, 0:SPAN], in1=cptmp[:])
                            vop("tensor_add", psP0[64:65, 0:SPAN], reads=[psP0, exr],
                                writes=[psP0], in0=psP0[64:65, 0:SPAN], in1=exr[0:1, bsl])
                            scalar_copy(aof[hg][hr, bsl], psP0[0:64, 0:SPAN],
                                        reads=[psP0], writes=[aof[hg]])
                            scalar_copy(dnr[0:1, bsl], psP0[64:65, 0:SPAN],
                                        reads=[psP0], writes=[dnr])
                    vop("reciprocal", rows4[0:65, :], reads=[rows4], writes=[rows4],
                        in_=rows4[0:65, :])
                    for sp in range(NSPAN):
                        sl = slice(sp * SPAN, (sp + 1) * SPAN)
                        def f0(eng, sl=sl):
                            return eng.matmul(out=psR[0:64, 0:SPAN],
                                              lhsT=allones[64:65, 0:64],
                                              rhs=denA[0:1, sl], start=True, stop=True)
                        b.op("t", f0, reads=[allones, denA], writes=[psR])
                        def f1(eng, sl=sl):
                            return eng.matmul(out=psR[64:128, 0:SPAN],
                                              lhsT=allones[0:1, 0:64],
                                              rhs=denB[0:1, sl], start=True, stop=True)
                        b.op("t", f1, reads=[allones, denB], writes=[psR])
                        vop("tensor_mul", aof[hg][:, sl], reads=[aof[hg], psR],
                            writes=[aof[hg]], in0=aof[hg][:, sl], in1=psR[0:128, 0:SPAN])
                        def csm2(eng, sl=sl, hg=hg):
                            return eng.tensor_mul(
                                out=aof[hg][:, sl].rearrange("p (s f) -> p s f", f=4),
                                in0=aof[hg][:, sl].rearrange("p (s f) -> p s f", f=4),
                                in1=csap)
                        b.op("v", csm2, reads=[aof[hg], csign], writes=[aof[hg]])
            wvv = H["l0b_ao"].rearrange("(kc p) f -> kc p f", p=128)
            b.dma_group("y", [(wstage[1][:, kc * 64:kc * 64 + 64], wvv[kc],
                               [], [wstage[1]]) for kc in range(KC)])
            cast_to("v", wtile[1][:, 0:KC * 64], wstage[1][:, 0:KC * 64],
                    reads=[wstage[1]], writes=[wtile[1]])
            for sp in range(NSPAN):
                sl = slice(sp * SPAN, (sp + 1) * SPAN)
                items = [(wtile[1][:, kc * 64:kc * 64 + 64], aof[kc][:, sl])
                         for kc in range(KC)]
                mm_group(psA[0:64, 0:SPAN], items, reads=[wtile[1]] + aof, writes=[psA])
                scalar_copy(t1[0:64, sl], psA[0:64, 0:SPAN], reads=[psA], writes=[t1])
            def evr(ft, sp, ps):
                sl = slice(sp * SPAN, (sp + 1) * SPAN)
                vop("tensor_add", xs[ft][:, sl], reads=[xs[ft], ps], writes=[xs[ft]],
                    in0=xs[ft][:, sl], in1=ps[0:128, 0:SPAN])
            def exo(ft, sp):
                return [(bsm[0:64, D + ft * 128:D + (ft + 1) * 128],
                         t1[0:64, sp * SPAN:(sp + 1) * SPAN], [bsm, t1])]
            proj_fm(None, aof, H["l0b_wo"], D, evr, extra_mm=exo)

    def l1_ffn(tag):
        with b.scope() as sc:
            emit_layernorm(xs, xln)
            musb = b.sbuf_in(sc, f"{tag}_mu", [128, 32], F32)
            b.dma("y", musb[:, 0:8], H["l1f_mur"][:, 0:8], writes=[musb])
            b.dma("y", musb[:, 8:16], H["l1f_muk"][:, 0:8], writes=[musb])
            vop("tensor_scalar", musb[:, 16:32], reads=[musb], writes=[musb],
                in0=musb[:, 0:16], scalar1=-1.0, scalar2=1.0, op0=ALU.mult, op1=ALU.add)
            r = [b.sbuf_in(sc, f"{tag}_r{c}", [128, T], BF16) for c in range(KC)]

            def mix(dst, mucol):
                for c in range(KC):
                    vop("tensor_scalar_mul", dst[c][:], reads=[xln[c], musb],
                        writes=[dst[c]], in0=xln[c][:], scalar1=musb[:, mucol + c:mucol + c + 1])
                    vop("scalar_tensor_tensor", dst[c][:, 1:T], reads=[xln[c], musb, dst[c]],
                        writes=[dst[c]], in0=xln[c][:, 0:T - 1],
                        scalar=musb[:, 16 + mucol + c:17 + mucol + c],
                        in1=dst[c][:, 1:T], op0=ALU.mult, op1=ALU.add)

            with b.scope() as sa:
                xr = [b.sbuf_in(sa, f"{tag}_xr{c}", [128, T], BF16) for c in range(KC)]
                mix(xr, 0)
                def evr(ft, sp, ps):
                    act(r[ft][:, sp * SPAN:(sp + 1) * SPAN], ps[0:128, 0:SPAN],
                        ACT.Sigmoid, reads=[ps], writes=[r[ft]])
                proj_fm(None, xr, H["l1f_wr"], D, evr)
            with b.scope() as sb_:
                xk = [b.sbuf_in(sb_, f"{tag}_xk{c}", [128, T], BF16) for c in range(KC)]
                mix(xk, 8)
                ka = xln
                def evk(ft, sp, ps):
                    act(ka[ft][:, sp * SPAN:(sp + 1) * SPAN], ps[0:128, 0:SPAN],
                        ACT.Relu, reads=[ps], writes=[ka[ft]])
                proj_fm(None, xk, H["l1f_wk"], D, evk)
            for c in range(KC):
                def sqk(eng, c=c):
                    return eng.activation(out=xln[c][:], in_=xln[c][:], func=ACT.Square)
                b.op("s", sqk, reads=[xln[c]], writes=[xln[c]])
            tws = b.sbuf_in(sc, f"{tag}_tws", [128, SPAN], F32)
            def evv(ft, sp, ps):
                sl = slice(sp * SPAN, (sp + 1) * SPAN)
                vop("tensor_mul", tws[:], reads=[r[ft], ps], writes=[tws],
                    in0=r[ft][:, sl], in1=ps[0:128, 0:SPAN])
                vop("tensor_add", xs[ft][:, sl], reads=[xs[ft], tws], writes=[xs[ft]],
                    in0=xs[ft][:, sl], in1=tws[:])
            proj_fm(None, xln, H["l1f_wv"], D, evv)

    def l1_attn_a(tag="p1"):
        M = 266
        DN = float(64.0 ** -0.25)
        RATIO = float(266.0 ** -0.5)
        EPS = 1e-4
        with b.scope() as sc:
            emit_layernorm(xs, xln)
            aof = [b.sbuf_in(sc, f"{tag}_ao{c}", [128, T], BF16) for c in range(KC)]
            pj2 = b.sbuf_in(sc, f"{tag}_pj2", [128, M], BF16)
            b.dma("y", wstage[0][0:64, 0:M], H["l1a_projT"], writes=[wstage[0]])
            cast_to("v", pj2[0:64, 0:M], wstage[0][0:64, 0:M], reads=[wstage[0]], writes=[pj2])
            cast_to("v", pj2[64:128, 0:M], wstage[0][0:64, 0:M], reads=[wstage[0]], writes=[pj2])
            for nm, col in [("bq", 0), ("bk", 8), ("bv", 16), ("bo", 24), ("bfc", 32)]:
                b.dma("y", bias_sb[:, col:col + 8], H[f"l1a_{nm}"][:, 0:8], writes=[bias_sb])
            vop("tensor_scalar_mul", bias_sb[:, 0:16], reads=[bias_sb], writes=[bias_sb],
                in0=bias_sb[:, 0:16], scalar1=DN)
            denP2 = b.sbuf_in(sc, f"{tag}_den", [64, T], F32)
            dnA = denP2[0:1, :]
            dnB = denP2[32:33, :]
            bqrow = b.sbuf_in(sc, f"{tag}_bqrow", [1, T], BF16)
            for hg in range(KC):
                hsl = slice(hg * 128, (hg + 1) * 128)
                with b.scope() as hs:
                    qb = b.sbuf_in(hs, f"{tag}_qb{hg}", [128, T], BF16)
                    kb = b.sbuf_in(hs, f"{tag}_kb{hg}", [128, T], BF16)
                    va = [b.sbuf_in(hs, f"{tag}_va{hg}_{t}", [128, 130], BF16)
                          for t in range(16)]
                    kp = b.sbuf_in(hs, f"{tag}_kp{hg}", [128, 16 * M], BF16)
                    qpt0 = b.sbuf_in(hs, f"{tag}_qpt0{hg}", [128, T], BF16)
                    qpt1 = b.sbuf_in(hs, f"{tag}_qpt1{hg}", [128, T], BF16)
                    qpt2 = b.sbuf_in(hs, f"{tag}_qpt2{hg}", [10, T], BF16)
                    stat = b.sbuf_in(hs, f"{tag}_st{hg}", [128, 64], F32)
                    trow = b.sbuf_in(hs, f"{tag}_tr{hg}", [16, 128], BF16)
                    ctx = b.sbuf_in(hs, f"{tag}_ctx{hg}", [65, M], BF16)
                    ctxa = [b.sbuf_in(hs, f"{tag}_cx{hg}_{i}", [128, 65], BF16)
                            for i in range(3)]
                    def evq(ft, sp, ps, dst=qb, bc=0):
                        act(dst[:, sp * SPAN:(sp + 1) * SPAN], ps[0:128, 0:SPAN],
                            ACT.Copy, reads=[ps, bias_sb], writes=[dst], scale=DN,
                            bias=bias_sb[:, bc + hg:bc + hg + 1])
                    def evk_(ft, sp, ps, dst=kb, bc=8):
                        act(dst[:, sp * SPAN:(sp + 1) * SPAN], ps[0:128, 0:SPAN],
                            ACT.Copy, reads=[ps, bias_sb], writes=[dst], scale=DN,
                            bias=bias_sb[:, bc + hg:bc + hg + 1])
                    proj_fm(None, xln, H["l1a_wq"][:, hsl], 128, evq)
                    proj_fm(None, xln, H["l1a_wk"][:, hsl], 128, evk_)
                    sg = hg % 2
                    wvv = H["l1a_wv"].rearrange("(kc p) f -> kc p f", p=128)
                    b.dma_group("y", [(wstage[sg][:, kc * 128:kc * 128 + 128],
                                       wvv[kc][:, hsl], [], [wstage[sg]])
                                      for kc in range(KC)])
                    cast_to("v", wtile[sg][:, 0:KC * 128], wstage[sg][:, 0:KC * 128],
                            reads=[wstage[sg]], writes=[wtile[sg]])
                    transpose(psA[0:1, 0:128], bias_sb[:, 16 + hg:17 + hg],
                              ident_f[:], reads=[bias_sb, ident_f], writes=[psA])
                    scalar_copy(bqrow[0:1, 0:128], psA[0:1, 0:128], reads=[psA],
                                writes=[bqrow])
                    for tt in range(16):
                        ps = psB if tt % 2 == 0 else psP0
                        items = [(xln[kc][:, tt * 128:tt * 128 + 128],
                                  wtile[sg][:, kc * 128:kc * 128 + 128]) for kc in range(KC)]
                        items.append((allones_bf[0:1, 0:128], bqrow[0:1, 0:128]))
                        mm_group(ps[0:128, 0:128], items,
                                 reads=[wtile[sg], bqrow, allones_bf] + list(xln), writes=[ps])
                        scalar_copy(va[tt][:, 0:64], ps[0:128, 0:64], reads=[ps], writes=[va[tt]])
                        scalar_copy(va[tt][:, 65:129], ps[0:128, 64:128], reads=[ps], writes=[va[tt]])
                        vop("memset", va[tt][:, 64:65], reads=[], writes=[va[tt]], constant=1.0)
                        vop("memset", va[tt][:, 129:130], reads=[], writes=[va[tt]], constant=1.0)
                    for hh in range(2):
                        h0 = hh * 64
                        hr = slice(h0, h0 + 64)
                        def sqq(eng, s=qb):
                            return eng.activation(out=sqbuf[:], in_=s[:], func=ACT.Square)
                        b.op("s", sqq, reads=[qb], writes=[sqbuf])
                        for nt in range(16):
                            mm_group(psS0[0:128, 0:1],
                                     [(sqbuf[hr, nt * 128:(nt + 1) * 128], ones_col[hr, :])],
                                     reads=[sqbuf, ones_col], writes=[psS0])
                            scalar_copy(stat[:, nt:nt + 1], psS0[0:128, 0:1],
                                        reads=[psS0], writes=[stat])
                            mm_group(psS1[0:128, 0:M],
                                     [(qb[hr, nt * 128:(nt + 1) * 128], pj2[hr, :])],
                                     reads=[qb, pj2], writes=[psS1])
                            vop("reduce_max", stat[:, 16 + nt:17 + nt], reads=[psS1],
                                writes=[stat], in_=psS1[0:128, 0:M], axis=AX)
                        def sqk2(eng, s=kb):
                            return eng.activation(out=sqbuf[:], in_=s[:], func=ACT.Square)
                        b.op("s", sqk2, reads=[kb], writes=[sqbuf])
                        for nt in range(16):
                            mm_group(psS0[0:128, 0:1],
                                     [(sqbuf[hr, nt * 128:(nt + 1) * 128], ones_col[hr, :])],
                                     reads=[sqbuf, ones_col], writes=[psS0])
                            scalar_copy(stat[:, 32 + nt:33 + nt], psS0[0:128, 0:1],
                                        reads=[psS0], writes=[stat])
                            mm_group(psS1[0:128, 0:M],
                                     [(kb[hr, nt * 128:(nt + 1) * 128], pj2[hr, :])],
                                     reads=[kb, pj2], writes=[psS1])
                            vop("reduce_max", stat[:, 48 + nt:49 + nt], reads=[psS1],
                                writes=[stat], in_=psS1[0:128, 0:M], axis=AX)
                        vop("reduce_max", stat[:, 48:49], reads=[stat], writes=[stat],
                            in_=stat[:, 48:64], axis=AX)
                        transpose(psA[0:1, 0:128], stat[:, 48:49], ident_f[:],
                                  reads=[stat, ident_f], writes=[psA])
                        vop("reduce_max", psA[0:1, 0:1], reads=[psA], writes=[psA],
                            in_=psA[0:1, 0:128], axis=AX)
                        scalar_copy(bqrow[0:1, 256:257], psA[0:1, 0:1], reads=[psA],
                                    writes=[bqrow])
                        def mxb(eng):
                            return eng.matmul(out=psR2[0:128, 0:1], lhsT=allones_bf[0:1, :],
                                              rhs=bqrow[0:1, 256:257], start=True, stop=True)
                        b.op("t", mxb, reads=[allones_bf, bqrow], writes=[psR2])
                        vop("scalar_tensor_tensor", stat[:, 32:48], reads=[stat, psR2],
                            writes=[stat], in0=stat[:, 32:48], scalar=-0.5,
                            in1=psR2[0:128, 0:1].broadcast_to([128, 16]),
                            op0=ALU.mult, op1=ALU.subtract)
                        vop("tensor_scalar_mul", stat[:, 0:16], reads=[stat], writes=[stat],
                            in0=stat[:, 0:16], scalar1=0.5)
                        vop("tensor_add", stat[:, 0:16], reads=[stat], writes=[stat],
                            in0=stat[:, 0:16], in1=stat[:, 16:32])
                        vop("tensor_scalar_mul", stat[:, 0:16], reads=[stat], writes=[stat],
                            in0=stat[:, 0:16], scalar1=-1.0)
                        for nt in range(16):
                            ps = psS0 if nt % 2 == 0 else psS1
                            mm_group(ps[0:128, 0:M],
                                     [(kb[hr, nt * 128:(nt + 1) * 128], pj2[hr, :])],
                                     reads=[kb, pj2], writes=[ps])
                            act(kp[:, nt * M:(nt + 1) * M], ps[0:128, 0:M], ACT.Exp,
                                reads=[ps, stat], writes=[kp],
                                bias=stat[:, 32 + nt:33 + nt])
                        vop("tensor_scalar", kp[:], reads=[kp], writes=[kp],
                            in0=kp[:], scalar1=RATIO, scalar2=RATIO * EPS,
                            op0=ALU.mult, op1=ALU.add)
                        transpose(psA[0:16, 0:128], stat[:, 0:16], ident_f[:],
                                  reads=[stat, ident_f], writes=[psA])
                        scalar_copy(trow[0:16, :], psA[0:16, 0:128], reads=[psA],
                                    writes=[trow])
                        b.dma_group("y", [(bqrow[0:1, nt * 128:(nt + 1) * 128],
                                           trow[nt:nt + 1, 0:128], [trow], [bqrow])
                                          for nt in range(16)])
                        qpt = [qpt0, qpt1, qpt2]
                        for mc in range(3):
                            mw = 128 if mc < 2 else 10
                            for sp in range(NSPAN):
                                sl = slice(sp * SPAN, (sp + 1) * SPAN)
                                ps = psS0 if sp % 2 == 0 else psS1
                                items = [(pj2[hr, mc * 128:mc * 128 + mw], qb[hr, sl]),
                                         (allones_bf[0:1, 0:mw], bqrow[0:1, sl])]
                                mm_group(ps[0:mw, 0:SPAN], items,
                                         reads=[pj2, qb, allones_bf, bqrow], writes=[ps])
                                act(qpt[mc][0:mw, sl], ps[0:mw, 0:SPAN], ACT.Exp,
                                    reads=[ps], writes=[qpt[mc]])
                            vop("tensor_scalar", qpt[mc][0:mw, :], reads=[qpt[mc]],
                                writes=[qpt[mc]], in0=qpt[mc][0:mw, :], scalar1=RATIO,
                                scalar2=RATIO * EPS, op0=ALU.mult, op1=ALU.add)
                        items = [(va[nt][:, hh * 65:hh * 65 + 65], kp[:, nt * M:(nt + 1) * M])
                                 for nt in range(16)]
                        mm_group(psP0[0:65, 0:M], items,
                                 reads=[kp] + va, writes=[psP0])
                        scalar_copy(ctx[0:65, 0:M], psP0[0:65, 0:M], reads=[psP0], writes=[ctx])
                        for mc in range(3):
                            mw = 128 if mc < 2 else 10
                            transpose(psT[0:mw, 0:65], ctx[0:65, mc * 128:mc * 128 + mw],
                                      ident_bf[0:65, 0:65], reads=[ctx, ident_bf], writes=[psT])
                            scalar_copy(ctxa[mc][0:mw, :], psT[0:mw, 0:65],
                                        reads=[psT], writes=[ctxa[mc]])
                        dn_r = dnA if hh == 0 else dnB
                        for sp in range(NSPAN):
                            sl = slice(sp * SPAN, (sp + 1) * SPAN)
                            items = [(ctxa[mc][0:(128 if mc < 2 else 10), :],
                                      qpt[mc][0:(128 if mc < 2 else 10), sl])
                                     for mc in range(3)]
                            mm_group(psP0[0:65, 0:SPAN], items,
                                     reads=[qpt0, qpt1, qpt2] + ctxa, writes=[psP0])
                            scalar_copy(aof[hg][hr, sl], psP0[0:64, 0:SPAN],
                                        reads=[psP0], writes=[aof[hg]])
                            scalar_copy(dn_r[0:1, sl], psP0[64:65, 0:SPAN],
                                        reads=[psP0], writes=[denP2])
                    vop("reciprocal", denP2[0:33, :], reads=[denP2], writes=[denP2],
                        in_=denP2[0:33, :])
                    for sp in range(NSPAN):
                        sl = slice(sp * SPAN, (sp + 1) * SPAN)
                        def g0(eng, sl=sl):
                            return eng.matmul(out=psR[0:64, 0:SPAN], lhsT=allones[0:1, 0:64],
                                              rhs=dnA[0:1, sl], start=True, stop=True)
                        b.op("t", g0, reads=[allones, denP2], writes=[psR])
                        def g1(eng, sl=sl):
                            return eng.matmul(out=psR[64:128, 0:SPAN], lhsT=allones[32:33, 0:64],
                                              rhs=dnB[0:1, sl], start=True, stop=True)
                        b.op("t", g1, reads=[allones, denP2], writes=[psR])
                        vop("tensor_mul", aof[hg][:, sl], reads=[aof[hg], psR],
                            writes=[aof[hg]], in0=aof[hg][:, sl], in1=psR[0:128, 0:SPAN])
            def evo(ft, sp, ps):
                act(xln[ft][:, sp * SPAN:(sp + 1) * SPAN], ps[0:128, 0:SPAN],
                    ACT.Copy, reads=[ps, bias_sb], writes=[xln[ft]],
                    bias=bias_sb[:, 24 + ft:25 + ft])
            proj_fm(None, aof, H["l1a_wo"], D, evo)
            tws2 = b.sbuf_in(sc, f"{tag}_tws", [128, SPAN], BF16)
            def evf(ft, sp, ps):
                sl = slice(sp * SPAN, (sp + 1) * SPAN)
                act(tws2[:], ps[0:128, 0:SPAN], ACT.Relu, reads=[ps, bias_sb],
                    writes=[tws2], bias=bias_sb[:, 32 + ft:33 + ft])
                vop("tensor_add", xs[ft][:, sl], reads=[xs[ft], tws2], writes=[xs[ft]],
                    in0=xs[ft][:, sl], in1=tws2[:])
            proj_fm(None, xln, H["l1a_wfc"], D, evf)

    def l1_attn_b(tag="q1"):
        xsv = xspill.rearrange("(c p) t -> c p t", p=128)
        with b.scope() as sc:
            b.dma_group("gd", [(xsv[c], xs[c][:], [xs[c]], []) for c in range(KC)])
            emit_layernorm(xs, xs)
            trimask = b.sbuf_in(sc, f"{tag}_tm", [128, 4 * SPAN], BF16)
            b.dma("gd", trimask[:], H["c_trimask"], writes=[trimask])
            gam = b.sbuf_in(sc, f"{tag}_gam", [128, 48], F32)
            for i, g in enumerate(["g1", "b1", "g2", "b2", "g3", "b3"]):
                b.dma("y", gam[:, i * 8:(i + 1) * 8], H[f"l1b_{g}"][:, 0:8], writes=[gam])

            def gamma_of(i):
                return ((gam, lambda c, i=i: 2 * i * 8 + c),
                        (gam, lambda c, i=i: (2 * i + 1) * 8 + c))

            aof = [b.sbuf_in(sc, f"{tag}_ao{c}", [128, T], BF16) for c in range(KC)]
            tws3 = b.sbuf_in(sc, f"{tag}_tws", [128, SPAN], BF16)

            g1, b1_ = gamma_of(0)
            emit_layernorm(xs, xln, gamma=g1, beta=b1_)
            b.dma("y", bias_sb[:, 0:8], H["l1b_sfb"][:, 0:8], writes=[bias_sb])
            sfsc = b.scope()
            sfsc.__enter__()
            mstage = b.sbuf_in(sfsc, f"{tag}_mst", [128, 1024], F32)
            wvv = H["l1b_sfw"].rearrange("(kc p) f -> kc p f", p=128)
            mvv = H["l1b_mask"].rearrange("(kc p) f -> kc p f", p=128)
            for ft in range(KC):
                sg = ft % 2
                b.dma_group("y", [(wstage[sg][:, kc * 128:kc * 128 + 128],
                                   wvv[kc][:, ft * 128:(ft + 1) * 128], [], [wstage[sg]])
                                  for kc in range(KC)])
                b.dma_group("y", [(mstage[:, kc * 128:kc * 128 + 128],
                                   mvv[kc][:, ft * 128:(ft + 1) * 128], [], [mstage])
                                  for kc in range(KC)])
                vop("tensor_mul", wstage[sg][:], reads=[wstage[sg], mstage],
                    writes=[wstage[sg]], in0=wstage[sg][:], in1=mstage[:])
                cast_to("v", wtile[sg][:, 0:KC * 128], wstage[sg][:, 0:KC * 128],
                        reads=[wstage[sg]], writes=[wtile[sg]])
                for sp in range(NSPAN):
                    ps = psA if sp % 2 == 0 else psB
                    sl = slice(sp * SPAN, (sp + 1) * SPAN)
                    items = [(wtile[sg][:, kc * 128:kc * 128 + 128], xln[kc][:, sl])
                             for kc in range(KC)]
                    mm_group(ps[0:128, 0:SPAN], items, reads=[wtile[sg]] + xln, writes=[ps])
                    act(tws3[:], ps[0:128, 0:SPAN], ACT.Copy, reads=[ps, bias_sb],
                        writes=[tws3], bias=bias_sb[:, ft:ft + 1])
                    vop("tensor_add", xs[ft][:, sl], reads=[xs[ft], tws3], writes=[xs[ft]],
                        in0=xs[ft][:, sl], in1=tws3[:])

            sfsc.__exit__(None, None, None)
            g2, b2_ = gamma_of(1)
            emit_layernorm(xs, xln, gamma=g2, beta=b2_)
            for nm, col in [("bq", 8), ("bk", 16), ("bv", 24), ("bo", 32)]:
                b.dma("y", bias_sb[:, col:col + 8], H[f"l1b_{nm}"][:, 0:8], writes=[bias_sb])
            vop("tensor_scalar_mul", bias_sb[:, 8:16], reads=[bias_sb], writes=[bias_sb],
                in0=bias_sb[:, 8:16], scalar1=0.125)
            atsc = b.scope()
            atsc.__enter__()
            qh2 = [b.sbuf_in(atsc, f"{tag}_q{i}", [128, T], BF16) for i in range(2)]
            kh2 = [b.sbuf_in(atsc, f"{tag}_k{i}", [128, T], BF16) for i in range(2)]
            va_s = [b.sbuf_in(atsc, f"{tag}_va_{t}", [128, 130], BF16) for t in range(16)]
            pt_s = b.sbuf_in(atsc, f"{tag}_pt", [128, 16 * SPAN], BF16)
            denQ_s = b.sbuf_in(atsc, f"{tag}_den", [64, T], F32)
            bvrow_q = b.sbuf_in(atsc, f"{tag}_bvr", [1, 128], BF16)
            for hg in range(KC):
                hsl = slice(hg * 128, (hg + 1) * 128)
                if True:
                    qh, kh, va, pt = qh2[hg % 2], kh2[hg % 2], va_s, pt_s
                    denP3 = denQ_s
                    dn0 = denP3[0:1, :]
                    dn1 = denP3[32:33, :]
                    def evq(ft, sp, ps):
                        act(qh[:, sp * SPAN:(sp + 1) * SPAN], ps[0:128, 0:SPAN],
                            ACT.Copy, reads=[ps, bias_sb], writes=[qh], scale=0.125,
                            bias=bias_sb[:, 8 + hg:9 + hg])
                    def evk(ft, sp, ps):
                        act(kh[:, sp * SPAN:(sp + 1) * SPAN], ps[0:128, 0:SPAN],
                            ACT.Copy, reads=[ps, bias_sb], writes=[kh],
                            bias=bias_sb[:, 16 + hg:17 + hg])
                    proj_fm(None, xln, H["l1b_wq"][:, hsl], 128, evq)
                    proj_fm(None, xln, H["l1b_wk"][:, hsl], 128, evk)
                    sg = hg % 2
                    wvv2 = H["l1b_wv"].rearrange("(kc p) f -> kc p f", p=128)
                    b.dma_group("y", [(wstage[sg][:, kc * 128:kc * 128 + 128],
                                       wvv2[kc][:, hsl], [], [wstage[sg]])
                                      for kc in range(KC)])
                    cast_to("v", wtile[sg][:, 0:KC * 128], wstage[sg][:, 0:KC * 128],
                            reads=[wstage[sg]], writes=[wtile[sg]])
                    transpose(psA[0:1, 0:128], bias_sb[:, 24 + hg:25 + hg],
                              ident_f[:], reads=[bias_sb, ident_f], writes=[psA])
                    scalar_copy(bvrow_q[0:1, 0:128], psA[0:1, 0:128], reads=[psA],
                                writes=[bvrow_q])
                    for tt in range(16):
                        ps = psB if tt % 2 == 0 else psP0
                        items = [(xln[kc][:, tt * 128:tt * 128 + 128],
                                  wtile[sg][:, kc * 128:kc * 128 + 128]) for kc in range(KC)]
                        items.append((allones_bf[0:1, 0:128], bvrow_q[0:1, 0:128]))
                        mm_group(ps[0:128, 0:128], items,
                                 reads=[wtile[sg], bvrow_q, allones_bf] + list(xln), writes=[ps])
                        scalar_copy(va[tt][:, 0:64], ps[0:128, 0:64], reads=[ps], writes=[va[tt]])
                        scalar_copy(va[tt][:, 65:129], ps[0:128, 64:128], reads=[ps], writes=[va[tt]])
                        vop("memset", va[tt][:, 64:65], reads=[], writes=[va[tt]], constant=1.0)
                        vop("memset", va[tt][:, 129:130], reads=[], writes=[va[tt]], constant=1.0)
                    for hh in range(2):
                        hr = slice(hh * 64, hh * 64 + 64)
                        dnr = dn0 if hh == 0 else dn1
                        for bb in range(NSPAN):
                            bsl = slice(bb * SPAN, (bb + 1) * SPAN)
                            nch = 4 * bb + 4
                            for cc0 in range(0, nch, 2):
                                ps = psS01 if (cc0 // 2) % 2 == 0 else psS23
                                def qkf(eng, cc0=cc0, bsl=bsl, ps=ps, hr=hr,
                                        kh=kh, qh=qh):
                                    i1 = eng.matmul(out=ps[0:128, 0:SPAN],
                                                    lhsT=kh[hr, cc0 * 128:(cc0 + 1) * 128],
                                                    rhs=qh[hr, bsl], start=True, stop=True)
                                    i1 = eng.matmul(out=ps[0:128, SPAN:2 * SPAN],
                                                    lhsT=kh[hr, (cc0 + 1) * 128:(cc0 + 2) * 128],
                                                    rhs=qh[hr, bsl], start=True, stop=True)
                                    return i1
                                b.op("t", qkf, reads=[kh, qh], writes=[ps])
                                for j in range(2):
                                    u = cc0 + j - 4 * bb
                                    if 0 <= u <= 3:
                                        vop("tensor_add", ps[0:128, j * SPAN:(j + 1) * SPAN],
                                            reads=[ps, trimask], writes=[ps],
                                            in0=ps[0:128, j * SPAN:(j + 1) * SPAN],
                                            in1=trimask[:, u * SPAN:(u + 1) * SPAN])
                                act(pt[:, cc0 * SPAN:(cc0 + 2) * SPAN], ps[0:128, 0:2 * SPAN],
                                    ACT.Relu, reads=[ps], writes=[pt])
                            items = [(va[ccn][:, hh * 65:hh * 65 + 65],
                                      pt[:, ccn * SPAN:(ccn + 1) * SPAN])
                                     for ccn in range(nch)]
                            mm_group(psP0[0:65, 0:SPAN], items,
                                     reads=[pt] + va, writes=[psP0])
                            scalar_copy(aof[hg][hr, bsl], psP0[0:64, 0:SPAN],
                                        reads=[psP0], writes=[aof[hg]])
                            scalar_copy(dnr[0:1, bsl], psP0[64:65, 0:SPAN],
                                        reads=[psP0], writes=[denP3])
                    vop("tensor_scalar_add", denP3[0:33, :], reads=[denP3], writes=[denP3],
                        in0=denP3[0:33, :], scalar1=1e-9)
                    vop("reciprocal", denP3[0:33, :], reads=[denP3], writes=[denP3],
                        in_=denP3[0:33, :])
                    for sp in range(NSPAN):
                        sl = slice(sp * SPAN, (sp + 1) * SPAN)
                        def g0(eng, sl=sl, dn0=dn0):
                            return eng.matmul(out=psR[0:64, 0:SPAN], lhsT=allones[0:1, 0:64],
                                              rhs=dn0[0:1, sl], start=True, stop=True)
                        b.op("t", g0, reads=[allones, denP3], writes=[psR])
                        def g1_(eng, sl=sl, dn1=dn1):
                            return eng.matmul(out=psR[64:128, 0:SPAN], lhsT=allones[32:33, 0:64],
                                              rhs=dn1[0:1, sl], start=True, stop=True)
                        b.op("t", g1_, reads=[allones, denP3], writes=[psR])
                        vop("tensor_mul", aof[hg][:, sl], reads=[aof[hg], psR],
                            writes=[aof[hg]], in0=aof[hg][:, sl], in1=psR[0:128, 0:SPAN])
            def evo2(ft, sp, ps):
                sl = slice(sp * SPAN, (sp + 1) * SPAN)
                act(tws3[:], ps[0:128, 0:SPAN], ACT.Copy, reads=[ps, bias_sb],
                    writes=[tws3], bias=bias_sb[:, 32 + ft:33 + ft])
                vop("tensor_add", xs[ft][:, sl], reads=[xs[ft], tws3], writes=[xs[ft]],
                    in0=xs[ft][:, sl], in1=tws3[:])
            atsc.__exit__(None, None, None)
            proj_fm(None, aof, H["l1b_wo"], D, evo2)

            g3, b3_ = gamma_of(2)
            emit_layernorm(xs, xln, gamma=g3, beta=b3_)
            b.dma("y", bias_sb[:, 0:32], H["l1b_ff1b"][:, 0:32], writes=[bias_sb])
            b.dma("y", bias_sb[:, 32:40], H["l1b_ff2b"][:, 0:8], writes=[bias_sb])
            hsp = b.sbuf_in(sc, f"{tag}_hsp", [128, 32 * SPAN], BF16)
            w2v = H["l1b_ff2w"].rearrange("(kc p) f -> kc p f", p=128)
            for sp in range(NSPAN):
                sl = slice(sp * SPAN, (sp + 1) * SPAN)
                def evh(ft, sp2, ps, sp=sp):
                    act(hsp[:, ft * SPAN:(ft + 1) * SPAN], ps[0:128, 0:SPAN],
                        ACT.Relu, reads=[ps, bias_sb], writes=[hsp],
                        bias=bias_sb[:, ft:ft + 1])
                proj_fm(None, xln, H["l1b_ff1w"], 4 * D, evh, spans=[sp])
                for ft in range(KC):
                    for kq in range(4):
                        sg = kq % 2
                        b.dma_group("y", [(wstage[sg][:, kc * 128:kc * 128 + 128],
                                           w2v[kq * 8 + kc][:, ft * 128:(ft + 1) * 128],
                                           [], [wstage[sg]]) for kc in range(KC)])
                        cast_to("v", wtile[sg][:, 0:KC * 128], wstage[sg][:, 0:KC * 128],
                                reads=[wstage[sg]], writes=[wtile[sg]])
                        def fn(eng, sg=sg, kq=kq, ft=ft, sl=sl):
                            inst = None
                            for kc in range(KC):
                                inst = eng.matmul(
                                    out=psA[0:128, 0:SPAN],
                                    lhsT=wtile[sg][:, kc * 128:kc * 128 + 128],
                                    rhs=hsp[:, (kq * 8 + kc) * SPAN:(kq * 8 + kc + 1) * SPAN],
                                    start=(kq == 0 and kc == 0),
                                    stop=(kq == 3 and kc == KC - 1))
                            return inst
                        b.op("t", fn, reads=[wtile[sg], hsp], writes=[psA])
                    act(tws3[:], psA[0:128, 0:SPAN], ACT.Copy, reads=[psA, bias_sb],
                        writes=[tws3], bias=bias_sb[:, 32 + ft:33 + ft])
                    vop("tensor_add", xs[ft][:, sl], reads=[xs[ft], tws3], writes=[xs[ft]],
                        in0=xs[ft][:, sl], in1=tws3[:])

            for c in range(KC):
                for h in range(2):
                    ob = wstage[h]
                    hc = slice(h * 1024, (h + 1) * 1024)
                    b.dma("y", ob[:], xsv[c][:, hc], writes=[ob])
                    vop("tensor_add", xs[c][:, hc], reads=[xs[c], ob], writes=[xs[c]],
                        in0=xs[c][:, hc], in1=ob[:])

    def store_out(src_chunks):
        ov = out_ap.rearrange("(c p) t -> c p t", p=128)
        for c in range(KC):
            b.dma("gd", ov[c], src_chunks[c][:], reads=[src_chunks[c]])

    if trunc >= 1:
        l0_attn_a()
    if trunc >= 2:
        l0_ffn("f0a")
    if trunc >= 3:
        l0_attn_b()
    if trunc >= 4:
        l0_ffn("f0b")
    if trunc >= 5:
        l1_attn_a()
    if trunc >= 6:
        l1_ffn("f1a")
    if trunc >= 7:
        l1_attn_b()
    if trunc >= 8:
        l1_ffn("f1b")
    store_out(xs)
    b.finalize(["out"])
    return nc


def _consts():
    c = {}
    c["c_ident_bf"] = np.eye(128, dtype=np.float32)
    c["c_ident_f32"] = np.eye(128, dtype=np.float32)
    c["c_ones_col"] = np.ones((128, 1), dtype=np.float32)
    c["c_ones_colf"] = np.ones((128, 1), dtype=np.float32)
    c["c_ones_row"] = np.ones((1, 128), dtype=np.float32)
    sel2 = np.zeros((2, 128), dtype=np.float32)
    sel2[0, 0:64] = 1.0
    sel2[1, 64:128] = 1.0
    c["c_sel2"] = sel2
    tm = np.zeros((128, 4 * SPAN), dtype=np.float32)
    r = np.arange(128)[:, None]
    col = np.arange(SPAN)[None, :]
    for u in range(4):
        tm[:, u * SPAN:(u + 1) * SPAN] = np.where(col < r + 128 * u, NEG, 0.0)
    c["c_trimask"] = tm
    c["c_slotmask"] = np.where(col < 4 * (r + 1), NEG, 0.0).astype(np.float32)
    had = np.array([[1, 1, 1, 1], [1, -1, 1, -1], [1, 1, -1, -1], [1, -1, -1, 1]],
                   dtype=np.float32)
    cdma = np.tile(had, (1, 16))
    cs = np.zeros((128, 4), dtype=np.float32)
    for p in range(128):
        cs[p, :] = cdma[:, p % 64]
    c["c_csign"] = cs
    base = 10000.0
    dh = 64
    inv = 1.0 / base ** (np.arange(0, dh, 2, dtype=np.float32) / dh)
    pos = np.arange(T, dtype=np.float32)
    fr = pos[:, None] * inv[None, :]
    emb = np.concatenate([fr, fr], axis=-1)
    cosv = np.cos(emb).T
    sinv = np.sin(emb).T
    cos2 = np.tile(cosv, (2, 1)).astype(np.float32)
    sin2 = np.tile(sinv, (2, 1)).astype(np.float32)
    csT = np.zeros((128, T), dtype=np.float32)
    for p in range(128):
        csT[p, :] = cdma[np.arange(T) % 4, p % 64]
    c["c_ropec"] = cos2
    c["c_ropes"] = sin2
    rsign = np.where((np.arange(128) % 64) < 32, -1.0, 1.0).astype(np.float32)[:, None]
    c["c_ropecc"] = cos2 * csT
    c["c_ropesc"] = sin2 * csT * rsign
    return c


_BF16_INPUTS = {"c_ident_bf", "c_ones_col", "c_trimask", "c_slotmask"}


def _to_ml_bf16(a):
    import ml_dtypes
    return np.ascontiguousarray(a).astype(ml_dtypes.bfloat16)


def kernel(**inputs):
    trunc = TRUNC
    nc = build_program(trunc)
    consts = _consts()

    def TT(a):
        return np.ascontiguousarray(np.asarray(a).T.astype(np.float32))

    def BIAS(a, nft=None):
        a = np.asarray(a).astype(np.float32)
        n = a.shape[0] // 128
        return np.ascontiguousarray(a.reshape(n, 128).T) if a.shape[0] >= 128 else \
            np.ascontiguousarray(a.reshape(1, -1)).T.reshape(128, -1) if False else \
            np.pad(a.reshape(1, -1), ((0, 0), (0, 0))).T.reshape(-1, 1) * np.ones((1, 1), np.float32)

    def BIAS128(a):
        a = np.asarray(a).astype(np.float32)
        if a.size % 128 == 0 and a.size >= 128:
            return np.ascontiguousarray(a.reshape(-1, 128).T)
        out = np.zeros((128, 1), dtype=np.float32)
        out[:a.size, 0] = a
        return out

    base = {}
    base["l0a_wqkv"] = np.ascontiguousarray(inputs["L0_attn_a_W_qkv"].astype(np.float32))
    for i in range(1, 5):
        base[f"cn_w{i}"] = np.ascontiguousarray(inputs[f"L0_ffn_cn_w{i}"].astype(np.float32))
        base[f"cn_b{i}"] = BIAS128(inputs[f"L0_ffn_cn_b{i}"])
    for i in range(1, 4):
        base[f"gc_w{i}"] = np.ascontiguousarray(inputs[f"L0_ffn_gc_w{i}"].astype(np.float32))
        base[f"gc_b{i}"] = BIAS128(inputs[f"L0_ffn_gc_b{i}"])
    base["ar"] = inputs["L0_ffn_adaptive_range"].reshape(1, 1).astype(np.float32)
    base["l0b_wq"] = np.ascontiguousarray(inputs["L0_attn_b_Wq"].astype(np.float32))
    base["l0b_wk"] = np.ascontiguousarray(inputs["L0_attn_b_Wk"].astype(np.float32))
    base["l0b_wv"] = np.ascontiguousarray(inputs["L0_attn_b_Wv"].astype(np.float32))
    base["l0b_wo"] = np.ascontiguousarray(inputs["L0_attn_b_Wo"].astype(np.float32))
    base["l0b_aq"] = np.ascontiguousarray(inputs["L0_attn_b_Aq"].astype(np.float32))
    base["l0b_av"] = np.ascontiguousarray(inputs["L0_attn_b_Av"].astype(np.float32))
    base["l0b_ao"] = np.ascontiguousarray(inputs["L0_attn_b_Ao"].astype(np.float32))
    base["l0b_bq"] = np.ascontiguousarray(inputs["L0_attn_b_Bq"].astype(np.float32))
    base["l0b_bv"] = np.ascontiguousarray(inputs["L0_attn_b_Bv"].astype(np.float32))
    base["l0b_bo"] = np.ascontiguousarray(inputs["L0_attn_b_Bo"].astype(np.float32))
    for w in ["Wq", "Wk", "Wv", "Wo", "Wfc"]:
        base[f"l1a_{w.lower()}"] = TT(inputs[f"L1_attn_a_{w}"])
    for w in ["bq", "bk", "bv", "bo", "bfc"]:
        base[f"l1a_{w}"] = BIAS128(inputs[f"L1_attn_a_{w}"])
    base["l1a_projT"] = TT(inputs["L1_attn_a_proj"])
    base["l1b_sfw"] = TT(inputs["L1_attn_b_sf_w"])
    base["l1b_mask"] = TT(inputs["L1_attn_b_mask"])
    base["l1b_sfb"] = BIAS128(inputs["L1_attn_b_sf_b"])
    for w in ["wq", "wk", "wv", "wo"]:
        base[f"l1b_{w}"] = TT(inputs[f"L1_attn_b_{w}"])
        base[f"l1b_b{w[1]}"] = BIAS128(inputs[f"L1_attn_b_b{w[1]}"])
    base["l1b_ff1w"] = TT(inputs["L1_attn_b_ff1_w"])
    base["l1b_ff1b"] = BIAS128(inputs["L1_attn_b_ff1_b"])
    base["l1b_ff2w"] = TT(inputs["L1_attn_b_ff2_w"])
    base["l1b_ff2b"] = BIAS128(inputs["L1_attn_b_ff2_b"])
    for g in ["g1", "b1", "g2", "b2", "g3", "b3"]:
        base[f"l1b_{g}"] = BIAS128(inputs[f"L1_attn_b_{g}"])
    base["l1f_wr"] = TT(inputs["L1_ffn_W_r"])
    base["l1f_wk"] = TT(inputs["L1_ffn_W_k"])
    base["l1f_wv"] = TT(inputs["L1_ffn_W_v"])
    base["l1f_mur"] = BIAS128(inputs["L1_ffn_mu_r"])
    base["l1f_muk"] = BIAS128(inputs["L1_ffn_mu_k"])
    base.update(consts)
    for k in list(base.keys()):
        if k in _BF16_INPUTS:
            base[k] = _to_ml_bf16(base[k])

    x = np.asarray(inputs["L0_attn_a_x"]).astype(np.float32)
    in_maps = []
    for core in range(8):
        m = dict(base)
        m["x"] = np.ascontiguousarray(x[core].T)
        in_maps.append(m)

    res = run_bass_kernel_spmd(nc, in_maps, core_ids=list(range(8)))
    if MK_DEBUG:
        import __main__
        __main__.MK_DBG_RESULTS = res.results
    outs = [np.asarray(r["out"]).T for r in res.results]
    return np.stack(outs, axis=0).astype(np.float32)


if __name__ == "__main__":
    print("build test")
    nc = build_program(1)
    print("instructions:", nc.n_instructions() if hasattr(nc, "n_instructions") else "?")
